# version 01
# baseline (speedup 1.0000x reference)
import numpy as np
import ml_dtypes
from contextlib import ExitStack

import concourse.bass as bass
import concourse.mybir as mybir
from concourse.bass_utils import run_bass_kernel_spmd

F32 = mybir.dt.float32
BF16 = mybir.dt.bfloat16
AF = mybir.ActivationFunctionType
ALU = mybir.AluOpType
BF = ml_dtypes.bfloat16

B, S, E = 2, 4096, 2048
H, KV, D = 16, 4, 128
HID = 8192
NCORE = 8
TOK = 1024
KEXT = 3072
NKB = 17
CAP = 50.0
SC_TANH = float(1.0 / (CAP * np.sqrt(128.0)))
SC_FWHT = float(1.0 / np.sqrt(HID))

OFF_CONST = 16896
OFF_X = 21312
OFF_Q = 86848
OFF_K = 119616
OFF_V = 144192
OFF_EO = 168768
OFF_G = 201536

G_TA = OFF_G + 0
G_TB = OFF_G + 4096
G_SR0 = OFF_G + 8192
G_SR1 = OFF_G + 12288
G_JUNK = OFF_G + 16384
G_CS = OFF_G + 18432

ENGS = ("sync", "gpsimd", "vector", "scalar", "tensor")
DEBUG_HOOK = None


def _hadamard(n):
    h = np.array([[1.0]], dtype=np.float64)
    while h.shape[0] < n:
        h = np.block([[h, h], [h, -h]])
    return h


class Plan:
    def __init__(self, nc, sems):
        self.nc = nc
        self.sems = sems
        self.prog = {e: [] for e in ENGS}
        self.cnt = {name: 0 for name in sems}
        self.nbar = 0

    def em(self, eng, fn):
        self.prog[eng].append(fn)

    def wait(self, eng, sem, val):
        if val is None or val <= 0:
            return
        h = self.sems[sem]
        self.em(eng, lambda e, h=h, v=val: e.wait_ge(h, v))

    def op(self, eng, fn, inc=None):
        if inc is None:
            self.em(eng, fn)
            return None
        h = self.sems[inc]
        self.cnt[inc] += 1
        v = self.cnt[inc]
        self.em(eng, lambda e, fn=fn, h=h: fn(e).then_inc(h, 1))
        return v

    def dma(self, eng, sem, out_ap, in_ap):
        h = self.sems[sem]
        self.cnt[sem] += 16
        v = self.cnt[sem]
        self.em(eng, lambda e, o=out_ap, i=in_ap, h=h: e.dma_start(out=o, in_=i).then_inc(h, 16))
        return v

    def barrier(self):
        self.nbar += 1
        tgt = self.nbar * len(ENGS)
        h = self.sems["bar"]
        for e in ENGS:
            self.em(e, lambda eng, h=h: eng.nop().then_inc(h, 1))
        for e in ENGS:
            self.em(e, lambda eng, h=h, t=tgt: eng.wait_ge(h, t))


def build_program():
    nc = bass.Bass("TRN2", target_bir_lowering=False)

    dr = {}
    def di(name, shape, dtype=F32):
        dr[name] = nc.dram_tensor(name, list(shape), dtype, kind="ExternalInput").ap()
    di("x_ext", (KEXT, E))
    di("cos_in", (128, KEXT)); di("sin_in", (128, KEXT))
    di("biask_in", (128, 24))
    di("qs_in", (128, 1)); di("ks_in", (128, 1))
    di("ident_in", (128, 128), BF16); di("identf_in", (128, 128)); di("h128_in", (128, 128), BF16)
    di("hp2_in", (128, 128), BF16)
    di("maskl_in", (128, 512), BF16); di("maskr_in", (128, 512), BF16)
    di("wqm", (512, 8192))
    di("wkvm", (128, 16384))
    di("wom", (512, 8192))
    di("w1m", (8192, 2048)); di("w3m", (8192, 2048))
    di("w2g", (8192, 2048))
    out = nc.dram_tensor("out", [TOK, E], F32, kind="ExternalOutput").ap()
    hid2_dram = nc.dram_tensor("hid2", [HID, TOK], BF16, kind="Internal").ap()
    q_dram = nc.dram_tensor("qdram", [128, 16384], BF16, kind="Internal").ap()

    A = nc.alloc_sbuf_tensor_at
    ones_bf = A("ones_bf", [128, 128], BF16, offset=OFF_CONST + 0)
    ones_f = A("ones_f", [128, 128], F32, offset=OFF_CONST + 256)
    ident = A("ident", [128, 128], BF16, offset=OFF_CONST + 768)
    identf = A("identf", [128, 128], F32, offset=OFF_CONST + 1024)
    h128c = A("h128c", [128, 128], BF16, offset=OFF_CONST + 1536)
    hp2c = A("hp2c", [128, 128], BF16, offset=OFF_CONST + 1792)
    maskl = A("maskl", [128, 512], BF16, offset=OFF_CONST + 2048)
    maskr = A("maskr", [128, 512], BF16, offset=OFF_CONST + 3072)
    biask = A("biask", [128, 24], F32, offset=OFF_CONST + 4096)
    qs_c = A("qs_c", [128, 1], F32, offset=OFF_CONST + 4224)
    ks_c = A("ks_c", [128, 1], F32, offset=OFF_CONST + 4256)
    eps5 = A("eps5", [128, 1], F32, offset=OFF_CONST + 4288)
    eps6 = A("eps6", [128, 1], F32, offset=OFF_CONST + 4320)
    ln_cols = A("ln_cols", [128, 16], F32, offset=OFF_CONST + 4352)

    Xr = A("Xr", [128, 16384], F32, offset=OFF_X)
    q_stage = [A("q_stage0", [128, 1024], BF16, offset=OFF_Q),
               A("q_stage1", [128, 1024], BF16, offset=OFF_Q + 2048)]
    qbuf = [A("qbuf0", [128, 1024], BF16, offset=G_JUNK),
            A("qbuf1", [128, 1024], BF16, offset=G_CS)]
    k_fm = A("k_fm", [128, 12288], BF16, offset=OFF_K)
    v_tm = A("v_tm", [128, 12288], BF16, offset=OFF_V)
    slotA = A("slotA", [128, 8192], BF16, offset=OFF_K)
    slotB = A("slotB", [128, 8192], BF16, offset=OFF_K + 16384)
    slot_t = [slotA, slotB]
    wkv_s = [A("wkvA", [128, 8192], BF16, offset=OFF_Q),
             A("wkvB", [128, 8192], BF16, offset=OFF_Q + 16384)]
    w1s = [A("w1s0", [128, 2048], BF16, offset=OFF_V + 8192),
           A("w1s1", [128, 2048], BF16, offset=OFF_V + 12288)]
    w3s = [A("w3s0", [128, 2048], BF16, offset=OFF_V + 16384),
           A("w3s1", [128, 2048], BF16, offset=OFF_V + 20480)]

    lnF = A("lnF", [128, 16384], BF16, offset=OFF_EO)
    lnH = A("lnH", [128, 8192], BF16, offset=OFF_EO)
    xin = A("xin", [128, 2048], F32, offset=OFF_EO + 16384)
    o_fm = A("o_fm", [128, 16384], BF16, offset=OFF_EO)

    t_a = A("t_a", [128, 1024], F32, offset=G_TA)
    t_b = A("t_b", [128, 1024], F32, offset=G_TB)
    sroot = [A("sroot0", [128, 1024], F32, offset=G_SR0),
             A("sroot1", [128, 1024], F32, offset=G_SR1)]
    junk2 = A("junk2", [128, 2048], F32, offset=G_SR0)
    ln_tmf = A("ln_tmf", [128, 2048], F32, offset=G_TA)
    ksqf = A("ksqf", [128, 512], F32, offset=G_JUNK)
    ksqf2 = A("ksqf2", [128, 512], F32, offset=OFF_V + 8192)
    cs_cos = A("cs_cos", [128, 1024], F32, offset=G_CS)
    cs_sin = A("cs_sin", [128, 1024], F32, offset=G_CS + 4096)
    a2_t1 = [A("a2_t1a", [128, 1024], F32, offset=G_TA),
             A("a2_t1b", [128, 1024], F32, offset=G_SR1)]
    a2_P = [A("a2_P0", [128, 1024], BF16, offset=G_TB),
            A("a2_P1", [128, 1024], BF16, offset=G_TB + 2048)]
    a2_rd = A("a2_rd", [128, 1024], F32, offset=G_SR0)
    f1_sg = A("f1_sg", [128, 1024], F32, offset=G_TA)
    f1_hid = [A("f1_hid0", [128, 1024], BF16, offset=G_TB),
              A("f1_hid1", [128, 1024], BF16, offset=G_TB + 2048)]
    f1_h2 = [A("f1_h2_0", [128, 1024], BF16, offset=G_SR0),
             A("f1_h2_1", [128, 1024], BF16, offset=G_SR0 + 2048)]
    fw = A("fw", [128, 65536], BF16, offset=OFF_Q)
    swp = [A("swp0", [128, 1024], BF16, offset=G_JUNK),
           A("swp1", [128, 1024], BF16, offset=G_CS),
           A("swp2", [128, 1024], BF16, offset=G_CS + 2048),
           A("swp3", [128, 1024], BF16, offset=G_CS + 4096)]
    w2s = [A("w2s0", [128, 2048], BF16, offset=G_JUNK),
           A("w2s1", [128, 2048], BF16, offset=G_CS + 2048)]

    ctx = ExitStack()
    sem_names = ["bar", "t", "v", "s", "dC", "dX", "dXin", "dCS", "dWqA", "dWqB",
                 "dWkvA", "dWkvB", "dWoA", "dWoB", "dW1A", "dW1B", "dHid",
                 "dSwpA", "dSwpB", "dSwpC", "dSwpD", "dW2A", "dW2B", "dOut", "dQA", "dQB", "dQS"]
    with ctx:
        sems = {n: ctx.enter_context(nc.semaphore(n)) for n in sem_names}
        P = Plan(nc, sems)
        ps = [ctx.enter_context(nc.psum_tensor(f"ps{i}", [128, 1024], F32))
              for i in range(4)]
        half = lambda i, hh: ps[i][:, hh * 512:(hh + 1) * 512]

        def _dbg(tag):
            if DEBUG_HOOK is not None:
                DEBUG_HOOK(tag, nc, P, dict(lnF=lnF, lnH=lnH, q_dram=q_dram, k_fm=k_fm,
                                            v_tm=v_tm, o_fm=o_fm, Xr=Xr, fw=fw,
                                            hid2_dram=hid2_dram, ln_cols=ln_cols,
                                            f1_sg=f1_sg, f1_hid1=f1_hid[1], f1_h2_1=f1_h2[1],
                                            biask=biask, cs_cos=cs_cos, cs_sin=cs_sin))

        st = {
            "ln_tiny_v": {0: 0, 1: 0},
            "ln_norm_s": {0: 0, 1: 0},
            "ln_tm_t": 0,
            "ln_evac_v": 0,
            "rope_recip_v": {0: 0, 1: 0},
            "rope_ksq_t": 0,
            "rope_fin_v": 0,
        }

        vC = 0
        for name, dst in [("ident_in", ident), ("identf_in", identf), ("h128_in", h128c), ("hp2_in", hp2c),
                          ("maskl_in", maskl), ("maskr_in", maskr), ("biask_in", biask),
                          ("qs_in", qs_c), ("ks_in", ks_c)]:
            vC = P.dma("sync", "dC", dst[:], dr[name])
        vC = P.dma("sync", "dC", cs_cos[:], dr["cos_in"][:, 1024:2048])
        vC = P.dma("sync", "dC", cs_sin[:], dr["sin_in"][:, 1024:2048])
        vXt = []
        for tl in range(8):
            vXt.append(P.dma("sync", "dX", Xr[:, tl * 2048:(tl + 1) * 2048],
                             dr["x_ext"][1024 + tl * 128:1024 + (tl + 1) * 128, :]))
        vWq = {0: P.dma("gpsimd", "dWqA", slotA[:], dr["wqm"][0:128, :]),
               1: P.dma("gpsimd", "dWqB", slotB[:], dr["wqm"][128:256, :])}
        P.op("vector", lambda v: v.memset(ones_bf[:], 1.0), inc="v")
        P.op("vector", lambda v: v.memset(ones_f[:], 1.0), inc="v")
        P.op("vector", lambda v: v.memset(eps5[:], 1e-5), inc="v")
        vMem = P.op("vector", lambda v: v.memset(eps6[:], 1e-6), inc="v")

        def emit_ln_tile(x_ap, dst_pitch, tl, wait_dma=None):
            par = tl % 2
            c0 = par * 8
            s1c = ln_cols[:, c0:c0 + 1]; s2c = ln_cols[:, c0 + 1:c0 + 2]
            mc = ln_cols[:, c0 + 2:c0 + 3]; vc = ln_cols[:, c0 + 3:c0 + 4]
            rc = ln_cols[:, c0 + 4:c0 + 5]; nbc = ln_cols[:, c0 + 5:c0 + 6]
            src = ln_cols[:, c0 + 6:c0 + 7]
            if wait_dma is not None:
                P.wait("scalar", wait_dma[0], wait_dma[1])
                P.wait("vector", wait_dma[0], wait_dma[1])
            P.wait("vector", "t", st["ln_tm_t"])
            P.wait("vector", "v", st["rope_fin_v"])
            P.op("vector", lambda v, x=x_ap: v.tensor_scalar(
                ln_tmf[:], x, 1.0, None, ALU.mult, ALU.add, accum_out=s1c), inc="v")
            vsum = P.op("vector", lambda v, x=x_ap: v.scalar_tensor_tensor(
                junk2[:], x, 1.0, x, ALU.mult, ALU.mult, accum_out=s2c), inc="v")
            P.wait("vector", "s", st["ln_norm_s"][par])
            P.wait("vector", "v", vsum)
            P.op("vector", lambda v: v.tensor_scalar(mc, s1c, 1.0 / 2048.0, None, ALU.mult), inc="v")
            vw = P.op("vector", lambda v: v.tensor_scalar(vc, s2c, 1.0 / 2048.0, None, ALU.mult), inc="v")
            P.wait("vector", "v", vw)
            vw = P.op("vector", lambda v: v.tensor_tensor(s1c, mc, mc, ALU.mult), inc="v")
            P.wait("vector", "v", vw)
            vv = P.op("vector", lambda v: v.tensor_tensor(vc, vc, s1c, ALU.subtract), inc="v")
            P.wait("scalar", "v", vv)
            vs = P.op("scalar", lambda s: s.activation(src, vc, AF.Sqrt, bias=eps5[:]), inc="s")
            P.wait("vector", "s", vs)
            vw = P.op("vector", lambda v: v.reciprocal(rc, src), inc="v")
            P.wait("vector", "v", vw)
            vtiny = P.op("vector", lambda v: v.tensor_scalar(nbc, mc, rc, -1.0, ALU.mult, ALU.mult), inc="v")
            st["ln_tiny_v"][par] = vtiny
            P.wait("scalar", "v", vtiny)
            vs = P.op("scalar", lambda s, x=x_ap: s.activation(ln_tmf[:], x, AF.Identity, bias=nbc, scale=rc), inc="s")
            st["ln_norm_s"][par] = vs
            P.wait("tensor", "s", vs)
            P.wait("tensor", "v", st["ln_evac_v"])
            vt = None
            for e in range(16):
                pt = ps[e // 8][:, (e % 8) * 128:(e % 8) * 128 + 128]
                vt = P.op("tensor", lambda t, o=pt, i=ln_tmf[:, e * 128:(e + 1) * 128]:
                          t.transpose(o, i, identf[:]),
                          inc="t" if e in (7, 15) else None)
            st["ln_tm_t"] = vt
            P.wait("vector", "t", vt)
            dtensor = lnF if dst_pitch == 1024 else lnH
            dap = dtensor.ap()
            v_ev = None
            for g in range(2):
                dst = bass.AP(tensor=dap.tensor,
                              offset=dap.offset + g * 8 * dst_pitch + tl * 128,
                              ap=[[dap.ap[0][0], 128], [dst_pitch, 8], [1, 128]])
                v_ev = P.op("vector", lambda v, o=dst, i=ps[g][:]: v.tensor_copy(o, i), inc="v")
            st["ln_evac_v"] = v_ev
            return v_ev

        def emit_rope_norm(pq_full, n, cs_off, scale_col, dst_ap, sr_par, t_stop):
            co = cs_cos[:, cs_off:cs_off + n]; si = cs_sin[:, cs_off:cs_off + n]
            tb = t_b[:, 0:n]
            P.wait("vector", "t", t_stop)
            P.op("vector", lambda v, o=t_a[0:64, 0:n], a=pq_full[64:128, :], b=si[0:64, :]:
                 v.tensor_tensor(o, a, b, ALU.mult), inc="v")
            P.op("vector", lambda v, o=t_a[64:128, 0:n], a=pq_full[0:64, :], b=si[64:128, :]:
                 v.tensor_tensor(o, a, b, ALU.mult), inc="v")
            P.op("vector", lambda v, o=tb, a=pq_full, b=co:
                 v.tensor_tensor(o, a, b, ALU.mult), inc="v")
            P.op("vector", lambda v, o=t_b[0:64, 0:n], a=t_b[0:64, 0:n], b=t_a[0:64, 0:n]:
                 v.tensor_tensor(o, a, b, ALU.subtract), inc="v")
            P.op("vector", lambda v, o=t_b[64:128, 0:n], a=t_b[64:128, 0:n], b=t_a[64:128, 0:n]:
                 v.tensor_tensor(o, a, b, ALU.add), inc="v")
            P.wait("vector", "t", st["rope_ksq_t"])
            vk = None
            nch = (n + 511) // 512
            for ch in range(nch):
                w = min(512, n - ch * 512)
                dstq = (ksqf if ch == 0 else ksqf2)[:, 0:w]
                vk = P.op("vector", lambda v, o=dstq, a=t_b[:, ch * 512:ch * 512 + w]:
                          v.tensor_tensor(o, a, a, ALU.mult), inc="v")
            P.wait("tensor", "v", vk)
            vt = None
            for ch in range(nch):
                w = min(512, n - ch * 512)
                srcq = (ksqf if ch == 0 else ksqf2)[:, 0:w]
                vt = P.op("tensor", lambda t, o=pq_full[:, ch * 512:ch * 512 + w], i=srcq:
                          t.matmul(o, ones_f[:], i, start=True, stop=True),
                          inc="t" if ch == nch - 1 else None)
            st["rope_ksq_t"] = vt
            P.wait("scalar", "t", vt)
            P.wait("scalar", "v", st["rope_recip_v"][sr_par])
            sr = sroot[sr_par][:, 0:n]
            vs = P.op("scalar", lambda s, o=sr, i=pq_full: s.activation(o, i, AF.Sqrt), inc="s")
            P.wait("vector", "s", vs)
            P.op("vector", lambda v, o=sr, i=sr: v.tensor_scalar(o, i, eps6[:], None, ALU.add), inc="v")
            vr = P.op("vector", lambda v, o=t_a[:, 0:n], i=sr: v.reciprocal(o, i), inc="v")
            st["rope_recip_v"][sr_par] = vr
            vfin = P.op("vector", lambda v, o=dst_ap, a=tb, sc=scale_col, b=t_a[:, 0:n]:
                        v.scalar_tensor_tensor(o, a, sc, b, ALU.mult, ALU.mult), inc="v")
            st["rope_fin_v"] = vfin
            return vfin, vs

        v_lastev = None
        for tl in range(8):
            v_lastev = emit_ln_tile(Xr[:, tl * 2048:(tl + 1) * 2048], 1024, tl,
                                    wait_dma=("dX", vXt[tl]))
        P.wait("tensor", "dC", vC)
        P.wait("tensor", "v", vMem)
        P.wait("tensor", "v", v_lastev)
        q_sems = ["dWqA", "dWqB"]
        sq_read = [0, 0, 0, 0]
        t_lastq = 0
        for h in range(16):
            qr, mloc = h // 4, h % 4
            par = qr % 2
            if mloc == 0:
                P.wait("tensor", q_sems[par], 16 * (qr // 2 + 1))
            pq = ps[h % 4]
            P.wait("tensor", "s", sq_read[h % 4])
            vt = None
            for e in range(16):
                for th in range(2):
                    vt = P.op("tensor", lambda t, o=pq[:, th * 512:(th + 1) * 512],
                              w=slot_t[par][:, e * 512 + mloc * 128:e * 512 + mloc * 128 + 128],
                              r=lnF[:, e * 1024 + th * 512:e * 1024 + (th + 1) * 512],
                              stt=(e == 0), sp=(e == 15):
                              t.matmul(o, w, r, start=stt, stop=sp),
                              inc="t" if (e == 15 and th == 1) else None)
            t_lastq = vt
            if mloc == 3 and qr < 2:
                P.wait("gpsimd", "t", vt)
                vWq[par] = P.dma("gpsimd", q_sems[par], slot_t[par][:],
                                 dr["wqm"][(qr + 2) * 128:(qr + 3) * 128, :])
            qpar = h % 2
            if h >= 2:
                P.wait("vector", ["dQA", "dQB"][qpar], 16 * (h // 2))
            vfin, sq = emit_rope_norm(pq[:], 1024, 0, qs_c[:], q_stage[qpar][:], h % 2, vt)
            sq_read[h % 4] = sq
            P.wait("sync", "v", vfin)
            qdst = bass.AP(tensor=q_dram.tensor, offset=q_dram.offset + h * 128,
                           ap=[[16384, 128], [2048, 8], [1, 128]])
            P.dma("sync", ["dQA", "dQB"][qpar], qdst, q_stage[qpar][:])
        _dbg("A1a")
        P.barrier()

        P.wait("gpsimd", "dQA", P.cnt["dQA"])
        P.wait("gpsimd", "dQB", P.cnt["dQB"])
        for g in range(2):
            P.dma("gpsimd", ["dWkvA", "dWkvB"][g], wkv_s[g][:],
                  dr["wkvm"][:, g * 8192:(g + 1) * 8192])
        kv_sq = [0, 0, 0, 0]
        kv_vev = [0, 0, 0, 0]

        def emit_kv_halfslab(ln_tensor, ln_pitch, th, slab, vln_done):
            pk = [half(0, 0), half(0, 1), half(1, 0), half(1, 1)]
            pv = [half(2, 0), half(2, 1), half(3, 0), half(3, 1)]
            P.wait("tensor", "s", max(kv_sq))
            P.wait("tensor", "v", max(kv_vev))
            if vln_done is not None:
                P.wait("tensor", "v", vln_done)
            vt = None
            for g in range(2):
                P.wait("tensor", ["dWkvA", "dWkvB"][g], P.cnt[["dWkvA", "dWkvB"][g]])
                for eL in range(8):
                    e = g * 8 + eL
                    rhs_off = e * ln_pitch + (th * 512 if ln_pitch == 1024 else 0)
                    for kv in range(4):
                        vt = P.op("tensor", lambda t, o=pk[kv],
                                  w=wkv_s[g][:, eL * 1024 + kv * 128:eL * 1024 + kv * 128 + 128],
                                  r=ln_tensor[:, rhs_off:rhs_off + 512],
                                  stt=(e == 0), sp=(e == 15):
                                  t.matmul(o, w, r, start=stt, stop=sp), inc=None)
                    for vtile in range(4):
                        vt = P.op("tensor", lambda t, o=pv[vtile],
                                  w=ln_tensor[:, rhs_off + vtile * 128:rhs_off + vtile * 128 + 128],
                                  r=wkv_s[g][:, eL * 1024 + 512:eL * 1024 + 1024],
                                  stt=(e == 0), sp=(e == 15):
                                  t.matmul(o, w, r, start=stt, stop=sp),
                                  inc="t" if (e == 15 and vtile == 3) else None)
            P.wait("vector", "t", vt)
            for vtile in range(4):
                gidx = slab * 8 + th * 4 + vtile
                kv_vev[vtile] = P.op("vector",
                                     lambda v, o=v_tm[:, gidx * 512:(gidx + 1) * 512], i=pv[vtile]:
                                     v.tensor_copy(o, i), inc="v")
            for kv in range(4):
                dst = k_fm[:, kv * 3072 + slab * 1024 + th * 512:
                           kv * 3072 + slab * 1024 + (th + 1) * 512]
                _, sq = emit_rope_norm(pk[kv], 512, th * 512, ks_c[:], dst, kv % 2, vt)
                kv_sq[kv] = sq

        for th in range(2):
            emit_kv_halfslab(lnF, 1024, th, 1, None)
        for sl in (0, 2):
            P.wait("sync", "v", P.cnt["v"])
            P.dma("sync", "dCS", cs_cos[:], dr["cos_in"][:, sl * 1024:(sl + 1) * 1024])
            vcs = P.dma("sync", "dCS", cs_sin[:], dr["sin_in"][:, sl * 1024:(sl + 1) * 1024])
            P.wait("vector", "dCS", vcs)
            for th in range(2):
                vln = None
                for tl4 in range(4):
                    P.wait("sync", "s", P.cnt["s"])
                    vxin = P.dma("sync", "dXin", xin[:],
                                 dr["x_ext"][sl * 1024 + (th * 4 + tl4) * 128:
                                             sl * 1024 + (th * 4 + tl4 + 1) * 128, :])
                    vln = emit_ln_tile(xin[:], 512, tl4, wait_dma=("dXin", vxin))
                emit_kv_halfslab(lnH, 512, th, sl, vln)
        _dbg("A1b")
        P.barrier()

        p_tanh = {}
        t_av = {}
        v_rd_prev = 0
        v_ow_prev = 0
        t_qlast = {}
        it_idx = -1
        P.wait("sync", "dHid", 0)
        for it0 in range(2):
            tt0, kvp0 = it0 // 2, it0 % 2
            P.dma("sync", ["dQA", "dQB"][it0 % 2], qbuf[it0 % 2][:],
                  q_dram[:, tt0 * 2048 + kvp0 * 1024:tt0 * 2048 + (kvp0 + 1) * 1024])
        for tt in range(8):
            for kvp in range(2):
                kva, kvb = kvp * 2, kvp * 2 + 1
                it_idx += 1
                qsl = qbuf[it_idx % 2]
                P.wait("tensor", ["dQA", "dQB"][it_idx % 2], P.cnt[["dQA", "dQB"][it_idx % 2]])
                prev = None if (tt, kvp) == (0, 0) else ((tt, kvp - 1) if kvp == 1 else (tt - 1, 1))
                for kb in range(NKB + 1):
                    if kb < NKB:
                        sp = ps[kb % 2]
                        if (tt, kvp, kb - 2) in p_tanh:
                            P.wait("tensor", "s", p_tanh[(tt, kvp, kb - 2)])
                        elif kb < 2 and prev is not None:
                            P.wait("tensor", "s", p_tanh[(prev[0], prev[1], 16 - kb)])
                        vt = None
                        for hh, kv in ((0, kva), (1, kvb)):
                            vt = P.op("tensor", lambda t, o=sp[:, hh * 512:(hh + 1) * 512],
                                      w=k_fm[:, kv * 3072 + (tt + kb) * 128:kv * 3072 + (tt + kb) * 128 + 128],
                                      r=qsl[:, hh * 512:(hh + 1) * 512]:
                                      t.matmul(o, w, r, start=True, stop=True),
                                      inc="t" if hh == 1 else None)
                        P.wait("scalar", "t", vt)
                        if kb == NKB - 1:
                            t_qlast[it_idx] = vt
                            nx = it_idx + 2
                            if nx < 16:
                                ttn, kvpn = nx // 2, nx % 2
                                P.wait("sync", "t", vt)
                                P.dma("sync", ["dQA", "dQB"][nx % 2], qbuf[nx % 2][:],
                                      q_dram[:, ttn * 2048 + kvpn * 1024:ttn * 2048 + (kvpn + 1) * 1024])
                        vs = P.op("scalar", lambda s, o=a2_t1[kb % 2][:], i=sp[:]:
                                  s.activation(o, i, AF.Tanh, scale=SC_TANH), inc="s")
                        p_tanh[(tt, kvp, kb)] = vs
                    if kb == 0:
                        continue
                    kx = kb - 1
                    if (tt, kvp, kx - 2) in t_av:
                        P.wait("scalar", "t", t_av[(tt, kvp, kx - 2)])
                    elif kx < 2 and prev is not None:
                        P.wait("scalar", "t", t_av[(prev[0], prev[1], 16 - kx)])
                    Pt = a2_P[kx % 2]
                    vs = P.op("scalar", lambda s, o=Pt[:], i=a2_t1[kx % 2][:],
                              b=biask[:, tt + kx:tt + kx + 1]:
                              s.activation(o, i, AF.Exp, bias=b, scale=CAP), inc="s")
                    need_v = None
                    if kx == 0 or kx == NKB - 1:
                        msk = maskl if kx == 0 else maskr
                        P.wait("vector", "s", vs)
                        for hh in range(2):
                            need_v = P.op("vector",
                                          lambda v, o=Pt[:, hh * 512:(hh + 1) * 512], b=msk[:]:
                                          v.tensor_tensor(o, o, b, ALU.mult), inc="v")
                    P.wait("tensor", "s", vs)
                    if need_v is not None:
                        P.wait("tensor", "v", need_v)
                    if kx == 0:
                        P.wait("tensor", "v", v_rd_prev)
                        P.wait("tensor", "v", v_ow_prev)
                    for hh in range(2):
                        P.op("tensor", lambda t, o=ps[2][:, hh * 512:(hh + 1) * 512],
                             r=Pt[:, hh * 512:(hh + 1) * 512],
                             stt=(kx == 0), sp=(kx == NKB - 1):
                             t.matmul(o, ones_bf[:], r, start=stt, stop=sp), inc=None)
                    vt = None
                    for hh, kv in ((0, kva), (1, kvb)):
                        vt = P.op("tensor", lambda t, o=ps[3][:, hh * 512:(hh + 1) * 512],
                                  w=v_tm[:, (tt + kx) * 512 + kv * 128:(tt + kx) * 512 + kv * 128 + 128],
                                  r=Pt[:, hh * 512:(hh + 1) * 512],
                                  stt=(kx == 0), sp=(kx == NKB - 1):
                                  t.matmul(o, w, r, start=stt, stop=sp),
                                  inc="t" if hh == 1 else None)
                    t_av[(tt, kvp, kx)] = vt
                P.wait("vector", "t", t_av[(tt, kvp, NKB - 1)])
                v_rd_prev = P.op("vector", lambda v: v.reciprocal(a2_rd[:], ps[2][:]), inc="v")
                v_ow_prev = P.op("vector", lambda v,
                                 o=o_fm[:, tt * 2048 + kvp * 1024:tt * 2048 + (kvp + 1) * 1024]:
                                 v.tensor_tensor(o, ps[3][:], a2_rd[:], ALU.mult), inc="v")
        _dbg("A2")
        P.barrier()

        po = [half(i, hh) for i in range(4) for hh in range(2)]
        wo_sems = ["dWoA", "dWoB"]
        for parx in range(2):
            P.dma("gpsimd", wo_sems[parx], slot_t[parx][:], dr["wom"][parx * 128:(parx + 1) * 128, :])
        v_eq_ev = {}
        for eq in range(4):
            par = eq % 2
            P.wait("tensor", wo_sems[par], 16 * (eq // 2 + 1))
            if eq >= 1:
                P.wait("tensor", "v", v_eq_ev[eq - 1])
            vt = None
            for hh in range(16):
                for tt in range(8):
                    vt = P.op("tensor", lambda t, o=po[tt],
                              w=o_fm[:, tt * 2048 + hh * 128:tt * 2048 + hh * 128 + 128],
                              r=slot_t[par][:, hh * 512:(hh + 1) * 512],
                              stt=(hh == 0), sp=(hh == 15):
                              t.matmul(o, w, r, start=stt, stop=sp),
                              inc="t" if (hh == 15 and tt == 7) else None)
            if eq < 2:
                P.wait("gpsimd", "t", vt)
                P.dma("gpsimd", wo_sems[par], slot_t[par][:],
                      dr["wom"][(eq + 2) * 128:(eq + 3) * 128, :])
            P.wait("vector", "t", vt)
            vev = None
            for tt in range(8):
                xsl = Xr[:, tt * 2048 + eq * 512:tt * 2048 + (eq + 1) * 512]
                vev = P.op("vector", lambda v, o=xsl, a=po[tt], b=xsl:
                           v.tensor_tensor(o, a, b, ALU.add), inc="v")
            v_eq_ev[eq] = vev
        for cc in range(2):
            P.dma("gpsimd", ["dW1A", "dW1B"][cc], w1s[cc][:], dr["w1m"][cc * 128:(cc + 1) * 128, :])
            P.dma("gpsimd", ["dW1A", "dW1B"][cc], w3s[cc][:], dr["w3m"][cc * 128:(cc + 1) * 128, :])
        _dbg("A3")
        P.barrier()

        v_lastev = None
        for tl in range(8):
            v_lastev = emit_ln_tile(Xr[:, tl * 2048:(tl + 1) * 2048], 1024, tl)
        P.wait("tensor", "v", v_lastev)
        s_silu = {0: 0}
        v_hid = {}
        s_ev = {}
        t_p1 = {}

        def emit_pass1(cc):
            P.wait("tensor", "v", v_hid[cc])
            if cc >= 1:
                P.wait("tensor", "s", s_ev[cc - 1])
            vt1 = None
            for th in range(2):
                vt1 = P.op("tensor", lambda t, o=ps[2][:, th * 512:(th + 1) * 512],
                           r=f1_hid[cc % 2][:, th * 512:(th + 1) * 512]:
                           t.matmul(o, h128c[:], r, start=True, stop=True),
                           inc="t" if th == 1 else None)
            t_p1[cc] = vt1
            P.wait("scalar", "t", vt1)
            if cc >= 2:
                P.wait("scalar", "dHid", 16 * (cc - 1))
            vse = P.op("scalar", lambda s, o=f1_h2[cc % 2][:]:
                       s.activation(o, ps[2][:], AF.Identity), inc="s")
            s_ev[cc] = vse
            P.wait("sync", "s", vse)
            P.dma("sync", "dHid", hid2_dram[cc * 128:(cc + 1) * 128, :], f1_h2[cc % 2][:])

        for c in range(64):
            par = c % 2
            P.wait("tensor", ["dW1A", "dW1B"][par], 32 * (c // 2 + 1))
            if c >= 1:
                P.wait("tensor", "s", s_silu[c - 1])
                P.wait("tensor", "v", v_hid[c - 1])
            vtg = None
            for e in range(16):
                for th in range(2):
                    vtg = P.op("tensor", lambda t, o=ps[0][:, th * 512:(th + 1) * 512],
                               w=w1s[par][:, e * 128:(e + 1) * 128],
                               r=lnF[:, e * 1024 + th * 512:e * 1024 + (th + 1) * 512],
                               stt=(e == 0), sp=(e == 15):
                               t.matmul(o, w, r, start=stt, stop=sp),
                               inc="t" if (e == 15 and th == 1) else None)
            vtl = None
            for e in range(16):
                for th in range(2):
                    vtl = P.op("tensor", lambda t, o=ps[1][:, th * 512:(th + 1) * 512],
                               w=w3s[par][:, e * 128:(e + 1) * 128],
                               r=lnF[:, e * 1024 + th * 512:e * 1024 + (th + 1) * 512],
                               stt=(e == 0), sp=(e == 15):
                               t.matmul(o, w, r, start=stt, stop=sp),
                               inc="t" if (e == 15 and th == 1) else None)
            if c < 62:
                P.wait("gpsimd", "t", vtl)
                P.dma("gpsimd", ["dW1A", "dW1B"][par], w1s[par][:],
                      dr["w1m"][(c + 2) * 128:(c + 3) * 128, :])
                P.dma("gpsimd", ["dW1A", "dW1B"][par], w3s[par][:],
                      dr["w3m"][(c + 2) * 128:(c + 3) * 128, :])
            if c >= 1:
                emit_pass1(c - 1)
            P.wait("scalar", "t", vtg)
            if c >= 1:
                P.wait("scalar", "v", v_hid[c - 1])
            vss = P.op("scalar", lambda s: s.activation(f1_sg[:], ps[0][:], AF.Silu), inc="s")
            s_silu[c] = vss
            P.wait("vector", "t", vtl)
            P.wait("vector", "s", vss)
            if c >= 2:
                P.wait("vector", "t", t_p1[c - 2])
            vvh = P.op("vector", lambda v, o=f1_hid[c % 2][:]:
                       v.tensor_tensor(o, f1_sg[:], ps[1][:], ALU.mult), inc="v")
            v_hid[c] = vvh
        emit_pass1(63)
        vhid_total = P.cnt["dHid"]
        _dbg("F1")
        P.barrier()

        P.wait("sync", "dHid", vhid_total)
        P.wait("scalar", "dHid", vhid_total)
        h2ap = hid2_dram
        s_p2ev = {}
        t_p2 = {}
        swp_sems = ["dSwpA", "dSwpB", "dSwpC", "dSwpD"]
        for j in range(64):
            sl = j % 4
            q_eng = "sync" if j % 2 == 0 else "scalar"
            if j >= 4:
                P.wait(q_eng, "t", t_p2[j - 4])
            srcap = bass.AP(tensor=h2ap.tensor, offset=h2ap.offset + j * 2048,
                            ap=[[131072, 64], [1024, 2], [1, 1024]])
            vsw = P.dma(q_eng, swp_sems[sl], swp[sl][:], srcap)
            P.wait("tensor", swp_sems[sl], vsw)
            pp = ps[j % 2]
            if j >= 2:
                P.wait("tensor", "s", s_p2ev[j - 2])
            vt = None
            for th in range(2):
                vt = P.op("tensor", lambda t, o=pp[:, th * 512:(th + 1) * 512],
                          r=swp[sl][:, th * 512:(th + 1) * 512]:
                          t.matmul(o, hp2c[:], r, start=True, stop=True),
                          inc="t" if th == 1 else None)
            t_p2[j] = vt
            P.wait("scalar", "t", vt)
            vse = P.op("scalar", lambda s, o=fw[:, j * 1024:(j + 1) * 1024], i=pp[:]:
                       s.activation(o, i, AF.Identity, scale=SC_FWHT), inc="s")
            s_p2ev[j] = vse
        _dbg("FW2a")
        P.barrier()

        w2_sems = ["dW2A", "dW2B"]
        for parx in range(2):
            P.dma("gpsimd", w2_sems[parx], w2s[parx][:], dr["w2g"][parx * 128:(parx + 1) * 128, :])
        v_ds_ev = {}
        for ds in range(4):
            if ds >= 1:
                P.wait("tensor", "v", v_ds_ev[ds - 1])
            vt = None
            for jg in range(16):
                gi = ds * 16 + jg
                par = gi % 2
                P.wait("tensor", w2_sems[par], 16 * (gi // 2 + 1))
                for j4 in range(4):
                    j = jg * 4 + j4
                    for tt in range(8):
                        vt = P.op("tensor", lambda t, o=po[tt],
                                  w=fw[:, j * 1024 + tt * 128:j * 1024 + tt * 128 + 128],
                                  r=w2s[par][:, j4 * 512:(j4 + 1) * 512],
                                  stt=(jg == 0 and j4 == 0), sp=(jg == 15 and j4 == 3):
                                  t.matmul(o, w, r, start=stt, stop=sp),
                                  inc="t" if (j4 == 3 and tt == 7) else None)
                if gi + 2 < 64:
                    P.wait("gpsimd", "t", vt)
                    P.dma("gpsimd", w2_sems[par], w2s[par][:],
                          dr["w2g"][(gi + 2) * 128:(gi + 3) * 128, :])
            P.wait("vector", "t", vt)
            vev = None
            for tt in range(8):
                xsl = Xr[:, tt * 2048 + ds * 512:tt * 2048 + (ds + 1) * 512]
                vev = P.op("vector", lambda v, o=xsl, a=po[tt], b=xsl:
                           v.tensor_tensor(o, a, b, ALU.add), inc="v")
                P.wait("sync", "v", vev)
                P.dma("sync", "dOut", out[tt * 128:(tt + 1) * 128, ds * 512:(ds + 1) * 512], xsl)
            v_ds_ev[ds] = vev
        P.wait("sync", "dOut", P.cnt["dOut"])

        with nc.Block() as block:
            block.sync(lambda e: [f(e) for f in P.prog["sync"]])
            block.gpsimd(lambda e: [f(e) for f in P.prog["gpsimd"]])
            block.vector(lambda e: [f(e) for f in P.prog["vector"]])
            block.scalar(lambda e: [f(e) for f in P.prog["scalar"]])
            block.tensor(lambda e: [f(e) for f in P.prog["tensor"]])
    return nc


_SHARED = None

def _shared_consts():
    global _SHARED
    if _SHARED is not None:
        return _SHARED
    tri = np.arange(128)
    ml = (tri[:, None] >= tri[None, :]).astype(np.float32)
    mr = (tri[:, None] <= tri[None, :]).astype(np.float32)
    maskL = np.ascontiguousarray(np.tile(ml, (1, 4)).astype(BF))
    maskR = np.ascontiguousarray(np.tile(mr, (1, 4)).astype(BF))
    ident = np.eye(128, dtype=BF)
    identf = np.eye(128, dtype=np.float32)
    h128 = np.ascontiguousarray(_hadamard(128).astype(BF))
    hp2 = np.ascontiguousarray(np.kron(_hadamard(64), np.eye(2)).astype(BF))
    half_d = 64
    inv_freq = 1.0 / (10000.0 ** (np.arange(0, half_d, dtype=np.float64) / half_d))
    angles = np.outer(np.arange(S, dtype=np.float64), inv_freq)
    ang = np.concatenate([angles, angles], axis=1)
    cosT = np.ascontiguousarray(np.cos(ang).T.astype(np.float32))
    sinT = np.ascontiguousarray(np.sin(ang).T.astype(np.float32))
    _SHARED = dict(maskL=maskL, maskR=maskR, ident=ident, identf=identf, h128=h128, hp2=hp2,
                   cosT=cosT, sinT=sinT)
    return _SHARED


def _prep_weights(wq, wk, wv, wo, W1, W2, W3):
    wq4 = wq.reshape(16, 128, 4, 4, 128)
    wqm = np.ascontiguousarray(wq4.transpose(2, 1, 0, 3, 4).reshape(512, 8192))
    wk3 = wk.reshape(16, 128, 512)
    wv3 = wv.reshape(16, 128, 512)
    wkv = np.concatenate([wk3, wv3], axis=2)
    wkvm = np.ascontiguousarray(wkv.transpose(1, 0, 2).reshape(128, 16384))
    wo4 = wo.reshape(16, 128, 4, 512)
    wom = np.ascontiguousarray(wo4.transpose(2, 1, 0, 3).reshape(512, 8192))
    w1r = W1.reshape(64, 128, 16, 128)
    w1m = np.ascontiguousarray(w1r.transpose(0, 3, 2, 1).reshape(8192, 2048))
    w3r = W3.reshape(64, 128, 16, 128)
    w3m = np.ascontiguousarray(w3r.transpose(0, 3, 2, 1).reshape(8192, 2048))
    j_i, c_i, s_i = np.meshgrid(np.arange(64), np.arange(64), np.arange(2), indexing="ij")
    perm = (c_i * 128 + 2 * j_i + s_i).reshape(64, 128)
    W2T = np.ascontiguousarray(W2.T)
    w2g = np.empty((8192, 2048), dtype=np.float32)
    for dsi in range(4):
        for jg in range(16):
            for jj in range(4):
                j = jg * 4 + jj
                w2g[(dsi * 16 + jg) * 128:(dsi * 16 + jg + 1) * 128,
                    jj * 512:(jj + 1) * 512] = W2T[perm[j], dsi * 512:(dsi + 1) * 512]
    return wqm, wkvm, wom, w1m, w3m, np.ascontiguousarray(w2g)


_NC_CACHE = None
_LAST_RES = None

def kernel(L0_attn_hidden_states, L0_attn_wq, L0_attn_wk, L0_attn_wv, L0_attn_wo,
           L0_attn_query_scale, L0_attn_key_scale, L0_ffn_W1, L0_ffn_W2, L0_ffn_W3):
    global _NC_CACHE
    x = np.asarray(L0_attn_hidden_states, dtype=np.float32)
    sh = _shared_consts()
    wqm, wkvm, wom, w1m, w3m, w2g = _prep_weights(
        np.asarray(L0_attn_wq, np.float32), np.asarray(L0_attn_wk, np.float32),
        np.asarray(L0_attn_wv, np.float32), np.asarray(L0_attn_wo, np.float32),
        np.asarray(L0_ffn_W1, np.float32), np.asarray(L0_ffn_W2, np.float32),
        np.asarray(L0_ffn_W3, np.float32))
    qs = np.ascontiguousarray(np.asarray(L0_attn_query_scale, np.float32).reshape(128, 1))
    ks = np.ascontiguousarray(np.asarray(L0_attn_key_scale, np.float32).reshape(128, 1))

    in_maps = []
    for c in range(NCORE):
        seq, jj = c // 4, c % 4
        lo = jj * 1024 - 1024
        x_ext = np.zeros((KEXT, E), np.float32)
        a = max(0, lo); bnd = min(S, lo + KEXT)
        x_ext[a - lo:bnd - lo] = x[seq, a:bnd]
        cos_e = np.zeros((128, KEXT), np.float32)
        sin_e = np.zeros((128, KEXT), np.float32)
        cos_e[:, a - lo:bnd - lo] = sh["cosT"][:, a:bnd]
        sin_e[:, a - lo:bnd - lo] = sh["sinT"][:, a:bnd]
        pos = lo + np.arange(KEXT)
        valid = (pos >= 0) & (pos < S)
        bk = np.ascontiguousarray(
            np.where(valid, 0.0, -30000.0).astype(np.float32).reshape(24, 128).T)
        in_maps.append(dict(
            x_ext=x_ext, cos_in=cos_e, sin_in=sin_e, biask_in=bk,
            qs_in=qs, ks_in=ks,
            ident_in=sh["ident"], identf_in=sh["identf"], h128_in=sh["h128"], hp2_in=sh["hp2"],
            maskl_in=sh["maskL"], maskr_in=sh["maskR"],
            wqm=wqm, wkvm=wkvm, wom=wom, w1m=w1m, w3m=w3m, w2g=w2g))
    if _NC_CACHE is None:
        _NC_CACHE = build_program()
    res = run_bass_kernel_spmd(_NC_CACHE, in_maps, core_ids=list(range(NCORE)))
    global _LAST_RES
    _LAST_RES = res
    outs = [np.asarray(res.results[c]["out"]) for c in range(NCORE)]
    return np.concatenate(outs, axis=0).reshape(B, S, E).astype(np.float32)


# version 03
# speedup vs baseline: 1.0513x; 1.0513x over previous
import numpy as np
import ml_dtypes
from contextlib import ExitStack

import concourse.bass as bass
import concourse.mybir as mybir
from concourse.bass_utils import run_bass_kernel_spmd

F32 = mybir.dt.float32
BF16 = mybir.dt.bfloat16
AF = mybir.ActivationFunctionType
ALU = mybir.AluOpType
BF = ml_dtypes.bfloat16

B, S, E = 2, 4096, 2048
H, KV, D = 16, 4, 128
HID = 8192
NCORE = 8
TOK = 1024
KEXT = 3072
NKB = 17
CAP = 50.0
SC_TANH = float(1.0 / (CAP * np.sqrt(128.0)))
SC_FWHT = float(1.0 / np.sqrt(HID))

OFF_CONST = 16896
OFF_X = 21312
OFF_Q = 86848
OFF_K = 119616
OFF_V = 144192
OFF_EO = 168768
OFF_G = 201536

G_TA = OFF_G + 0
G_TB = OFF_G + 4096
G_SR0 = OFF_G + 8192
G_SR1 = OFF_G + 12288
G_JUNK = OFF_G + 16384
G_CS = OFF_G + 18432

ENGS = ("sync", "gpsimd", "vector", "scalar", "tensor")
DEBUG_HOOK = None


def _hadamard(n):
    h = np.array([[1.0]], dtype=np.float64)
    while h.shape[0] < n:
        h = np.block([[h, h], [h, -h]])
    return h


class Plan:
    def __init__(self, nc, sems):
        self.nc = nc
        self.sems = sems
        self.prog = {e: [] for e in ENGS}
        self.cnt = {name: 0 for name in sems}
        self.nbar = 0

    def em(self, eng, fn):
        self.prog[eng].append(fn)

    def wait(self, eng, sem, val):
        if val is None or val <= 0:
            return
        h = self.sems[sem]
        self.em(eng, lambda e, h=h, v=val: e.wait_ge(h, v))

    def op(self, eng, fn, inc=None):
        if inc is None:
            self.em(eng, fn)
            return None
        h = self.sems[inc]
        self.cnt[inc] += 1
        v = self.cnt[inc]
        self.em(eng, lambda e, fn=fn, h=h: fn(e).then_inc(h, 1))
        return v

    def dma(self, eng, sem, out_ap, in_ap):
        h = self.sems[sem]
        self.cnt[sem] += 16
        v = self.cnt[sem]
        self.em(eng, lambda e, o=out_ap, i=in_ap, h=h: e.dma_start(out=o, in_=i).then_inc(h, 16))
        return v

    def barrier(self):
        self.nbar += 1
        tgt = self.nbar * len(ENGS)
        h = self.sems["bar"]
        for e in ENGS:
            self.em(e, lambda eng, h=h: eng.nop().then_inc(h, 1))
        for e in ENGS:
            self.em(e, lambda eng, h=h, t=tgt: eng.wait_ge(h, t))


def build_program():
    nc = bass.Bass("TRN2", target_bir_lowering=False)

    dr = {}
    def di(name, shape, dtype=F32):
        dr[name] = nc.dram_tensor(name, list(shape), dtype, kind="ExternalInput").ap()
    di("x_ext", (KEXT, E))
    di("cos_in", (128, KEXT)); di("sin_in", (128, KEXT))
    di("biask_in", (128, 24))
    di("qs_in", (128, 1)); di("ks_in", (128, 1))
    di("ident_in", (128, 128), BF16); di("identf_in", (128, 128)); di("h128_in", (128, 128), BF16)
    di("hp2_in", (128, 128), BF16)
    di("maskl_in", (128, 512), BF16); di("maskr_in", (128, 512), BF16)
    di("wqm", (512, 8192))
    di("wkvm", (128, 16384))
    di("wom", (512, 8192))
    di("w1m", (8192, 2048)); di("w3m", (8192, 2048))
    di("w2g", (8192, 2048))
    out = nc.dram_tensor("out", [TOK, E], F32, kind="ExternalOutput").ap()
    hid2_dram = nc.dram_tensor("hid2", [HID, TOK], BF16, kind="Internal").ap()
    q_dram = nc.dram_tensor("qdram", [128, 16384], BF16, kind="Internal").ap()

    A = nc.alloc_sbuf_tensor_at
    ones_bf = A("ones_bf", [128, 128], BF16, offset=OFF_CONST + 0)
    ones_f = A("ones_f", [128, 128], F32, offset=OFF_CONST + 256)
    ident = A("ident", [128, 128], BF16, offset=OFF_CONST + 768)
    identf = A("identf", [128, 128], F32, offset=OFF_CONST + 1024)
    h128c = A("h128c", [128, 128], BF16, offset=OFF_CONST + 1536)
    hp2c = A("hp2c", [128, 128], BF16, offset=OFF_CONST + 1792)
    maskl = A("maskl", [128, 512], BF16, offset=OFF_CONST + 2048)
    maskr = A("maskr", [128, 512], BF16, offset=OFF_CONST + 3072)
    biask = A("biask", [128, 24], F32, offset=OFF_CONST + 4096)
    qs_c = A("qs_c", [128, 1], F32, offset=OFF_CONST + 4224)
    ks_c = A("ks_c", [128, 1], F32, offset=OFF_CONST + 4256)
    eps5 = A("eps5", [128, 1], F32, offset=OFF_CONST + 4288)
    eps6 = A("eps6", [128, 1], F32, offset=OFF_CONST + 4320)
    ln_cols = A("ln_cols", [128, 16], F32, offset=OFF_CONST + 4352)

    Xr = A("Xr", [128, 16384], F32, offset=OFF_X)
    q_stage = [A("q_stage0", [128, 1024], BF16, offset=OFF_Q),
               A("q_stage1", [128, 1024], BF16, offset=OFF_Q + 2048)]
    qbuf = [A("qbuf0", [128, 1024], BF16, offset=G_JUNK),
            A("qbuf1", [128, 1024], BF16, offset=G_CS)]
    k_fm = A("k_fm", [128, 12288], BF16, offset=OFF_K)
    v_tm = A("v_tm", [128, 12288], BF16, offset=OFF_V)
    slotA = A("slotA", [128, 8192], BF16, offset=OFF_K)
    slotB = A("slotB", [128, 8192], BF16, offset=OFF_K + 16384)
    slot_t = [slotA, slotB]
    wkv_s = [A("wkvA", [128, 8192], BF16, offset=OFF_Q),
             A("wkvB", [128, 8192], BF16, offset=OFF_Q + 16384)]
    w1s = [A("w1s0", [128, 2048], BF16, offset=OFF_V + 8192),
           A("w1s1", [128, 2048], BF16, offset=OFF_V + 12288)]
    w3s = [A("w3s0", [128, 2048], BF16, offset=OFF_V + 16384),
           A("w3s1", [128, 2048], BF16, offset=OFF_V + 20480)]

    lnF = A("lnF", [128, 16384], BF16, offset=OFF_EO)
    lnH = A("lnH", [128, 8192], BF16, offset=OFF_EO)
    xin = A("xin", [128, 2048], F32, offset=OFF_EO + 16384)
    o_fm = A("o_fm", [128, 16384], BF16, offset=OFF_EO)

    t_a = A("t_a", [128, 1024], F32, offset=G_TA)
    t_b = A("t_b", [128, 1024], F32, offset=G_TB)
    sroot = [A("sroot0", [128, 1024], F32, offset=G_SR0),
             A("sroot1", [128, 1024], F32, offset=G_SR1)]
    junk2 = A("junk2", [128, 2048], F32, offset=G_SR0)
    rn_q = A("rn_q", [128, 1024], F32, offset=OFF_Q + 4096)
    rn_k = A("rn_k", [128, 512], F32, offset=G_JUNK)
    ln_tmf = A("ln_tmf", [128, 2048], F32, offset=G_TA)
    ksqf = A("ksqf", [128, 512], F32, offset=G_JUNK)
    ksqf2 = A("ksqf2", [128, 512], F32, offset=OFF_V + 8192)
    cs_cos = A("cs_cos", [128, 1024], F32, offset=G_CS)
    cs_sin = A("cs_sin", [128, 1024], F32, offset=G_CS + 4096)
    a2_t1 = [A("a2_t1a", [128, 1024], F32, offset=G_TA),
             A("a2_t1b", [128, 1024], F32, offset=G_SR1)]
    a2_P = [A("a2_P0", [128, 1024], BF16, offset=G_TB),
            A("a2_P1", [128, 1024], BF16, offset=G_TB + 2048)]
    a2_rd = A("a2_rd", [128, 1024], F32, offset=G_SR0)
    a2_den = A("a2_den", [128, 1024], F32, offset=G_CS + 2048)
    f1_sg = [A("f1_sga", [128, 1024], F32, offset=G_TA),
             A("f1_sgb", [128, 1024], F32, offset=G_CS)]
    f1_hid = [A("f1_hid0", [128, 1024], BF16, offset=G_TB),
              A("f1_hid1", [128, 1024], BF16, offset=G_TB + 2048)]
    f1_h2 = [A("f1_h2_0", [128, 1024], BF16, offset=G_SR0),
             A("f1_h2_1", [128, 1024], BF16, offset=G_SR0 + 2048)]
    fw = A("fw", [128, 65536], BF16, offset=OFF_Q)
    swp = [A("swp0", [128, 1024], BF16, offset=G_JUNK),
           A("swp1", [128, 1024], BF16, offset=G_CS),
           A("swp2", [128, 1024], BF16, offset=G_CS + 2048),
           A("swp3", [128, 1024], BF16, offset=G_CS + 4096)]
    w2s = [A("w2s0", [128, 2048], BF16, offset=G_JUNK),
           A("w2s1", [128, 2048], BF16, offset=G_CS + 2048)]

    ctx = ExitStack()
    sem_names = ["bar", "t", "v", "s", "dC", "dX", "dXin", "dCS", "dWqA", "dWqB",
                 "dWkvA", "dWkvB", "dWoA", "dWoB", "dW1A", "dW1B", "dHid",
                 "dSwpA", "dSwpB", "dSwpC", "dSwpD", "dW2A", "dW2B", "dOut", "dQA", "dQB", "dQS"]
    with ctx:
        sems = {n: ctx.enter_context(nc.semaphore(n)) for n in sem_names}
        P = Plan(nc, sems)
        ps = [ctx.enter_context(nc.psum_tensor(f"ps{i}", [128, 1024], F32))
              for i in range(4)]
        half = lambda i, hh: ps[i][:, hh * 512:(hh + 1) * 512]

        def _dbg(tag):
            if DEBUG_HOOK is not None:
                DEBUG_HOOK(tag, nc, P, dict(lnF=lnF, lnH=lnH, q_dram=q_dram, k_fm=k_fm,
                                            v_tm=v_tm, o_fm=o_fm, Xr=Xr, fw=fw,
                                            hid2_dram=hid2_dram, ln_cols=ln_cols,
                                            f1_sg=f1_sg[1], f1_hid1=f1_hid[1], f1_h2_1=f1_h2[1],
                                            biask=biask, cs_cos=cs_cos, cs_sin=cs_sin))

        st = {
            "ln_tiny_v": {0: 0, 1: 0},
            "ln_norm_s": {0: 0, 1: 0},
            "ln_tm_t": 0,
            "ln_evac_v": 0,
            "rope_ksq_t": 0,
            "rope_fin_v": 0,
        }

        vC = 0
        for name, dst in [("ident_in", ident), ("identf_in", identf), ("h128_in", h128c), ("hp2_in", hp2c),
                          ("maskl_in", maskl), ("maskr_in", maskr), ("biask_in", biask),
                          ("qs_in", qs_c), ("ks_in", ks_c)]:
            vC = P.dma("sync", "dC", dst[:], dr[name])
        vC = P.dma("sync", "dC", cs_cos[:], dr["cos_in"][:, 1024:2048])
        vC = P.dma("sync", "dC", cs_sin[:], dr["sin_in"][:, 1024:2048])
        vXt = []
        for tl in range(8):
            vXt.append(P.dma("sync", "dX", Xr[:, tl * 2048:(tl + 1) * 2048],
                             dr["x_ext"][1024 + tl * 128:1024 + (tl + 1) * 128, :]))
        vWq = {0: P.dma("gpsimd", "dWqA", slotA[:], dr["wqm"][0:128, :]),
               1: P.dma("gpsimd", "dWqB", slotB[:], dr["wqm"][128:256, :])}
        P.op("vector", lambda v: v.memset(ones_bf[:], 1.0), inc="v")
        P.op("vector", lambda v: v.memset(ones_f[:], 1.0), inc="v")
        P.op("vector", lambda v: v.memset(eps5[:], 1e-5), inc="v")
        vMem = P.op("vector", lambda v: v.memset(eps6[:], 1e-6), inc="v")

        def emit_ln_tile(x_ap, dst_pitch, tl, wait_dma=None):
            par = tl % 2
            c0 = par * 8
            s1c = ln_cols[:, c0:c0 + 1]; s2c = ln_cols[:, c0 + 1:c0 + 2]
            mc = ln_cols[:, c0 + 2:c0 + 3]; vc = ln_cols[:, c0 + 3:c0 + 4]
            rc = ln_cols[:, c0 + 4:c0 + 5]; nbc = ln_cols[:, c0 + 5:c0 + 6]
            src = ln_cols[:, c0 + 6:c0 + 7]
            if wait_dma is not None:
                P.wait("scalar", wait_dma[0], wait_dma[1])
                P.wait("vector", wait_dma[0], wait_dma[1])
            P.wait("vector", "t", st["ln_tm_t"])
            P.wait("vector", "v", st["rope_fin_v"])
            P.op("vector", lambda v, x=x_ap: v.tensor_scalar(
                ln_tmf[:], x, 1.0, None, ALU.mult, ALU.add, accum_out=s1c), inc="v")
            vsum = P.op("vector", lambda v, x=x_ap: v.scalar_tensor_tensor(
                junk2[:], x, 1.0, x, ALU.mult, ALU.mult, accum_out=s2c), inc="v")
            P.wait("vector", "s", st["ln_norm_s"][par])
            P.wait("vector", "v", vsum)
            P.op("vector", lambda v: v.tensor_scalar(mc, s1c, 1.0 / 2048.0, None, ALU.mult), inc="v")
            vw = P.op("vector", lambda v: v.tensor_scalar(vc, s2c, 1.0 / 2048.0, None, ALU.mult), inc="v")
            P.wait("vector", "v", vw)
            vw = P.op("vector", lambda v: v.tensor_tensor(s1c, mc, mc, ALU.mult), inc="v")
            P.wait("vector", "v", vw)
            vv = P.op("vector", lambda v: v.tensor_tensor(vc, vc, s1c, ALU.subtract), inc="v")
            P.wait("scalar", "v", vv)
            vs = P.op("scalar", lambda s: s.activation(src, vc, AF.Ln, bias=eps5[:]), inc="s")
            P.wait("scalar", "s", vs)
            vs = P.op("scalar", lambda s: s.activation(rc, src, AF.Exp, scale=-0.5), inc="s")
            P.wait("vector", "s", vs)
            vtiny = P.op("vector", lambda v: v.tensor_scalar(nbc, mc, rc, -1.0, ALU.mult, ALU.mult), inc="v")
            st["ln_tiny_v"][par] = vtiny
            P.wait("scalar", "v", vtiny)
            vs = P.op("scalar", lambda s, x=x_ap: s.activation(ln_tmf[:], x, AF.Identity, bias=nbc, scale=rc), inc="s")
            st["ln_norm_s"][par] = vs
            P.wait("tensor", "s", vs)
            P.wait("tensor", "v", st["ln_evac_v"])
            vt = None
            for e in range(16):
                pt = ps[e // 8][:, (e % 8) * 128:(e % 8) * 128 + 128]
                vt = P.op("tensor", lambda t, o=pt, i=ln_tmf[:, e * 128:(e + 1) * 128]:
                          t.transpose(o, i, identf[:]),
                          inc="t" if e in (7, 15) else None)
            st["ln_tm_t"] = vt
            P.wait("vector", "t", vt)
            dtensor = lnF if dst_pitch == 1024 else lnH
            dap = dtensor.ap()
            v_ev = None
            for g in range(2):
                dst = bass.AP(tensor=dap.tensor,
                              offset=dap.offset + g * 8 * dst_pitch + tl * 128,
                              ap=[[dap.ap[0][0], 128], [dst_pitch, 8], [1, 128]])
                v_ev = P.op("vector", lambda v, o=dst, i=ps[g][:]: v.tensor_copy(o, i), inc="v")
            st["ln_evac_v"] = v_ev
            return v_ev

        def emit_rope_norm(pq_full, n, cs_off, scale_col, dst_ap, sr_par, t_stop):
            co = cs_cos[:, cs_off:cs_off + n]; si = cs_sin[:, cs_off:cs_off + n]
            tb = t_b[:, 0:n]
            P.wait("vector", "t", t_stop)
            P.op("vector", lambda v, o=t_a[0:64, 0:n], a=pq_full[64:128, :], b=si[0:64, :]:
                 v.tensor_tensor(o, a, b, ALU.mult), inc="v")
            P.op("vector", lambda v, o=t_a[64:128, 0:n], a=pq_full[0:64, :], b=si[64:128, :]:
                 v.tensor_tensor(o, a, b, ALU.mult), inc="v")
            P.op("vector", lambda v, o=tb, a=pq_full, b=co:
                 v.tensor_tensor(o, a, b, ALU.mult), inc="v")
            P.op("vector", lambda v, o=t_b[0:64, 0:n], a=t_b[0:64, 0:n], b=t_a[0:64, 0:n]:
                 v.tensor_tensor(o, a, b, ALU.subtract), inc="v")
            P.op("vector", lambda v, o=t_b[64:128, 0:n], a=t_b[64:128, 0:n], b=t_a[64:128, 0:n]:
                 v.tensor_tensor(o, a, b, ALU.add), inc="v")
            P.wait("vector", "t", st["rope_ksq_t"])
            vk = None
            nch = (n + 511) // 512
            for ch in range(nch):
                w = min(512, n - ch * 512)
                dstq = (ksqf if ch == 0 else ksqf2)[:, 0:w]
                vk = P.op("vector", lambda v, o=dstq, a=t_b[:, ch * 512:ch * 512 + w]:
                          v.tensor_tensor(o, a, a, ALU.mult), inc="v")
            P.wait("tensor", "v", vk)
            vt = None
            for ch in range(nch):
                w = min(512, n - ch * 512)
                srcq = (ksqf if ch == 0 else ksqf2)[:, 0:w]
                vt = P.op("tensor", lambda t, o=pq_full[:, ch * 512:ch * 512 + w], i=srcq:
                          t.matmul(o, ones_f[:], i, start=True, stop=True),
                          inc="t" if ch == nch - 1 else None)
            st["rope_ksq_t"] = vt
            P.wait("scalar", "t", vt)
            sr = sroot[sr_par][:, 0:n]
            rb = (rn_q if n == 1024 else rn_k)[:, 0:n]
            vs = P.op("scalar", lambda s, o=sr, i=pq_full: s.activation(o, i, AF.Ln), inc="s")
            P.wait("scalar", "s", vs)
            vs = P.op("scalar", lambda s, o=rb, i=sr: s.activation(o, i, AF.Exp, scale=0.5), inc="s")
            P.wait("scalar", "s", vs)
            vs = P.op("scalar", lambda s, o=sr, i=rb: s.activation(o, i, AF.Ln, bias=eps6[:]), inc="s")
            P.wait("scalar", "s", vs)
            vs = P.op("scalar", lambda s, o=rb, i=sr: s.activation(o, i, AF.Exp, scale=-1.0), inc="s")
            P.wait("vector", "s", vs)
            vfin = P.op("vector", lambda v, o=dst_ap, a=tb, sc=scale_col, b=rb:
                        v.scalar_tensor_tensor(o, a, sc, b, ALU.mult, ALU.mult), inc="v")
            st["rope_fin_v"] = vfin
            return vfin, vs

        v_lastev = None
        for tl in range(8):
            v_lastev = emit_ln_tile(Xr[:, tl * 2048:(tl + 1) * 2048], 1024, tl,
                                    wait_dma=("dX", vXt[tl]))
        P.wait("tensor", "dC", vC)
        P.wait("tensor", "v", vMem)
        P.wait("tensor", "v", v_lastev)
        q_sems = ["dWqA", "dWqB"]
        sq_read = [0, 0, 0, 0]
        t_lastq = 0
        for h in range(16):
            qr, mloc = h // 4, h % 4
            par = qr % 2
            if mloc == 0:
                P.wait("tensor", q_sems[par], 16 * (qr // 2 + 1))
            pq = ps[h % 4]
            P.wait("tensor", "s", sq_read[h % 4])
            vt = None
            for e in range(16):
                for th in range(2):
                    vt = P.op("tensor", lambda t, o=pq[:, th * 512:(th + 1) * 512],
                              w=slot_t[par][:, e * 512 + mloc * 128:e * 512 + mloc * 128 + 128],
                              r=lnF[:, e * 1024 + th * 512:e * 1024 + (th + 1) * 512],
                              stt=(e == 0), sp=(e == 15):
                              t.matmul(o, w, r, start=stt, stop=sp),
                              inc="t" if (e == 15 and th == 1) else None)
            t_lastq = vt
            if mloc == 3 and qr < 2:
                P.wait("gpsimd", "t", vt)
                vWq[par] = P.dma("gpsimd", q_sems[par], slot_t[par][:],
                                 dr["wqm"][(qr + 2) * 128:(qr + 3) * 128, :])
            qpar = h % 2
            if h >= 2:
                P.wait("vector", ["dQA", "dQB"][qpar], 16 * (h // 2))
            vfin, sq = emit_rope_norm(pq[:], 1024, 0, qs_c[:], q_stage[qpar][:], h % 2, vt)
            sq_read[h % 4] = sq
            P.wait("sync", "v", vfin)
            qdst = bass.AP(tensor=q_dram.tensor, offset=q_dram.offset + h * 128,
                           ap=[[16384, 128], [2048, 8], [1, 128]])
            P.dma("sync", ["dQA", "dQB"][qpar], qdst, q_stage[qpar][:])
        _dbg("A1a")
        P.barrier()

        P.wait("gpsimd", "dQA", P.cnt["dQA"])
        P.wait("gpsimd", "dQB", P.cnt["dQB"])
        for g in range(2):
            P.dma("gpsimd", ["dWkvA", "dWkvB"][g], wkv_s[g][:],
                  dr["wkvm"][:, g * 8192:(g + 1) * 8192])
        kv_sq = [0, 0, 0, 0]
        kv_vev = [0, 0, 0, 0]

        def emit_kv_halfslab(ln_tensor, ln_pitch, th, slab, vln_done):
            pk = [half(0, 0), half(0, 1), half(1, 0), half(1, 1)]
            pv = [half(2, 0), half(2, 1), half(3, 0), half(3, 1)]
            P.wait("tensor", "s", max(kv_sq))
            P.wait("tensor", "v", max(kv_vev))
            if vln_done is not None:
                P.wait("tensor", "v", vln_done)
            vt = None
            for g in range(2):
                P.wait("tensor", ["dWkvA", "dWkvB"][g], P.cnt[["dWkvA", "dWkvB"][g]])
                for eL in range(8):
                    e = g * 8 + eL
                    rhs_off = e * ln_pitch + (th * 512 if ln_pitch == 1024 else 0)
                    for kv in range(4):
                        vt = P.op("tensor", lambda t, o=pk[kv],
                                  w=wkv_s[g][:, eL * 1024 + kv * 128:eL * 1024 + kv * 128 + 128],
                                  r=ln_tensor[:, rhs_off:rhs_off + 512],
                                  stt=(e == 0), sp=(e == 15):
                                  t.matmul(o, w, r, start=stt, stop=sp), inc=None)
                    for vtile in range(4):
                        vt = P.op("tensor", lambda t, o=pv[vtile],
                                  w=ln_tensor[:, rhs_off + vtile * 128:rhs_off + vtile * 128 + 128],
                                  r=wkv_s[g][:, eL * 1024 + 512:eL * 1024 + 1024],
                                  stt=(e == 0), sp=(e == 15):
                                  t.matmul(o, w, r, start=stt, stop=sp),
                                  inc="t" if (e == 15 and vtile == 3) else None)
            P.wait("vector", "t", vt)
            for vtile in range(4):
                gidx = slab * 8 + th * 4 + vtile
                kv_vev[vtile] = P.op("vector",
                                     lambda v, o=v_tm[:, gidx * 512:(gidx + 1) * 512], i=pv[vtile]:
                                     v.tensor_copy(o, i), inc="v")
            for kv in range(4):
                dst = k_fm[:, kv * 3072 + slab * 1024 + th * 512:
                           kv * 3072 + slab * 1024 + (th + 1) * 512]
                _, sq = emit_rope_norm(pk[kv], 512, th * 512, ks_c[:], dst, kv % 2, vt)
                kv_sq[kv] = sq

        for th in range(2):
            emit_kv_halfslab(lnF, 1024, th, 1, None)
        for sl in (0, 2):
            P.wait("sync", "v", P.cnt["v"])
            P.dma("sync", "dCS", cs_cos[:], dr["cos_in"][:, sl * 1024:(sl + 1) * 1024])
            vcs = P.dma("sync", "dCS", cs_sin[:], dr["sin_in"][:, sl * 1024:(sl + 1) * 1024])
            P.wait("vector", "dCS", vcs)
            for th in range(2):
                vln = None
                for tl4 in range(4):
                    P.wait("sync", "s", P.cnt["s"])
                    vxin = P.dma("sync", "dXin", xin[:],
                                 dr["x_ext"][sl * 1024 + (th * 4 + tl4) * 128:
                                             sl * 1024 + (th * 4 + tl4 + 1) * 128, :])
                    vln = emit_ln_tile(xin[:], 512, tl4, wait_dma=("dXin", vxin))
                emit_kv_halfslab(lnH, 512, th, sl, vln)
        _dbg("A1b")
        P.barrier()

        p_tanh = {}
        t_avstop = {}
        t_av = {}
        v_dencp = {}
        v_omul = {}
        P.wait("sync", "dQA", P.cnt["dQA"])
        P.wait("sync", "dQB", P.cnt["dQB"])
        for it0 in range(2):
            tt0, kvp0 = it0 // 2, it0 % 2
            P.dma("sync", ["dQA", "dQB"][it0 % 2], qbuf[it0 % 2][:],
                  q_dram[:, tt0 * 2048 + kvp0 * 1024:tt0 * 2048 + (kvp0 + 1) * 1024])
        it = -1
        for tt in range(8):
            for kvp in range(2):
                kva, kvb = kvp * 2, kvp * 2 + 1
                it += 1
                av_ps = ps[1] if it % 2 == 0 else ps[3]
                qsl = qbuf[it % 2]
                P.wait("tensor", ["dQA", "dQB"][it % 2], P.cnt[["dQA", "dQB"][it % 2]])
                for kb in range(NKB + 1):
                    if kb < NKB:
                        gstep = it * NKB + kb
                        if gstep >= 1:
                            P.wait("tensor", "s", p_tanh[gstep - 1])
                        vt = None
                        for hh, kv in ((0, kva), (1, kvb)):
                            vt = P.op("tensor", lambda t, o=ps[0][:, hh * 512:(hh + 1) * 512],
                                      w=k_fm[:, kv * 3072 + (tt + kb) * 128:kv * 3072 + (tt + kb) * 128 + 128],
                                      r=qsl[:, hh * 512:(hh + 1) * 512]:
                                      t.matmul(o, w, r, start=True, stop=True),
                                      inc="t" if hh == 1 else None)
                        P.wait("scalar", "t", vt)
                        if kb == NKB - 1:
                            nx = it + 2
                            if nx < 16:
                                ttn, kvpn = nx // 2, nx % 2
                                P.wait("sync", "t", vt)
                                P.dma("sync", ["dQA", "dQB"][nx % 2], qbuf[nx % 2][:],
                                      q_dram[:, ttn * 2048 + kvpn * 1024:ttn * 2048 + (kvpn + 1) * 1024])
                        vs = P.op("scalar", lambda s, o=a2_t1[kb % 2][:], i=ps[0][:]:
                                  s.activation(o, i, AF.Tanh, scale=SC_TANH), inc="s")
                        p_tanh[gstep] = vs
                    if kb == 0:
                        continue
                    kx = kb - 1
                    if kx >= 2:
                        P.wait("scalar", "t", t_av[(it, kx - 2)])
                    elif it >= 1:
                        P.wait("scalar", "t", t_av[(it - 1, 15 + kx)])
                    Pt = a2_P[kx % 2]
                    vs = P.op("scalar", lambda s, o=Pt[:], i=a2_t1[kx % 2][:],
                              b=biask[:, tt + kx:tt + kx + 1]:
                              s.activation(o, i, AF.Exp, bias=b, scale=CAP), inc="s")
                    need_v = None
                    if kx == 0 or kx == NKB - 1:
                        msk = maskl if kx == 0 else maskr
                        P.wait("vector", "s", vs)
                        for hh in range(2):
                            need_v = P.op("vector",
                                          lambda v, o=Pt[:, hh * 512:(hh + 1) * 512], b=msk[:]:
                                          v.tensor_tensor(o, o, b, ALU.mult), inc="v")
                    P.wait("tensor", "s", vs)
                    if need_v is not None:
                        P.wait("tensor", "v", need_v)
                    if kx == 0:
                        if it >= 1:
                            P.wait("tensor", "v", v_dencp[it - 1])
                        if it >= 2:
                            P.wait("tensor", "v", v_omul[it - 2])
                    for hh in range(2):
                        P.op("tensor", lambda t, o=ps[2][:, hh * 512:(hh + 1) * 512],
                             r=Pt[:, hh * 512:(hh + 1) * 512],
                             stt=(kx == 0), sp=(kx == NKB - 1):
                             t.matmul(o, ones_bf[:], r, start=stt, stop=sp), inc=None)
                    vt = None
                    for hh, kv in ((0, kva), (1, kvb)):
                        vt = P.op("tensor", lambda t, o=av_ps[:, hh * 512:(hh + 1) * 512],
                                  w=v_tm[:, (tt + kx) * 512 + kv * 128:(tt + kx) * 512 + kv * 128 + 128],
                                  r=Pt[:, hh * 512:(hh + 1) * 512],
                                  stt=(kx == 0), sp=(kx == NKB - 1):
                                  t.matmul(o, w, r, start=stt, stop=sp),
                                  inc="t" if hh == 1 else None)
                    t_av[(it, kx)] = vt
                t_avstop[it] = vt
                P.wait("vector", "t", vt)
                vcp = P.op("vector", lambda v: v.tensor_copy(a2_den[:], ps[2][:]), inc="v")
                v_dencp[it] = vcp
                P.op("vector", lambda v: v.reciprocal(a2_rd[:], a2_den[:]), inc="v")
                vom = P.op("vector", lambda v,
                           o=o_fm[:, tt * 2048 + kvp * 1024:tt * 2048 + (kvp + 1) * 1024],
                           a=av_ps[:]:
                           v.tensor_tensor(o, a, a2_rd[:], ALU.mult), inc="v")
                v_omul[it] = vom
        _dbg("A2")
        P.barrier()

        po = [half(i, hh) for i in range(4) for hh in range(2)]
        wo_sems = ["dWoA", "dWoB"]
        for parx in range(2):
            P.dma("gpsimd", wo_sems[parx], slot_t[parx][:], dr["wom"][parx * 128:(parx + 1) * 128, :])
        v_eq_ev = {}
        for eq in range(4):
            par = eq % 2
            P.wait("tensor", wo_sems[par], 16 * (eq // 2 + 1))
            if eq >= 1:
                P.wait("tensor", "v", v_eq_ev[eq - 1])
            vt = None
            for hh in range(16):
                for tt in range(8):
                    vt = P.op("tensor", lambda t, o=po[tt],
                              w=o_fm[:, tt * 2048 + hh * 128:tt * 2048 + hh * 128 + 128],
                              r=slot_t[par][:, hh * 512:(hh + 1) * 512],
                              stt=(hh == 0), sp=(hh == 15):
                              t.matmul(o, w, r, start=stt, stop=sp),
                              inc="t" if (hh == 15 and tt == 7) else None)
            if eq < 2:
                P.wait("gpsimd", "t", vt)
                P.dma("gpsimd", wo_sems[par], slot_t[par][:],
                      dr["wom"][(eq + 2) * 128:(eq + 3) * 128, :])
            P.wait("vector", "t", vt)
            vev = None
            for tt in range(8):
                xsl = Xr[:, tt * 2048 + eq * 512:tt * 2048 + (eq + 1) * 512]
                vev = P.op("vector", lambda v, o=xsl, a=po[tt], b=xsl:
                           v.tensor_tensor(o, a, b, ALU.add), inc="v")
            v_eq_ev[eq] = vev
        for cc in range(2):
            P.dma("gpsimd", ["dW1A", "dW1B"][cc], w1s[cc][:], dr["w1m"][cc * 128:(cc + 1) * 128, :])
            P.dma("gpsimd", ["dW1A", "dW1B"][cc], w3s[cc][:], dr["w3m"][cc * 128:(cc + 1) * 128, :])
        _dbg("A3")
        P.barrier()

        v_lastev = None
        for tl in range(8):
            v_lastev = emit_ln_tile(Xr[:, tl * 2048:(tl + 1) * 2048], 1024, tl)
        P.wait("tensor", "v", v_lastev)
        s_silu = {}
        v_hid = {}
        s_ev = {}
        t_p1 = {}

        def emit_pass1(cc):
            P.wait("tensor", "v", v_hid[cc])
            if cc >= 1:
                P.wait("tensor", "s", s_ev[cc - 1])
            vt1 = None
            for th in range(2):
                vt1 = P.op("tensor", lambda t, o=ps[2][:, th * 512:(th + 1) * 512],
                           r=f1_hid[cc % 2][:, th * 512:(th + 1) * 512]:
                           t.matmul(o, h128c[:], r, start=True, stop=True),
                           inc="t" if th == 1 else None)
            t_p1[cc] = vt1
            P.wait("scalar", "t", vt1)
            if cc >= 2:
                P.wait("scalar", "dHid", 16 * (cc - 1))
            vse = P.op("scalar", lambda s, o=f1_h2[cc % 2][:]:
                       s.activation(o, ps[2][:], AF.Identity), inc="s")
            s_ev[cc] = vse
            P.wait("sync", "s", vse)
            P.dma("sync", "dHid", hid2_dram[cc * 128:(cc + 1) * 128, :], f1_h2[cc % 2][:])

        for c in range(64):
            par = c % 2
            P.wait("tensor", ["dW1A", "dW1B"][par], 32 * (c // 2 + 1))
            if c >= 1:
                P.wait("tensor", "s", s_silu[c - 1])
            if c >= 2:
                P.wait("tensor", "v", v_hid[c - 2])
            psL = ps[1] if par == 0 else ps[3]
            vtg = None
            for e in range(16):
                for th in range(2):
                    vtg = P.op("tensor", lambda t, o=ps[0][:, th * 512:(th + 1) * 512],
                               w=w1s[par][:, e * 128:(e + 1) * 128],
                               r=lnF[:, e * 1024 + th * 512:e * 1024 + (th + 1) * 512],
                               stt=(e == 0), sp=(e == 15):
                               t.matmul(o, w, r, start=stt, stop=sp),
                               inc="t" if (e == 15 and th == 1) else None)
            vtl = None
            for e in range(16):
                for th in range(2):
                    vtl = P.op("tensor", lambda t, o=psL[:, th * 512:(th + 1) * 512],
                               w=w3s[par][:, e * 128:(e + 1) * 128],
                               r=lnF[:, e * 1024 + th * 512:e * 1024 + (th + 1) * 512],
                               stt=(e == 0), sp=(e == 15):
                               t.matmul(o, w, r, start=stt, stop=sp),
                               inc="t" if (e == 15 and th == 1) else None)
            if c < 62:
                P.wait("gpsimd", "t", vtl)
                P.dma("gpsimd", ["dW1A", "dW1B"][par], w1s[par][:],
                      dr["w1m"][(c + 2) * 128:(c + 3) * 128, :])
                P.dma("gpsimd", ["dW1A", "dW1B"][par], w3s[par][:],
                      dr["w3m"][(c + 2) * 128:(c + 3) * 128, :])
            if c >= 1:
                emit_pass1(c - 1)
            P.wait("scalar", "t", vtg)
            if c >= 2:
                P.wait("scalar", "v", v_hid[c - 2])
            vss = P.op("scalar", lambda s, o=f1_sg[par][:]: s.activation(o, ps[0][:], AF.Silu), inc="s")
            s_silu[c] = vss
            P.wait("vector", "t", vtl)
            P.wait("vector", "s", vss)
            if c >= 2:
                P.wait("vector", "t", t_p1[c - 2])
            vvh = P.op("vector", lambda v, o=f1_hid[c % 2][:], a=f1_sg[par][:], b=psL[:]:
                       v.tensor_tensor(o, a, b, ALU.mult), inc="v")
            v_hid[c] = vvh
        emit_pass1(63)
        vhid_total = P.cnt["dHid"]
        _dbg("F1")
        P.barrier()

        P.wait("sync", "dHid", vhid_total)
        P.wait("scalar", "dHid", vhid_total)
        h2ap = hid2_dram
        s_p2ev = {}
        t_p2 = {}
        swp_sems = ["dSwpA", "dSwpB", "dSwpC", "dSwpD"]
        for j in range(64):
            sl = j % 4
            q_eng = "sync" if j % 2 == 0 else "scalar"
            if j >= 4:
                P.wait(q_eng, "t", t_p2[j - 4])
            srcap = bass.AP(tensor=h2ap.tensor, offset=h2ap.offset + j * 2048,
                            ap=[[131072, 64], [1024, 2], [1, 1024]])
            vsw = P.dma(q_eng, swp_sems[sl], swp[sl][:], srcap)
            P.wait("tensor", swp_sems[sl], vsw)
            pp = ps[j % 2]
            if j >= 2:
                P.wait("tensor", "s", s_p2ev[j - 2])
            vt = None
            for th in range(2):
                vt = P.op("tensor", lambda t, o=pp[:, th * 512:(th + 1) * 512],
                          r=swp[sl][:, th * 512:(th + 1) * 512]:
                          t.matmul(o, hp2c[:], r, start=True, stop=True),
                          inc="t" if th == 1 else None)
            t_p2[j] = vt
            P.wait("scalar", "t", vt)
            vse = P.op("scalar", lambda s, o=fw[:, j * 1024:(j + 1) * 1024], i=pp[:]:
                       s.activation(o, i, AF.Identity, scale=SC_FWHT), inc="s")
            s_p2ev[j] = vse
        _dbg("FW2a")
        P.barrier()

        w2_sems = ["dW2A", "dW2B"]
        for parx in range(2):
            P.dma("gpsimd", w2_sems[parx], w2s[parx][:], dr["w2g"][parx * 128:(parx + 1) * 128, :])
        v_ds_ev = {}
        for ds in range(4):
            if ds >= 1:
                P.wait("tensor", "v", v_ds_ev[ds - 1])
            vt = None
            for jg in range(16):
                gi = ds * 16 + jg
                par = gi % 2
                P.wait("tensor", w2_sems[par], 16 * (gi // 2 + 1))
                for j4 in range(4):
                    j = jg * 4 + j4
                    for tt in range(8):
                        vt = P.op("tensor", lambda t, o=po[tt],
                                  w=fw[:, j * 1024 + tt * 128:j * 1024 + tt * 128 + 128],
                                  r=w2s[par][:, j4 * 512:(j4 + 1) * 512],
                                  stt=(jg == 0 and j4 == 0), sp=(jg == 15 and j4 == 3):
                                  t.matmul(o, w, r, start=stt, stop=sp),
                                  inc="t" if (j4 == 3 and tt == 7) else None)
                if gi + 2 < 64:
                    P.wait("gpsimd", "t", vt)
                    P.dma("gpsimd", w2_sems[par], w2s[par][:],
                          dr["w2g"][(gi + 2) * 128:(gi + 3) * 128, :])
            P.wait("vector", "t", vt)
            vev = None
            for tt in range(8):
                xsl = Xr[:, tt * 2048 + ds * 512:tt * 2048 + (ds + 1) * 512]
                vev = P.op("vector", lambda v, o=xsl, a=po[tt], b=xsl:
                           v.tensor_tensor(o, a, b, ALU.add), inc="v")
                P.wait("sync", "v", vev)
                P.dma("sync", "dOut", out[tt * 128:(tt + 1) * 128, ds * 512:(ds + 1) * 512], xsl)
            v_ds_ev[ds] = vev
        P.wait("sync", "dOut", P.cnt["dOut"])

        with nc.Block() as block:
            block.sync(lambda e: [f(e) for f in P.prog["sync"]])
            block.gpsimd(lambda e: [f(e) for f in P.prog["gpsimd"]])
            block.vector(lambda e: [f(e) for f in P.prog["vector"]])
            block.scalar(lambda e: [f(e) for f in P.prog["scalar"]])
            block.tensor(lambda e: [f(e) for f in P.prog["tensor"]])
    return nc


_SHARED = None

def _shared_consts():
    global _SHARED
    if _SHARED is not None:
        return _SHARED
    tri = np.arange(128)
    ml = (tri[:, None] >= tri[None, :]).astype(np.float32)
    mr = (tri[:, None] <= tri[None, :]).astype(np.float32)
    maskL = np.ascontiguousarray(np.tile(ml, (1, 4)).astype(BF))
    maskR = np.ascontiguousarray(np.tile(mr, (1, 4)).astype(BF))
    ident = np.eye(128, dtype=BF)
    identf = np.eye(128, dtype=np.float32)
    h128 = np.ascontiguousarray(_hadamard(128).astype(BF))
    hp2 = np.ascontiguousarray(np.kron(_hadamard(64), np.eye(2)).astype(BF))
    half_d = 64
    inv_freq = 1.0 / (10000.0 ** (np.arange(0, half_d, dtype=np.float64) / half_d))
    angles = np.outer(np.arange(S, dtype=np.float64), inv_freq)
    ang = np.concatenate([angles, angles], axis=1)
    cosT = np.ascontiguousarray(np.cos(ang).T.astype(np.float32))
    sinT = np.ascontiguousarray(np.sin(ang).T.astype(np.float32))
    _SHARED = dict(maskL=maskL, maskR=maskR, ident=ident, identf=identf, h128=h128, hp2=hp2,
                   cosT=cosT, sinT=sinT)
    return _SHARED


def _prep_weights(wq, wk, wv, wo, W1, W2, W3):
    wq4 = wq.reshape(16, 128, 4, 4, 128)
    wqm = np.ascontiguousarray(wq4.transpose(2, 1, 0, 3, 4).reshape(512, 8192))
    wk3 = wk.reshape(16, 128, 512)
    wv3 = wv.reshape(16, 128, 512)
    wkv = np.concatenate([wk3, wv3], axis=2)
    wkvm = np.ascontiguousarray(wkv.transpose(1, 0, 2).reshape(128, 16384))
    wo4 = wo.reshape(16, 128, 4, 512)
    wom = np.ascontiguousarray(wo4.transpose(2, 1, 0, 3).reshape(512, 8192))
    w1r = W1.reshape(64, 128, 16, 128)
    w1m = np.ascontiguousarray(w1r.transpose(0, 3, 2, 1).reshape(8192, 2048))
    w3r = W3.reshape(64, 128, 16, 128)
    w3m = np.ascontiguousarray(w3r.transpose(0, 3, 2, 1).reshape(8192, 2048))
    j_i, c_i, s_i = np.meshgrid(np.arange(64), np.arange(64), np.arange(2), indexing="ij")
    perm = (c_i * 128 + 2 * j_i + s_i).reshape(64, 128)
    W2T = np.ascontiguousarray(W2.T)
    w2g = np.empty((8192, 2048), dtype=np.float32)
    for dsi in range(4):
        for jg in range(16):
            for jj in range(4):
                j = jg * 4 + jj
                w2g[(dsi * 16 + jg) * 128:(dsi * 16 + jg + 1) * 128,
                    jj * 512:(jj + 1) * 512] = W2T[perm[j], dsi * 512:(dsi + 1) * 512]
    return wqm, wkvm, wom, w1m, w3m, np.ascontiguousarray(w2g)


_NC_CACHE = None
_LAST_RES = None

def kernel(L0_attn_hidden_states, L0_attn_wq, L0_attn_wk, L0_attn_wv, L0_attn_wo,
           L0_attn_query_scale, L0_attn_key_scale, L0_ffn_W1, L0_ffn_W2, L0_ffn_W3):
    global _NC_CACHE
    x = np.asarray(L0_attn_hidden_states, dtype=np.float32)
    sh = _shared_consts()
    wqm, wkvm, wom, w1m, w3m, w2g = _prep_weights(
        np.asarray(L0_attn_wq, np.float32), np.asarray(L0_attn_wk, np.float32),
        np.asarray(L0_attn_wv, np.float32), np.asarray(L0_attn_wo, np.float32),
        np.asarray(L0_ffn_W1, np.float32), np.asarray(L0_ffn_W2, np.float32),
        np.asarray(L0_ffn_W3, np.float32))
    qs = np.ascontiguousarray(np.asarray(L0_attn_query_scale, np.float32).reshape(128, 1))
    ks = np.ascontiguousarray(np.asarray(L0_attn_key_scale, np.float32).reshape(128, 1))

    in_maps = []
    for c in range(NCORE):
        seq, jj = c // 4, c % 4
        lo = jj * 1024 - 1024
        x_ext = np.zeros((KEXT, E), np.float32)
        a = max(0, lo); bnd = min(S, lo + KEXT)
        x_ext[a - lo:bnd - lo] = x[seq, a:bnd]
        cos_e = np.zeros((128, KEXT), np.float32)
        sin_e = np.zeros((128, KEXT), np.float32)
        cos_e[:, a - lo:bnd - lo] = sh["cosT"][:, a:bnd]
        sin_e[:, a - lo:bnd - lo] = sh["sinT"][:, a:bnd]
        pos = lo + np.arange(KEXT)
        valid = (pos >= 0) & (pos < S)
        bk = np.ascontiguousarray(
            np.where(valid, 0.0, -30000.0).astype(np.float32).reshape(24, 128).T)
        in_maps.append(dict(
            x_ext=x_ext, cos_in=cos_e, sin_in=sin_e, biask_in=bk,
            qs_in=qs, ks_in=ks,
            ident_in=sh["ident"], identf_in=sh["identf"], h128_in=sh["h128"], hp2_in=sh["hp2"],
            maskl_in=sh["maskL"], maskr_in=sh["maskR"],
            wqm=wqm, wkvm=wkvm, wom=wom, w1m=w1m, w3m=w3m, w2g=w2g))
    if _NC_CACHE is None:
        _NC_CACHE = build_program()
    res = run_bass_kernel_spmd(_NC_CACHE, in_maps, core_ids=list(range(NCORE)))
    global _LAST_RES
    _LAST_RES = res
    outs = [np.asarray(res.results[c]["out"]) for c in range(NCORE)]
    return np.concatenate(outs, axis=0).reshape(B, S, E).astype(np.float32)


# version 07
# speedup vs baseline: 1.0568x; 1.0052x over previous
import numpy as np
import ml_dtypes
from contextlib import ExitStack

import concourse.bass as bass
import concourse.mybir as mybir
from concourse.bass_utils import run_bass_kernel_spmd

F32 = mybir.dt.float32
BF16 = mybir.dt.bfloat16
AF = mybir.ActivationFunctionType
ALU = mybir.AluOpType
BF = ml_dtypes.bfloat16

B, S, E = 2, 4096, 2048
H, KV, D = 16, 4, 128
HID = 8192
NCORE = 8
TOK = 1024
KEXT = 3072
NKB = 17
CAP = 50.0
SC_TANH = float(1.0 / (CAP * np.sqrt(128.0)))
SC_FWHT = float(1.0 / np.sqrt(HID))

OFF_CONST = 16896
OFF_X = 21312
OFF_Q = 86848
OFF_K = 119616
OFF_V = 144192
OFF_EO = 168768
OFF_G = 201536

G_TA = OFF_G + 0
G_TB = OFF_G + 4096
G_SR0 = OFF_G + 8192
G_SR1 = OFF_G + 12288
G_JUNK = OFF_G + 16384
G_CS = OFF_G + 18432

ENGS = ("sync", "gpsimd", "vector", "scalar", "tensor")
DEBUG_HOOK = None


def _hadamard(n):
    h = np.array([[1.0]], dtype=np.float64)
    while h.shape[0] < n:
        h = np.block([[h, h], [h, -h]])
    return h


class Plan:
    def __init__(self, nc, sems):
        self.nc = nc
        self.sems = sems
        self.prog = {e: [] for e in ENGS}
        self.cnt = {name: 0 for name in sems}
        self.nbar = 0

    def em(self, eng, fn):
        self.prog[eng].append(fn)

    def wait(self, eng, sem, val):
        if val is None or val <= 0:
            return
        h = self.sems[sem]
        self.em(eng, lambda e, h=h, v=val: e.wait_ge(h, v))

    def op(self, eng, fn, inc=None):
        if inc is None:
            self.em(eng, fn)
            return None
        h = self.sems[inc]
        self.cnt[inc] += 1
        v = self.cnt[inc]
        self.em(eng, lambda e, fn=fn, h=h: fn(e).then_inc(h, 1))
        return v

    def dma(self, eng, sem, out_ap, in_ap):
        h = self.sems[sem]
        self.cnt[sem] += 16
        v = self.cnt[sem]
        self.em(eng, lambda e, o=out_ap, i=in_ap, h=h: e.dma_start(out=o, in_=i).then_inc(h, 16))
        return v

    def barrier(self):
        self.nbar += 1
        tgt = self.nbar * len(ENGS)
        h = self.sems["bar"]
        for e in ENGS:
            self.em(e, lambda eng, h=h: eng.nop().then_inc(h, 1))
        for e in ENGS:
            self.em(e, lambda eng, h=h, t=tgt: eng.wait_ge(h, t))


def build_program():
    nc = bass.Bass("TRN2", target_bir_lowering=False)

    dr = {}
    def di(name, shape, dtype=F32):
        dr[name] = nc.dram_tensor(name, list(shape), dtype, kind="ExternalInput").ap()
    di("x_ext", (KEXT, E))
    di("cos_in", (128, KEXT)); di("sin_in", (128, KEXT))
    di("biask_in", (128, 24))
    di("qs_in", (128, 1)); di("ks_in", (128, 1))
    di("ident_in", (128, 128), BF16); di("identf_in", (128, 128)); di("h128_in", (128, 128), BF16)
    di("hp2_in", (128, 128), BF16)
    di("maskl_in", (128, 512), BF16); di("maskr_in", (128, 512), BF16)
    di("wqm", (512, 8192))
    di("wkvm", (128, 16384))
    di("wom", (512, 8192))
    di("w1m", (8192, 2048)); di("w3m", (8192, 2048))
    di("w2g", (8192, 2048))
    out = nc.dram_tensor("out", [TOK, E], F32, kind="ExternalOutput").ap()
    hid2_dram = nc.dram_tensor("hid2", [HID, TOK], BF16, kind="Internal").ap()
    q_dram = nc.dram_tensor("qdram", [128, 16384], BF16, kind="Internal").ap()

    A = nc.alloc_sbuf_tensor_at
    ones_bf = A("ones_bf", [128, 128], BF16, offset=OFF_CONST + 0)
    ones_f = A("ones_f", [128, 128], F32, offset=OFF_CONST + 256)
    ident = A("ident", [128, 128], BF16, offset=OFF_CONST + 768)
    identf = A("identf", [128, 128], F32, offset=OFF_CONST + 1024)
    h128c = A("h128c", [128, 128], BF16, offset=OFF_CONST + 1536)
    hp2c = A("hp2c", [128, 128], BF16, offset=OFF_CONST + 1792)
    maskl = A("maskl", [128, 512], BF16, offset=OFF_CONST + 2048)
    maskr = A("maskr", [128, 512], BF16, offset=OFF_CONST + 3072)
    biask = A("biask", [128, 24], F32, offset=OFF_CONST + 4096)
    qs_c = A("qs_c", [128, 1], F32, offset=OFF_CONST + 4224)
    ks_c = A("ks_c", [128, 1], F32, offset=OFF_CONST + 4256)
    eps5 = A("eps5", [128, 1], F32, offset=OFF_CONST + 4288)
    eps6 = A("eps6", [128, 1], F32, offset=OFF_CONST + 4320)
    ln_cols = A("ln_cols", [128, 16], F32, offset=OFF_CONST + 4352)

    Xr = A("Xr", [128, 16384], F32, offset=OFF_X)
    q_stage = [A("q_stage0", [128, 1024], BF16, offset=OFF_Q),
               A("q_stage1", [128, 1024], BF16, offset=OFF_Q + 2048)]
    qbuf = [A("qbuf0", [128, 1024], BF16, offset=G_JUNK),
            A("qbuf1", [128, 1024], BF16, offset=G_CS)]
    k_fm = A("k_fm", [128, 12288], BF16, offset=OFF_K)
    v_tm = A("v_tm", [128, 12288], BF16, offset=OFF_V)
    slotA = A("slotA", [128, 8192], BF16, offset=OFF_K)
    slotB = A("slotB", [128, 8192], BF16, offset=OFF_K + 16384)
    slot_t = [slotA, slotB]
    wkv_s = [A("wkvA", [128, 8192], BF16, offset=OFF_Q),
             A("wkvB", [128, 8192], BF16, offset=OFF_Q + 16384)]
    wo_a2 = A("wo_a2", [128, 16384], BF16, offset=OFF_Q)
    w1s = [A("w1s0", [128, 2048], BF16, offset=OFF_V + 8192),
           A("w1s1", [128, 2048], BF16, offset=OFF_V + 12288)]
    w3s = [A("w3s0", [128, 2048], BF16, offset=OFF_V + 16384),
           A("w3s1", [128, 2048], BF16, offset=OFF_V + 20480)]

    lnF = A("lnF", [128, 16384], BF16, offset=OFF_EO)
    lnH = A("lnH", [128, 8192], BF16, offset=OFF_EO)
    xin = A("xin", [128, 2048], F32, offset=OFF_EO + 16384)
    o_fm = A("o_fm", [128, 16384], BF16, offset=OFF_EO)

    t_a = A("t_a", [128, 1024], F32, offset=G_TA)
    t_b = A("t_b", [128, 1024], F32, offset=G_TB)
    sroot = [A("sroot0", [128, 1024], F32, offset=G_SR0),
             A("sroot1", [128, 1024], F32, offset=G_SR1)]
    junk2 = A("junk2", [128, 2048], F32, offset=G_SR0)
    rn_q = A("rn_q", [128, 1024], F32, offset=OFF_Q + 4096)
    rn_k = A("rn_k", [128, 512], F32, offset=G_SR1 + 2048)
    ln_tmf = [A("ln_tmf0", [128, 2048], F32, offset=G_TA),
              A("ln_tmf1", [128, 2048], F32, offset=OFF_Q + 8192)]
    krope_q = A("krope_q", [128, 1024], F32, offset=OFF_Q + 24576)
    ksq_q = [A("ksq_q0", [128, 1024], F32, offset=OFF_Q + 16384),
             A("ksq_q1", [128, 1024], F32, offset=OFF_Q + 20480)]
    ksq_k = [A("ksq_k0", [128, 512], F32, offset=G_JUNK),
             A("ksq_k1", [128, 512], F32, offset=G_SR0 + 2048)]
    cs_cos = A("cs_cos", [128, 1024], F32, offset=G_CS)
    cs_sin = A("cs_sin", [128, 1024], F32, offset=G_CS + 4096)
    a2_t1 = [A("a2_t1a", [128, 1024], F32, offset=G_TA),
             A("a2_t1b", [128, 1024], F32, offset=G_SR1)]
    a2_P = [A("a2_P0", [128, 1024], BF16, offset=G_TB),
            A("a2_P1", [128, 1024], BF16, offset=G_TB + 2048)]
    a2_rd = A("a2_rd", [128, 1024], F32, offset=G_SR0)
    a2_den = A("a2_den", [128, 1024], F32, offset=G_CS + 2048)
    f1_sg = [A("f1_sga", [128, 1024], F32, offset=G_TA),
             A("f1_sgb", [128, 1024], F32, offset=G_CS)]
    f1_hid = [A("f1_hid0", [128, 1024], BF16, offset=G_TB),
              A("f1_hid1", [128, 1024], BF16, offset=G_TB + 2048)]
    f1_h2 = [A("f1_h2_0", [128, 1024], BF16, offset=G_SR0),
             A("f1_h2_1", [128, 1024], BF16, offset=G_SR0 + 2048)]
    fw = A("fw", [128, 65536], BF16, offset=OFF_Q)
    swp = [A("swp0", [128, 1024], BF16, offset=G_JUNK),
           A("swp1", [128, 1024], BF16, offset=G_CS),
           A("swp2", [128, 1024], BF16, offset=G_CS + 2048),
           A("swp3", [128, 1024], BF16, offset=G_CS + 4096)]
    w2s = [A("w2s0", [128, 2048], BF16, offset=G_JUNK),
           A("w2s1", [128, 2048], BF16, offset=G_CS + 2048)]

    ctx = ExitStack()
    sem_names = ["bar", "t", "v", "s", "dC", "dX", "dXin", "dCS", "dWqA", "dWqB",
                 "dWkvA", "dWkvB", "dWoA", "dWoB", "dW1A", "dW1B", "dHid",
                 "dSwpA", "dSwpB", "dSwpC", "dSwpD", "dW2A", "dW2B", "dOut", "dQA", "dQB", "dQS"]
    with ctx:
        sems = {n: ctx.enter_context(nc.semaphore(n)) for n in sem_names}
        P = Plan(nc, sems)
        ps = [ctx.enter_context(nc.psum_tensor(f"ps{i}", [128, 1024], F32))
              for i in range(4)]
        half = lambda i, hh: ps[i][:, hh * 512:(hh + 1) * 512]

        def _dbg(tag):
            if DEBUG_HOOK is not None:
                DEBUG_HOOK(tag, nc, P, dict(lnF=lnF, lnH=lnH, q_dram=q_dram, k_fm=k_fm,
                                            v_tm=v_tm, o_fm=o_fm, Xr=Xr, fw=fw,
                                            hid2_dram=hid2_dram, ln_cols=ln_cols,
                                            f1_sg=f1_sg[1], f1_hid1=f1_hid[1], f1_h2_1=f1_h2[1],
                                            biask=biask, cs_cos=cs_cos, cs_sin=cs_sin))

        st = {
            "ln_tiny_v": {0: 0, 1: 0},
            "ln_norm_s": {0: 0, 1: 0},
            "ln_tm_t": {0: 0, 1: 0},
            "ln_evac_v": 0,
            "rope_ksq_t": {0: 0, 1: 0},
            "rope_fin_v": 0,
        }

        vC = 0
        for name, dst in [("ident_in", ident), ("identf_in", identf), ("h128_in", h128c), ("hp2_in", hp2c),
                          ("maskl_in", maskl), ("maskr_in", maskr), ("biask_in", biask),
                          ("qs_in", qs_c), ("ks_in", ks_c)]:
            vC = P.dma("sync", "dC", dst[:], dr[name])
        vC = P.dma("sync", "dC", cs_cos[:], dr["cos_in"][:, 1024:2048])
        vC = P.dma("sync", "dC", cs_sin[:], dr["sin_in"][:, 1024:2048])
        vXt = []
        for tl in range(8):
            vXt.append(P.dma("sync", "dX", Xr[:, tl * 2048:(tl + 1) * 2048],
                             dr["x_ext"][1024 + tl * 128:1024 + (tl + 1) * 128, :]))
        vWq = {0: P.dma("gpsimd", "dWqA", slotA[:], dr["wqm"][0:128, :]),
               1: P.dma("gpsimd", "dWqB", slotB[:], dr["wqm"][128:256, :])}
        P.op("vector", lambda v: v.memset(ones_bf[:], 1.0), inc="v")
        P.op("vector", lambda v: v.memset(ones_f[:], 1.0), inc="v")
        P.op("vector", lambda v: v.memset(eps5[:], 1e-5), inc="v")
        vMem = P.op("vector", lambda v: v.memset(eps6[:], 1e-6), inc="v")

        def emit_ln_tile(x_ap, dst_pitch, tl, wait_dma=None, dbuf=True):
            par = tl % 2
            lpar = par if dbuf else 0
            lnt = ln_tmf[lpar]
            c0 = par * 8
            s1c = ln_cols[:, c0:c0 + 1]; s2c = ln_cols[:, c0 + 1:c0 + 2]
            mc = ln_cols[:, c0 + 2:c0 + 3]; vc = ln_cols[:, c0 + 3:c0 + 4]
            rc = ln_cols[:, c0 + 4:c0 + 5]; nbc = ln_cols[:, c0 + 5:c0 + 6]
            src = ln_cols[:, c0 + 6:c0 + 7]
            if wait_dma is not None:
                P.wait("scalar", wait_dma[0], wait_dma[1])
                P.wait("vector", wait_dma[0], wait_dma[1])
            P.wait("vector", "t", st["ln_tm_t"][lpar])
            P.wait("vector", "v", st["rope_fin_v"])
            P.wait("vector", "t", max(st["rope_ksq_t"].values()))
            P.op("vector", lambda v, x=x_ap: v.tensor_scalar(
                lnt[:], x, 1.0, None, ALU.mult, ALU.add, accum_out=s1c), inc="v")
            jout = lnt if dbuf else junk2
            vsum = P.op("vector", lambda v, x=x_ap, o=jout: v.scalar_tensor_tensor(
                o[:], x, 1.0, x, ALU.mult, ALU.mult, accum_out=s2c), inc="v")
            P.wait("vector", "s", st["ln_norm_s"][par])
            P.wait("vector", "v", vsum)
            P.op("vector", lambda v: v.tensor_scalar(mc, s1c, 1.0 / 2048.0, None, ALU.mult), inc="v")
            vw = P.op("vector", lambda v: v.tensor_scalar(vc, s2c, 1.0 / 2048.0, None, ALU.mult), inc="v")
            P.wait("vector", "v", vw)
            vw = P.op("vector", lambda v: v.tensor_tensor(s1c, mc, mc, ALU.mult), inc="v")
            P.wait("vector", "v", vw)
            vv = P.op("vector", lambda v: v.tensor_tensor(vc, vc, s1c, ALU.subtract), inc="v")
            P.wait("scalar", "v", vv)
            vs = P.op("scalar", lambda s: s.activation(src, vc, AF.Ln, bias=eps5[:]), inc="s")
            P.wait("scalar", "s", vs)
            vs = P.op("scalar", lambda s: s.activation(rc, src, AF.Exp, scale=-0.5), inc="s")
            P.wait("vector", "s", vs)
            vtiny = P.op("vector", lambda v: v.tensor_scalar(nbc, mc, rc, -1.0, ALU.mult, ALU.mult), inc="v")
            st["ln_tiny_v"][par] = vtiny
            P.wait("scalar", "v", vtiny)
            vs = P.op("scalar", lambda s, x=x_ap: s.activation(lnt[:], x, AF.Identity, bias=nbc, scale=rc), inc="s")
            st["ln_norm_s"][par] = vs
            P.wait("tensor", "s", vs)
            P.wait("tensor", "v", st["ln_evac_v"])
            vt = None
            for e in range(16):
                pt = ps[e // 8][:, (e % 8) * 128:(e % 8) * 128 + 128]
                vt = P.op("tensor", lambda t, o=pt, i=lnt[:, e * 128:(e + 1) * 128]:
                          t.transpose(o, i, identf[:]),
                          inc="t" if e in (7, 15) else None)
            st["ln_tm_t"][lpar] = vt
            P.wait("vector", "t", vt)
            dtensor = lnF if dst_pitch == 1024 else lnH
            dap = dtensor.ap()
            v_ev = None
            for g in range(2):
                dst = bass.AP(tensor=dap.tensor,
                              offset=dap.offset + g * 8 * dst_pitch + tl * 128,
                              ap=[[dap.ap[0][0], 128], [dst_pitch, 8], [1, 128]])
                v_ev = P.op("vector", lambda v, o=dst, i=ps[g][:]: v.tensor_copy(o, i), inc="v")
            st["ln_evac_v"] = v_ev
            return v_ev

        def rope_front(pq_full, n, cs_off, t_stop, kpar):
            co = cs_cos[:, cs_off:cs_off + n]; si = cs_sin[:, cs_off:cs_off + n]
            tb = t_b[:, 0:n]
            P.wait("vector", "t", t_stop)
            P.op("vector", lambda v, o=t_a[0:64, 0:n], a=pq_full[64:128, :], b=si[0:64, :]:
                 v.tensor_tensor(o, a, b, ALU.mult), inc="v")
            P.op("vector", lambda v, o=t_a[64:128, 0:n], a=pq_full[0:64, :], b=si[64:128, :]:
                 v.tensor_tensor(o, a, b, ALU.mult), inc="v")
            P.op("vector", lambda v, o=tb, a=pq_full, b=co:
                 v.tensor_tensor(o, a, b, ALU.mult), inc="v")
            P.op("vector", lambda v, o=t_b[0:64, 0:n], a=t_b[0:64, 0:n], b=t_a[0:64, 0:n]:
                 v.tensor_tensor(o, a, b, ALU.subtract), inc="v")
            P.op("vector", lambda v, o=t_b[64:128, 0:n], a=t_b[64:128, 0:n], b=t_a[64:128, 0:n]:
                 v.tensor_tensor(o, a, b, ALU.add), inc="v")
            if n == 1024:
                kr = krope_q[:, 0:n]
                P.op("vector", lambda v, o=kr, i=tb: v.tensor_copy(o, i), inc="v")
            else:
                kr = tb
            ksq = (ksq_q if n == 1024 else ksq_k)[kpar][:, 0:n]
            P.wait("vector", "t", st["rope_ksq_t"][kpar])
            vk = None
            nch = (n + 511) // 512
            for ch in range(nch):
                w = min(512, n - ch * 512)
                vk = P.op("vector", lambda v, o=ksq[:, ch * 512:ch * 512 + w],
                          a=kr[:, ch * 512:ch * 512 + w]:
                          v.tensor_tensor(o, a, a, ALU.mult), inc="v")
            return vk

        def rope_tail(pq_full, n, scale_col, dst_ap, sr_par, kpar, vk):
            ksq = (ksq_q if n == 1024 else ksq_k)[kpar][:, 0:n]
            kr = krope_q[:, 0:n] if n == 1024 else t_b[:, 0:n]
            P.wait("tensor", "v", vk)
            vt = None
            nch = (n + 511) // 512
            for ch in range(nch):
                w = min(512, n - ch * 512)
                vt = P.op("tensor", lambda t, o=pq_full[:, ch * 512:ch * 512 + w],
                          i=ksq[:, ch * 512:ch * 512 + w]:
                          t.matmul(o, ones_f[:], i, start=True, stop=True),
                          inc="t" if ch == nch - 1 else None)
            st["rope_ksq_t"][kpar] = vt
            P.wait("scalar", "t", vt)
            sr = sroot[sr_par][:, 0:n]
            rb = (rn_q if n == 1024 else rn_k)[:, 0:n]
            vs = P.op("scalar", lambda s, o=sr, i=pq_full: s.activation(o, i, AF.Ln), inc="s")
            P.wait("scalar", "s", vs)
            vs = P.op("scalar", lambda s, o=rb, i=sr: s.activation(o, i, AF.Exp, scale=0.5), inc="s")
            P.wait("scalar", "s", vs)
            vs = P.op("scalar", lambda s, o=sr, i=rb: s.activation(o, i, AF.Ln, bias=eps6[:]), inc="s")
            P.wait("scalar", "s", vs)
            vs = P.op("scalar", lambda s, o=rb, i=sr: s.activation(o, i, AF.Exp, scale=-1.0), inc="s")
            P.wait("vector", "s", vs)
            vfin = P.op("vector", lambda v, o=dst_ap, a=kr, sc=scale_col, b=rb:
                        v.scalar_tensor_tensor(o, a, sc, b, ALU.mult, ALU.mult), inc="v")
            st["rope_fin_v"] = vfin
            return vfin, vs

        v_lastev = None
        for tl in range(8):
            v_lastev = emit_ln_tile(Xr[:, tl * 2048:(tl + 1) * 2048], 1024, tl,
                                    wait_dma=("dX", vXt[tl]))
        P.wait("tensor", "dC", vC)
        P.wait("tensor", "v", vMem)
        P.wait("tensor", "v", v_lastev)
        q_sems = ["dWqA", "dWqB"]
        sq_read = [0, 0, 0, 0]
        pend = None
        for h in range(16):
            qr, mloc = h // 4, h % 4
            par = qr % 2
            if mloc == 0:
                P.wait("tensor", q_sems[par], 16 * (qr // 2 + 1))
            pq = ps[h % 4]
            P.wait("tensor", "s", sq_read[h % 4])
            vt = None
            for e in range(16):
                for th in range(2):
                    vt = P.op("tensor", lambda t, o=pq[:, th * 512:(th + 1) * 512],
                              w=slot_t[par][:, e * 512 + mloc * 128:e * 512 + mloc * 128 + 128],
                              r=lnF[:, e * 1024 + th * 512:e * 1024 + (th + 1) * 512],
                              stt=(e == 0), sp=(e == 15):
                              t.matmul(o, w, r, start=stt, stop=sp),
                              inc="t" if (e == 15 and th == 1) else None)
            if mloc == 3 and qr < 2:
                P.wait("gpsimd", "t", vt)
                vWq[par] = P.dma("gpsimd", q_sems[par], slot_t[par][:],
                                 dr["wqm"][(qr + 2) * 128:(qr + 3) * 128, :])
            if pend is not None:
                hp, pqp, vkp = pend
                _, sq = rope_tail(pqp[:], 1024, qs_c[:], q_stage[hp % 2][:], hp % 2, hp % 2, vkp)
                sq_read[hp % 4] = sq
                P.wait("sync", "v", P.cnt["v"])
                qdst = bass.AP(tensor=q_dram.tensor, offset=q_dram.offset + hp * 128,
                               ap=[[16384, 128], [2048, 8], [1, 128]])
                P.dma("sync", ["dQA", "dQB"][hp % 2], qdst, q_stage[hp % 2][:])
            if h >= 2:
                P.wait("vector", ["dQA", "dQB"][h % 2], 16 * (h // 2))
            vk = rope_front(pq[:], 1024, 0, vt, h % 2)
            pend = (h, pq, vk)
        hp, pqp, vkp = pend
        _, sq = rope_tail(pqp[:], 1024, qs_c[:], q_stage[hp % 2][:], hp % 2, hp % 2, vkp)
        sq_read[hp % 4] = sq
        P.wait("sync", "v", P.cnt["v"])
        qdst = bass.AP(tensor=q_dram.tensor, offset=q_dram.offset + hp * 128,
                       ap=[[16384, 128], [2048, 8], [1, 128]])
        P.dma("sync", ["dQA", "dQB"][hp % 2], qdst, q_stage[hp % 2][:])
        _dbg("A1a")
        P.barrier()

        P.wait("gpsimd", "dQA", P.cnt["dQA"])
        P.wait("gpsimd", "dQB", P.cnt["dQB"])
        for g in range(2):
            P.dma("gpsimd", ["dWkvA", "dWkvB"][g], wkv_s[g][:],
                  dr["wkvm"][:, g * 8192:(g + 1) * 8192])
        kv_sq = [0, 0, 0, 0]
        kv_vev = [0, 0, 0, 0]

        def emit_kv_halfslab(ln_tensor, ln_pitch, th, slab, vln_done):
            pk = [half(0, 0), half(0, 1), half(1, 0), half(1, 1)]
            pv = [half(2, 0), half(2, 1), half(3, 0), half(3, 1)]
            P.wait("tensor", "s", max(kv_sq))
            P.wait("tensor", "v", max(kv_vev))
            if vln_done is not None:
                P.wait("tensor", "v", vln_done)
            P.wait("tensor", "dWkvA", P.cnt["dWkvA"])
            P.wait("tensor", "dWkvB", P.cnt["dWkvB"])
            tk = None
            for g in range(2):
                for eL in range(8):
                    e = g * 8 + eL
                    rhs_off = e * ln_pitch + (th * 512 if ln_pitch == 1024 else 0)
                    for kv in range(4):
                        tk = P.op("tensor", lambda t, o=pk[kv],
                                  w=wkv_s[g][:, eL * 1024 + kv * 128:eL * 1024 + kv * 128 + 128],
                                  r=ln_tensor[:, rhs_off:rhs_off + 512],
                                  stt=(e == 0), sp=(e == 15):
                                  t.matmul(o, w, r, start=stt, stop=sp),
                                  inc="t" if (e == 15 and kv == 3) else None)
            vt = None
            for g in range(2):
                for eL in range(8):
                    e = g * 8 + eL
                    rhs_off = e * ln_pitch + (th * 512 if ln_pitch == 1024 else 0)
                    for vtile in range(4):
                        vt = P.op("tensor", lambda t, o=pv[vtile],
                                  w=ln_tensor[:, rhs_off + vtile * 128:rhs_off + vtile * 128 + 128],
                                  r=wkv_s[g][:, eL * 1024 + 512:eL * 1024 + 1024],
                                  stt=(e == 0), sp=(e == 15):
                                  t.matmul(o, w, r, start=stt, stop=sp),
                                  inc="t" if (e == 15 and vtile == 3) else None)
                    if e % 4 == 1:
                        kv = e // 4
                        dst = k_fm[:, kv * 3072 + slab * 1024 + th * 512:
                                   kv * 3072 + slab * 1024 + (th + 1) * 512]
                        vkf = rope_front(pk[kv], 512, th * 512, tk, kv % 2)
                        _, sq = rope_tail(pk[kv], 512, ks_c[:], dst, kv % 2, kv % 2, vkf)
                        kv_sq[kv] = sq
            P.wait("vector", "t", vt)
            for vtile in range(4):
                gidx = slab * 8 + th * 4 + vtile
                kv_vev[vtile] = P.op("vector",
                                     lambda v, o=v_tm[:, gidx * 512:(gidx + 1) * 512], i=pv[vtile]:
                                     v.tensor_copy(o, i), inc="v")

        for th in range(2):
            emit_kv_halfslab(lnF, 1024, th, 1, None)
        for sl in (0, 2):
            P.wait("sync", "v", P.cnt["v"])
            P.dma("sync", "dCS", cs_cos[:], dr["cos_in"][:, sl * 1024:(sl + 1) * 1024])
            vcs = P.dma("sync", "dCS", cs_sin[:], dr["sin_in"][:, sl * 1024:(sl + 1) * 1024])
            P.wait("vector", "dCS", vcs)
            for th in range(2):
                vln = None
                for tl4 in range(4):
                    P.wait("sync", "s", P.cnt["s"])
                    vxin = P.dma("sync", "dXin", xin[:],
                                 dr["x_ext"][sl * 1024 + (th * 4 + tl4) * 128:
                                             sl * 1024 + (th * 4 + tl4 + 1) * 128, :])
                    vln = emit_ln_tile(xin[:], 512, tl4, wait_dma=("dXin", vxin), dbuf=False)
                emit_kv_halfslab(lnH, 512, th, sl, vln)
        _dbg("A1b")
        P.barrier()

        p_tanh = {}
        t_avstop = {}
        t_av = {}
        v_dencp = {}
        v_omul = {}
        P.wait("sync", "dQA", P.cnt["dQA"])
        P.wait("sync", "dQB", P.cnt["dQB"])
        for eqh in range(2):
            P.dma("gpsimd", "dWoA", wo_a2[:, eqh * 8192:(eqh + 1) * 8192],
                  dr["wom"][eqh * 128:(eqh + 1) * 128, :])
        st["t_wo_last"] = 0
        for it0 in range(2):
            tt0, kvp0 = it0 // 2, it0 % 2
            P.dma("sync", ["dQA", "dQB"][it0 % 2], qbuf[it0 % 2][:],
                  q_dram[:, tt0 * 2048 + kvp0 * 1024:tt0 * 2048 + (kvp0 + 1) * 1024])
        def emit_a2_epilogue(ite):
            tte, kvpe = ite // 2, ite % 2
            avp = ps[1] if ite % 2 == 0 else ps[3]
            P.wait("vector", "t", t_avstop[ite])
            vcp = P.op("vector", lambda v: v.tensor_copy(a2_den[:], ps[2][:]), inc="v")
            v_dencp[ite] = vcp
            P.op("vector", lambda v: v.reciprocal(a2_rd[:], a2_den[:]), inc="v")
            vom = P.op("vector", lambda v,
                       o=o_fm[:, tte * 2048 + kvpe * 1024:tte * 2048 + (kvpe + 1) * 1024],
                       a=avp[:]:
                       v.tensor_tensor(o, a, a2_rd[:], ALU.mult), inc="v")
            v_omul[ite] = vom
            if ite % 2 == 1:
                P.wait("tensor", "dWoA", 32)
                P.wait("tensor", "v", vom)
                tw = None
                for eqh in range(2):
                    for hh in range(16):
                        tw = P.op("tensor", lambda t, o=ps[3][:, eqh * 512:(eqh + 1) * 512],
                                  w=o_fm[:, tte * 2048 + hh * 128:tte * 2048 + hh * 128 + 128],
                                  r=wo_a2[:, eqh * 8192 + hh * 512:eqh * 8192 + (hh + 1) * 512],
                                  stt=(hh == 0), sp=(hh == 15):
                                  t.matmul(o, w, r, start=stt, stop=sp),
                                  inc="t" if (hh == 15 and eqh == 1) else None)
                st["t_wo_last"] = tw
                P.wait("vector", "t", tw)
                vev = None
                for eqh in range(2):
                    xsl = Xr[:, tte * 2048 + eqh * 512:tte * 2048 + (eqh + 1) * 512]
                    vev = P.op("vector", lambda v, o=xsl, a=ps[3][:, eqh * 512:(eqh + 1) * 512], b=xsl:
                               v.tensor_tensor(o, a, b, ALU.add), inc="v")
                v_omul[ite] = vev

        it = -1
        for tt in range(8):
            for kvp in range(2):
                kva, kvb = kvp * 2, kvp * 2 + 1
                it += 1
                av_ps = ps[1] if it % 2 == 0 else ps[3]
                qsl = qbuf[it % 2]
                P.wait("tensor", ["dQA", "dQB"][it % 2], P.cnt[["dQA", "dQB"][it % 2]])
                for kb in range(NKB + 1):
                    if kb < NKB:
                        gstep = it * NKB + kb
                        if gstep >= 1:
                            P.wait("tensor", "s", p_tanh[gstep - 1])
                        vt = None
                        for hh, kv in ((0, kva), (1, kvb)):
                            vt = P.op("tensor", lambda t, o=ps[0][:, hh * 512:(hh + 1) * 512],
                                      w=k_fm[:, kv * 3072 + (tt + kb) * 128:kv * 3072 + (tt + kb) * 128 + 128],
                                      r=qsl[:, hh * 512:(hh + 1) * 512]:
                                      t.matmul(o, w, r, start=True, stop=True),
                                      inc="t" if hh == 1 else None)
                        P.wait("scalar", "t", vt)
                        if kb == NKB - 1:
                            nx = it + 2
                            if nx < 16:
                                ttn, kvpn = nx // 2, nx % 2
                                P.wait("sync", "t", vt)
                                P.dma("sync", ["dQA", "dQB"][nx % 2], qbuf[nx % 2][:],
                                      q_dram[:, ttn * 2048 + kvpn * 1024:ttn * 2048 + (kvpn + 1) * 1024])
                        vs = P.op("scalar", lambda s, o=a2_t1[kb % 2][:], i=ps[0][:]:
                                  s.activation(o, i, AF.Tanh, scale=SC_TANH), inc="s")
                        p_tanh[gstep] = vs
                    if kb == 0:
                        continue
                    kx = kb - 1
                    if kx >= 2:
                        P.wait("scalar", "t", t_av[(it, kx - 2)])
                    elif it >= 1:
                        P.wait("scalar", "t", t_av[(it - 1, 15 + kx)])
                    Pt = a2_P[kx % 2]
                    vs = P.op("scalar", lambda s, o=Pt[:], i=a2_t1[kx % 2][:],
                              b=biask[:, tt + kx:tt + kx + 1]:
                              s.activation(o, i, AF.Exp, bias=b, scale=CAP), inc="s")
                    need_v = None
                    if kx == 0 or kx == NKB - 1:
                        msk = maskl if kx == 0 else maskr
                        P.wait("vector", "s", vs)
                        for hh in range(2):
                            need_v = P.op("vector",
                                          lambda v, o=Pt[:, hh * 512:(hh + 1) * 512], b=msk[:]:
                                          v.tensor_tensor(o, o, b, ALU.mult), inc="v")
                    if kx == 0 and it >= 1:
                        emit_a2_epilogue(it - 1)
                    P.wait("tensor", "s", vs)
                    if need_v is not None:
                        P.wait("tensor", "v", need_v)
                    if kx == 0:
                        if it >= 1:
                            P.wait("tensor", "v", v_dencp[it - 1])
                        if it >= 2:
                            P.wait("tensor", "v", v_omul[it - 2])
                    for hh in range(2):
                        P.op("tensor", lambda t, o=ps[2][:, hh * 512:(hh + 1) * 512],
                             r=Pt[:, hh * 512:(hh + 1) * 512],
                             stt=(kx == 0), sp=(kx == NKB - 1):
                             t.matmul(o, ones_bf[:], r, start=stt, stop=sp), inc=None)
                    vt = None
                    for hh, kv in ((0, kva), (1, kvb)):
                        vt = P.op("tensor", lambda t, o=av_ps[:, hh * 512:(hh + 1) * 512],
                                  w=v_tm[:, (tt + kx) * 512 + kv * 128:(tt + kx) * 512 + kv * 128 + 128],
                                  r=Pt[:, hh * 512:(hh + 1) * 512],
                                  stt=(kx == 0), sp=(kx == NKB - 1):
                                  t.matmul(o, w, r, start=stt, stop=sp),
                                  inc="t" if hh == 1 else None)
                    t_av[(it, kx)] = vt
                t_avstop[it] = vt
        emit_a2_epilogue(15)
        _dbg("A2")
        P.barrier()

        P.wait("gpsimd", "t", st["t_wo_last"])
        vwo3 = 0
        for eqh in range(2):
            vwo3 = P.dma("gpsimd", "dWoB", wo_a2[:, eqh * 8192:(eqh + 1) * 8192],
                         dr["wom"][(eqh + 2) * 128:(eqh + 3) * 128, :])
        v_eq_ev = {}
        for tt in range(8):
            P.wait("tensor", "dWoB", vwo3)
            if tt >= 2:
                P.wait("tensor", "v", v_eq_ev[tt - 2])
            pw = ps[tt % 2]
            vt = None
            for eqh in range(2):
                for hh in range(16):
                    vt = P.op("tensor", lambda t, o=pw[:, eqh * 512:(eqh + 1) * 512],
                              w=o_fm[:, tt * 2048 + hh * 128:tt * 2048 + hh * 128 + 128],
                              r=wo_a2[:, eqh * 8192 + hh * 512:eqh * 8192 + (hh + 1) * 512],
                              stt=(hh == 0), sp=(hh == 15):
                              t.matmul(o, w, r, start=stt, stop=sp),
                              inc="t" if (hh == 15 and eqh == 1) else None)
            P.wait("vector", "t", vt)
            vev = None
            for eqh in range(2):
                xsl = Xr[:, tt * 2048 + (eqh + 2) * 512:tt * 2048 + (eqh + 3) * 512]
                vev = P.op("vector", lambda v, o=xsl, a=pw[:, eqh * 512:(eqh + 1) * 512], b=xsl:
                           v.tensor_tensor(o, a, b, ALU.add), inc="v")
            v_eq_ev[tt] = vev
        for cc in range(2):
            P.dma("gpsimd", ["dW1A", "dW1B"][cc], w1s[cc][:], dr["w1m"][cc * 128:(cc + 1) * 128, :])
            P.dma("gpsimd", ["dW1A", "dW1B"][cc], w3s[cc][:], dr["w3m"][cc * 128:(cc + 1) * 128, :])
        _dbg("A3")
        P.barrier()

        v_lastev = None
        for tl in range(8):
            v_lastev = emit_ln_tile(Xr[:, tl * 2048:(tl + 1) * 2048], 1024, tl)
        P.wait("tensor", "v", v_lastev)
        s_silu = {}
        v_hid = {}
        s_ev = {}
        t_p1 = {}

        def emit_pass1(cc):
            P.wait("tensor", "v", v_hid[cc])
            if cc >= 1:
                P.wait("tensor", "s", s_ev[cc - 1])
            vt1 = None
            for th in range(2):
                vt1 = P.op("tensor", lambda t, o=ps[2][:, th * 512:(th + 1) * 512],
                           r=f1_hid[cc % 2][:, th * 512:(th + 1) * 512]:
                           t.matmul(o, h128c[:], r, start=True, stop=True),
                           inc="t" if th == 1 else None)
            t_p1[cc] = vt1
            P.wait("scalar", "t", vt1)
            if cc >= 2:
                P.wait("scalar", "dHid", 16 * (cc - 1))
            vse = P.op("scalar", lambda s, o=f1_h2[cc % 2][:]:
                       s.activation(o, ps[2][:], AF.Identity), inc="s")
            s_ev[cc] = vse
            P.wait("sync", "s", vse)
            P.dma("sync", "dHid", hid2_dram[cc * 128:(cc + 1) * 128, :], f1_h2[cc % 2][:])

        for c in range(64):
            par = c % 2
            P.wait("tensor", ["dW1A", "dW1B"][par], 32 * (c // 2 + 1))
            if c >= 1:
                P.wait("tensor", "s", s_silu[c - 1])
            if c >= 2:
                P.wait("tensor", "v", v_hid[c - 2])
            psL = ps[1] if par == 0 else ps[3]
            vtg = None
            for e in range(16):
                for th in range(2):
                    vtg = P.op("tensor", lambda t, o=ps[0][:, th * 512:(th + 1) * 512],
                               w=w1s[par][:, e * 128:(e + 1) * 128],
                               r=lnF[:, e * 1024 + th * 512:e * 1024 + (th + 1) * 512],
                               stt=(e == 0), sp=(e == 15):
                               t.matmul(o, w, r, start=stt, stop=sp),
                               inc="t" if (e == 15 and th == 1) else None)
            vtl = None
            for e in range(16):
                for th in range(2):
                    vtl = P.op("tensor", lambda t, o=psL[:, th * 512:(th + 1) * 512],
                               w=w3s[par][:, e * 128:(e + 1) * 128],
                               r=lnF[:, e * 1024 + th * 512:e * 1024 + (th + 1) * 512],
                               stt=(e == 0), sp=(e == 15):
                               t.matmul(o, w, r, start=stt, stop=sp),
                               inc="t" if (e == 15 and th == 1) else None)
            if c < 62:
                P.wait("gpsimd", "t", vtl)
                P.dma("gpsimd", ["dW1A", "dW1B"][par], w1s[par][:],
                      dr["w1m"][(c + 2) * 128:(c + 3) * 128, :])
                P.dma("gpsimd", ["dW1A", "dW1B"][par], w3s[par][:],
                      dr["w3m"][(c + 2) * 128:(c + 3) * 128, :])
            if c >= 1:
                emit_pass1(c - 1)
            P.wait("scalar", "t", vtg)
            if c >= 2:
                P.wait("scalar", "v", v_hid[c - 2])
            vss = P.op("scalar", lambda s, o=f1_sg[par][:]: s.activation(o, ps[0][:], AF.Silu), inc="s")
            s_silu[c] = vss
            P.wait("vector", "t", vtl)
            P.wait("vector", "s", vss)
            if c >= 2:
                P.wait("vector", "t", t_p1[c - 2])
            vvh = P.op("vector", lambda v, o=f1_hid[c % 2][:], a=f1_sg[par][:], b=psL[:]:
                       v.tensor_tensor(o, a, b, ALU.mult), inc="v")
            v_hid[c] = vvh
        emit_pass1(63)
        vhid_total = P.cnt["dHid"]
        _dbg("F1")
        P.barrier()

        P.wait("sync", "dHid", vhid_total)
        P.wait("scalar", "dHid", vhid_total)
        h2ap = hid2_dram
        s_p2ev = {}
        t_p2 = {}
        swp_sems = ["dSwpA", "dSwpB", "dSwpC", "dSwpD"]
        for j in range(64):
            sl = j % 4
            q_eng = "sync" if j % 2 == 0 else "scalar"
            if j >= 4:
                P.wait(q_eng, "t", t_p2[j - 4])
            srcap = bass.AP(tensor=h2ap.tensor, offset=h2ap.offset + j * 2048,
                            ap=[[131072, 64], [1024, 2], [1, 1024]])
            vsw = P.dma(q_eng, swp_sems[sl], swp[sl][:], srcap)
            P.wait("tensor", swp_sems[sl], vsw)
            pp = ps[j % 2]
            if j >= 2:
                P.wait("tensor", "s", s_p2ev[j - 2])
            vt = None
            for th in range(2):
                vt = P.op("tensor", lambda t, o=pp[:, th * 512:(th + 1) * 512],
                          r=swp[sl][:, th * 512:(th + 1) * 512]:
                          t.matmul(o, hp2c[:], r, start=True, stop=True),
                          inc="t" if th == 1 else None)
            t_p2[j] = vt
            P.wait("scalar", "t", vt)
            vse = P.op("scalar", lambda s, o=fw[:, j * 1024:(j + 1) * 1024], i=pp[:]:
                       s.activation(o, i, AF.Identity, scale=SC_FWHT), inc="s")
            s_p2ev[j] = vse
        _dbg("FW2a")
        P.barrier()

        po = [half(i, hh) for i in range(4) for hh in range(2)]
        w2_sems = ["dW2A", "dW2B"]
        for parx in range(2):
            P.dma("gpsimd", w2_sems[parx], w2s[parx][:], dr["w2g"][parx * 128:(parx + 1) * 128, :])
        v_ds_ev = {}
        for ds in range(4):
            if ds >= 1:
                P.wait("tensor", "v", v_ds_ev[ds - 1])
            vt = None
            for jg in range(16):
                gi = ds * 16 + jg
                par = gi % 2
                P.wait("tensor", w2_sems[par], 16 * (gi // 2 + 1))
                for j4 in range(4):
                    j = jg * 4 + j4
                    for tt in range(8):
                        vt = P.op("tensor", lambda t, o=po[tt],
                                  w=fw[:, j * 1024 + tt * 128:j * 1024 + tt * 128 + 128],
                                  r=w2s[par][:, j4 * 512:(j4 + 1) * 512],
                                  stt=(jg == 0 and j4 == 0), sp=(jg == 15 and j4 == 3):
                                  t.matmul(o, w, r, start=stt, stop=sp),
                                  inc="t" if (j4 == 3 and tt == 7) else None)
                if gi + 2 < 64:
                    P.wait("gpsimd", "t", vt)
                    P.dma("gpsimd", w2_sems[par], w2s[par][:],
                          dr["w2g"][(gi + 2) * 128:(gi + 3) * 128, :])
            P.wait("vector", "t", vt)
            vev = None
            for tt in range(8):
                xsl = Xr[:, tt * 2048 + ds * 512:tt * 2048 + (ds + 1) * 512]
                vev = P.op("vector", lambda v, o=xsl, a=po[tt], b=xsl:
                           v.tensor_tensor(o, a, b, ALU.add), inc="v")
                P.wait("sync", "v", vev)
                P.dma("sync", "dOut", out[tt * 128:(tt + 1) * 128, ds * 512:(ds + 1) * 512], xsl)
            v_ds_ev[ds] = vev
        P.wait("sync", "dOut", P.cnt["dOut"])

        with nc.Block() as block:
            block.sync(lambda e: [f(e) for f in P.prog["sync"]])
            block.gpsimd(lambda e: [f(e) for f in P.prog["gpsimd"]])
            block.vector(lambda e: [f(e) for f in P.prog["vector"]])
            block.scalar(lambda e: [f(e) for f in P.prog["scalar"]])
            block.tensor(lambda e: [f(e) for f in P.prog["tensor"]])
    return nc


_SHARED = None

def _shared_consts():
    global _SHARED
    if _SHARED is not None:
        return _SHARED
    tri = np.arange(128)
    ml = (tri[:, None] >= tri[None, :]).astype(np.float32)
    mr = (tri[:, None] <= tri[None, :]).astype(np.float32)
    maskL = np.ascontiguousarray(np.tile(ml, (1, 4)).astype(BF))
    maskR = np.ascontiguousarray(np.tile(mr, (1, 4)).astype(BF))
    ident = np.eye(128, dtype=BF)
    identf = np.eye(128, dtype=np.float32)
    h128 = np.ascontiguousarray(_hadamard(128).astype(BF))
    hp2 = np.ascontiguousarray(np.kron(_hadamard(64), np.eye(2)).astype(BF))
    half_d = 64
    inv_freq = 1.0 / (10000.0 ** (np.arange(0, half_d, dtype=np.float64) / half_d))
    angles = np.outer(np.arange(S, dtype=np.float64), inv_freq)
    ang = np.concatenate([angles, angles], axis=1)
    cosT = np.ascontiguousarray(np.cos(ang).T.astype(np.float32))
    sinT = np.ascontiguousarray(np.sin(ang).T.astype(np.float32))
    _SHARED = dict(maskL=maskL, maskR=maskR, ident=ident, identf=identf, h128=h128, hp2=hp2,
                   cosT=cosT, sinT=sinT)
    return _SHARED


def _prep_weights(wq, wk, wv, wo, W1, W2, W3):
    wq4 = wq.reshape(16, 128, 4, 4, 128)
    wqm = np.ascontiguousarray(wq4.transpose(2, 1, 0, 3, 4).reshape(512, 8192))
    wk3 = wk.reshape(16, 128, 512)
    wv3 = wv.reshape(16, 128, 512)
    wkv = np.concatenate([wk3, wv3], axis=2)
    wkvm = np.ascontiguousarray(wkv.transpose(1, 0, 2).reshape(128, 16384))
    wo4 = wo.reshape(16, 128, 4, 512)
    wom = np.ascontiguousarray(wo4.transpose(2, 1, 0, 3).reshape(512, 8192))
    w1r = W1.reshape(64, 128, 16, 128)
    w1m = np.ascontiguousarray(w1r.transpose(0, 3, 2, 1).reshape(8192, 2048))
    w3r = W3.reshape(64, 128, 16, 128)
    w3m = np.ascontiguousarray(w3r.transpose(0, 3, 2, 1).reshape(8192, 2048))
    j_i, c_i, s_i = np.meshgrid(np.arange(64), np.arange(64), np.arange(2), indexing="ij")
    perm = (c_i * 128 + 2 * j_i + s_i).reshape(64, 128)
    W2T = np.ascontiguousarray(W2.T)
    w2g = np.empty((8192, 2048), dtype=np.float32)
    for dsi in range(4):
        for jg in range(16):
            for jj in range(4):
                j = jg * 4 + jj
                w2g[(dsi * 16 + jg) * 128:(dsi * 16 + jg + 1) * 128,
                    jj * 512:(jj + 1) * 512] = W2T[perm[j], dsi * 512:(dsi + 1) * 512]
    return wqm, wkvm, wom, w1m, w3m, np.ascontiguousarray(w2g)


_NC_CACHE = None
_LAST_RES = None

def kernel(L0_attn_hidden_states, L0_attn_wq, L0_attn_wk, L0_attn_wv, L0_attn_wo,
           L0_attn_query_scale, L0_attn_key_scale, L0_ffn_W1, L0_ffn_W2, L0_ffn_W3):
    global _NC_CACHE
    x = np.asarray(L0_attn_hidden_states, dtype=np.float32)
    sh = _shared_consts()
    wqm, wkvm, wom, w1m, w3m, w2g = _prep_weights(
        np.asarray(L0_attn_wq, np.float32), np.asarray(L0_attn_wk, np.float32),
        np.asarray(L0_attn_wv, np.float32), np.asarray(L0_attn_wo, np.float32),
        np.asarray(L0_ffn_W1, np.float32), np.asarray(L0_ffn_W2, np.float32),
        np.asarray(L0_ffn_W3, np.float32))
    qs = np.ascontiguousarray(np.asarray(L0_attn_query_scale, np.float32).reshape(128, 1))
    ks = np.ascontiguousarray(np.asarray(L0_attn_key_scale, np.float32).reshape(128, 1))

    in_maps = []
    for c in range(NCORE):
        seq, jj = c // 4, c % 4
        lo = jj * 1024 - 1024
        x_ext = np.zeros((KEXT, E), np.float32)
        a = max(0, lo); bnd = min(S, lo + KEXT)
        x_ext[a - lo:bnd - lo] = x[seq, a:bnd]
        cos_e = np.zeros((128, KEXT), np.float32)
        sin_e = np.zeros((128, KEXT), np.float32)
        cos_e[:, a - lo:bnd - lo] = sh["cosT"][:, a:bnd]
        sin_e[:, a - lo:bnd - lo] = sh["sinT"][:, a:bnd]
        pos = lo + np.arange(KEXT)
        valid = (pos >= 0) & (pos < S)
        bk = np.ascontiguousarray(
            np.where(valid, 0.0, -30000.0).astype(np.float32).reshape(24, 128).T)
        in_maps.append(dict(
            x_ext=x_ext, cos_in=cos_e, sin_in=sin_e, biask_in=bk,
            qs_in=qs, ks_in=ks,
            ident_in=sh["ident"], identf_in=sh["identf"], h128_in=sh["h128"], hp2_in=sh["hp2"],
            maskl_in=sh["maskL"], maskr_in=sh["maskR"],
            wqm=wqm, wkvm=wkvm, wom=wom, w1m=w1m, w3m=w3m, w2g=w2g))
    if _NC_CACHE is None:
        _NC_CACHE = build_program()
    res = run_bass_kernel_spmd(_NC_CACHE, in_maps, core_ids=list(range(NCORE)))
    global _LAST_RES
    _LAST_RES = res
    outs = [np.asarray(res.results[c]["out"]) for c in range(NCORE)]
    return np.concatenate(outs, axis=0).reshape(B, S, E).astype(np.float32)


# version 08
# speedup vs baseline: 1.0881x; 1.0297x over previous
import numpy as np
import ml_dtypes
from contextlib import ExitStack

import concourse.bass as bass
import concourse.mybir as mybir
from concourse.bass_utils import run_bass_kernel_spmd

F32 = mybir.dt.float32
BF16 = mybir.dt.bfloat16
AF = mybir.ActivationFunctionType
ALU = mybir.AluOpType
BF = ml_dtypes.bfloat16

B, S, E = 2, 4096, 2048
H, KV, D = 16, 4, 128
HID = 8192
NCORE = 8
TOK = 1024
KEXT = 3072
NKB = 17
CAP = 50.0
SC_TANH = float(1.0 / (CAP * np.sqrt(128.0)))
SC_FWHT = float(1.0 / np.sqrt(HID))

OFF_CONST = 16896
OFF_X = 21312
OFF_Q = 86848
OFF_K = 119616
OFF_V = 144192
OFF_EO = 168768
OFF_G = 201536

G_TA = OFF_G + 0
G_TB = OFF_G + 4096
G_SR0 = OFF_G + 8192
G_SR1 = OFF_G + 12288
G_JUNK = OFF_G + 16384
G_CS = OFF_G + 18432

ENGS = ("sync", "gpsimd", "vector", "scalar", "tensor")
DEBUG_HOOK = None


def _hadamard(n):
    h = np.array([[1.0]], dtype=np.float64)
    while h.shape[0] < n:
        h = np.block([[h, h], [h, -h]])
    return h


class Plan:
    def __init__(self, nc, sems):
        self.nc = nc
        self.sems = sems
        self.prog = {e: [] for e in ENGS}
        self.cnt = {name: 0 for name in sems}
        self.nbar = 0

    def em(self, eng, fn):
        self.prog[eng].append(fn)

    def wait(self, eng, sem, val):
        if val is None or val <= 0:
            return
        h = self.sems[sem]
        self.em(eng, lambda e, h=h, v=val: e.wait_ge(h, v))

    def op(self, eng, fn, inc=None):
        if inc is None:
            self.em(eng, fn)
            return None
        h = self.sems[inc]
        self.cnt[inc] += 1
        v = self.cnt[inc]
        self.em(eng, lambda e, fn=fn, h=h: fn(e).then_inc(h, 1))
        return v

    def dma(self, eng, sem, out_ap, in_ap):
        h = self.sems[sem]
        self.cnt[sem] += 16
        v = self.cnt[sem]
        self.em(eng, lambda e, o=out_ap, i=in_ap, h=h: e.dma_start(out=o, in_=i).then_inc(h, 16))
        return v

    def barrier(self):
        self.nbar += 1
        tgt = self.nbar * len(ENGS)
        h = self.sems["bar"]
        for e in ENGS:
            self.em(e, lambda eng, h=h: eng.nop().then_inc(h, 1))
        for e in ENGS:
            self.em(e, lambda eng, h=h, t=tgt: eng.wait_ge(h, t))


def build_program():
    nc = bass.Bass("TRN2", target_bir_lowering=False)

    dr = {}
    def di(name, shape, dtype=F32):
        dr[name] = nc.dram_tensor(name, list(shape), dtype, kind="ExternalInput").ap()
    di("x_ext", (KEXT, E))
    di("cos_in", (128, KEXT)); di("sin_in", (128, KEXT))
    di("biask_in", (128, 24))
    di("qs_in", (128, 1)); di("ks_in", (128, 1))
    di("ident_in", (128, 128), BF16); di("identf_in", (128, 128)); di("h128_in", (128, 128), BF16)
    di("hp2_in", (128, 128), BF16)
    di("maskl_in", (128, 512), BF16); di("maskr_in", (128, 512), BF16)
    di("wqm", (512, 8192))
    di("wkvm", (128, 16384))
    di("wom", (512, 8192))
    di("w1m", (8192, 2048)); di("w3m", (8192, 2048))
    di("w2g", (8192, 2048))
    out = nc.dram_tensor("out", [TOK, E], F32, kind="ExternalOutput").ap()
    hid2_dram = nc.dram_tensor("hid2", [HID, TOK], BF16, kind="Internal").ap()
    q_dram = nc.dram_tensor("qdram", [128, 16384], BF16, kind="Internal").ap()

    A = nc.alloc_sbuf_tensor_at
    ones_bf = A("ones_bf", [128, 128], BF16, offset=OFF_CONST + 0)
    ones_f = A("ones_f", [128, 128], F32, offset=OFF_CONST + 256)
    ident = A("ident", [128, 128], BF16, offset=OFF_CONST + 768)
    identf = A("identf", [128, 128], F32, offset=OFF_CONST + 1024)
    h128c = A("h128c", [128, 128], BF16, offset=OFF_CONST + 1536)
    hp2c = A("hp2c", [128, 128], BF16, offset=OFF_CONST + 1792)
    maskl = A("maskl", [128, 512], BF16, offset=OFF_CONST + 2048)
    maskr = A("maskr", [128, 512], BF16, offset=OFF_CONST + 3072)
    biask = A("biask", [128, 24], F32, offset=OFF_CONST + 4096)
    qs_c = A("qs_c", [128, 1], F32, offset=OFF_CONST + 4224)
    ks_c = A("ks_c", [128, 1], F32, offset=OFF_CONST + 4256)
    eps5 = A("eps5", [128, 1], F32, offset=OFF_CONST + 4288)
    eps6 = A("eps6", [128, 1], F32, offset=OFF_CONST + 4320)
    ln_cols = A("ln_cols", [128, 16], F32, offset=OFF_CONST + 4352)

    Xr = A("Xr", [128, 16384], F32, offset=OFF_X)
    q_stage = [A("q_stage0", [128, 1024], BF16, offset=OFF_Q),
               A("q_stage1", [128, 1024], BF16, offset=OFF_Q + 2048)]
    qbuf = [A("qbuf0", [128, 1024], BF16, offset=G_JUNK),
            A("qbuf1", [128, 1024], BF16, offset=G_CS)]
    k_fm = A("k_fm", [128, 12288], BF16, offset=OFF_K)
    v_tm = A("v_tm", [128, 12288], BF16, offset=OFF_V)
    slotA = A("slotA", [128, 8192], BF16, offset=OFF_K)
    slotB = A("slotB", [128, 8192], BF16, offset=OFF_K + 16384)
    slot_t = [slotA, slotB]
    wkv_s = [A("wkvA", [128, 8192], BF16, offset=OFF_Q),
             A("wkvB", [128, 8192], BF16, offset=OFF_Q + 16384)]
    wo_a2 = A("wo_a2", [128, 16384], BF16, offset=OFF_Q)
    w1s = [A("w1s0", [128, 2048], BF16, offset=OFF_V + 8192),
           A("w1s1", [128, 2048], BF16, offset=OFF_V + 12288)]
    w3s = [A("w3s0", [128, 2048], BF16, offset=OFF_V + 16384),
           A("w3s1", [128, 2048], BF16, offset=OFF_V + 20480)]

    lnF = A("lnF", [128, 16384], BF16, offset=OFF_EO)
    lnH = A("lnH", [128, 8192], BF16, offset=OFF_EO)
    xin = A("xin", [128, 2048], F32, offset=OFF_EO + 16384)
    o_fm = A("o_fm", [128, 16384], BF16, offset=OFF_EO)

    t_a = A("t_a", [128, 1024], F32, offset=G_TA)
    t_b = A("t_b", [128, 1024], F32, offset=G_TB)
    sroot = [A("sroot0", [128, 1024], F32, offset=G_SR0),
             A("sroot1", [128, 1024], F32, offset=G_SR1)]
    junk2 = A("junk2", [128, 2048], F32, offset=G_SR0)
    rn_q = A("rn_q", [128, 1024], F32, offset=OFF_Q + 4096)
    rn_k = A("rn_k", [128, 512], F32, offset=G_SR1 + 2048)
    ln_tmf = [A("ln_tmf0", [128, 2048], F32, offset=G_TA),
              A("ln_tmf1", [128, 2048], F32, offset=OFF_Q + 8192)]
    krope_q = A("krope_q", [128, 1024], F32, offset=OFF_Q + 24576)
    ksq_q = [A("ksq_q0", [128, 1024], F32, offset=OFF_Q + 16384),
             A("ksq_q1", [128, 1024], F32, offset=OFF_Q + 20480)]
    ksq_k = [A("ksq_k0", [128, 512], F32, offset=G_JUNK),
             A("ksq_k1", [128, 512], F32, offset=G_SR0 + 2048)]
    cs_cos = A("cs_cos", [128, 1024], F32, offset=G_CS)
    cs_sin = A("cs_sin", [128, 1024], F32, offset=G_CS + 4096)
    a2_t1 = [A("a2_t1a", [128, 1024], F32, offset=G_TA),
             A("a2_t1b", [128, 1024], F32, offset=G_SR1)]
    a2_P = [A("a2_P0", [128, 1024], BF16, offset=G_TB),
            A("a2_P1", [128, 1024], BF16, offset=G_TB + 2048)]
    a2_rd = A("a2_rd", [128, 1024], F32, offset=G_SR0)
    a2_den = A("a2_den", [128, 1024], F32, offset=G_CS + 2048)
    f1_sg = [A("f1_sga", [128, 1024], F32, offset=G_TA),
             A("f1_sgb", [128, 1024], F32, offset=G_CS)]
    f1_hid = [A("f1_hid0", [128, 1024], BF16, offset=G_TB),
              A("f1_hid1", [128, 1024], BF16, offset=G_TB + 2048)]
    f1_h2 = [A("f1_h2_0", [128, 1024], BF16, offset=G_SR0),
             A("f1_h2_1", [128, 1024], BF16, offset=G_SR0 + 2048)]
    fw = A("fw", [128, 65536], BF16, offset=OFF_Q)
    swp = [A("swp0", [128, 1024], BF16, offset=G_JUNK),
           A("swp1", [128, 1024], BF16, offset=G_CS),
           A("swp2", [128, 1024], BF16, offset=G_CS + 2048),
           A("swp3", [128, 1024], BF16, offset=G_CS + 4096)]
    w2s = [A("w2s0", [128, 2048], BF16, offset=G_JUNK),
           A("w2s1", [128, 2048], BF16, offset=G_CS + 2048)]

    ctx = ExitStack()
    sem_names = ["bar", "t", "v", "s", "dC", "dX", "dXin", "dCS", "dWqA", "dWqB",
                 "dWkvA", "dWkvB", "dWoA", "dWoB", "dW1A", "dW1B", "dHid",
                 "dSwpA", "dSwpB", "dSwpC", "dSwpD", "dW2A", "dW2B", "dOut", "dQA", "dQB", "dQS"]
    with ctx:
        sems = {n: ctx.enter_context(nc.semaphore(n)) for n in sem_names}
        P = Plan(nc, sems)
        ps = [ctx.enter_context(nc.psum_tensor(f"ps{i}", [128, 1024], F32))
              for i in range(4)]
        half = lambda i, hh: ps[i][:, hh * 512:(hh + 1) * 512]

        def _dbg(tag):
            if DEBUG_HOOK is not None:
                DEBUG_HOOK(tag, nc, P, dict(lnF=lnF, lnH=lnH, q_dram=q_dram, k_fm=k_fm,
                                            v_tm=v_tm, o_fm=o_fm, Xr=Xr, fw=fw,
                                            hid2_dram=hid2_dram, ln_cols=ln_cols,
                                            f1_sg=f1_sg[1], f1_hid1=f1_hid[1], f1_h2_1=f1_h2[1],
                                            biask=biask, cs_cos=cs_cos, cs_sin=cs_sin))

        st = {
            "ln_tiny_v": {0: 0, 1: 0},
            "ln_norm_s": {0: 0, 1: 0},
            "ln_tm_t": {0: 0, 1: 0},
            "ln_evac_v": 0,
            "rope_ksq_t": {0: 0, 1: 0},
            "rope_fin_v": 0,
        }

        vC = 0
        for name, dst in [("ident_in", ident), ("identf_in", identf), ("h128_in", h128c), ("hp2_in", hp2c),
                          ("maskl_in", maskl), ("maskr_in", maskr), ("biask_in", biask),
                          ("qs_in", qs_c), ("ks_in", ks_c)]:
            vC = P.dma("sync", "dC", dst[:], dr[name])
        vC = P.dma("sync", "dC", cs_cos[:], dr["cos_in"][:, 1024:2048])
        vC = P.dma("sync", "dC", cs_sin[:], dr["sin_in"][:, 1024:2048])
        vXt = []
        for tl in range(8):
            vXt.append(P.dma("sync", "dX", Xr[:, tl * 2048:(tl + 1) * 2048],
                             dr["x_ext"][1024 + tl * 128:1024 + (tl + 1) * 128, :]))
        vWq = {0: P.dma("gpsimd", "dWqA", slotA[:], dr["wqm"][0:128, :]),
               1: P.dma("gpsimd", "dWqB", slotB[:], dr["wqm"][128:256, :])}
        P.op("vector", lambda v: v.memset(ones_bf[:], 1.0), inc="v")
        P.op("vector", lambda v: v.memset(ones_f[:], 1.0), inc="v")
        P.op("vector", lambda v: v.memset(eps5[:], 1e-5), inc="v")
        vMem = P.op("vector", lambda v: v.memset(eps6[:], 1e-6), inc="v")

        def emit_ln_tile(x_ap, dst_pitch, tl, wait_dma=None, dbuf=True):
            par = tl % 2
            lpar = par if dbuf else 0
            lnt = ln_tmf[lpar]
            c0 = par * 8
            s1c = ln_cols[:, c0:c0 + 1]; s2c = ln_cols[:, c0 + 1:c0 + 2]
            mc = ln_cols[:, c0 + 2:c0 + 3]; vc = ln_cols[:, c0 + 3:c0 + 4]
            rc = ln_cols[:, c0 + 4:c0 + 5]; nbc = ln_cols[:, c0 + 5:c0 + 6]
            src = ln_cols[:, c0 + 6:c0 + 7]
            if wait_dma is not None:
                P.wait("scalar", wait_dma[0], wait_dma[1])
                P.wait("vector", wait_dma[0], wait_dma[1])
            P.wait("vector", "t", st["ln_tm_t"][lpar])
            P.wait("vector", "v", st["rope_fin_v"])
            P.wait("vector", "t", max(st["rope_ksq_t"].values()))
            P.op("vector", lambda v, x=x_ap: v.tensor_scalar(
                lnt[:], x, 1.0, None, ALU.mult, ALU.add, accum_out=s1c), inc="v")
            jout = lnt if dbuf else junk2
            vsum = P.op("vector", lambda v, x=x_ap, o=jout: v.scalar_tensor_tensor(
                o[:], x, 1.0, x, ALU.mult, ALU.mult, accum_out=s2c), inc="v")
            P.wait("vector", "s", st["ln_norm_s"][par])
            P.wait("vector", "v", vsum)
            P.op("vector", lambda v: v.tensor_scalar(mc, s1c, 1.0 / 2048.0, None, ALU.mult), inc="v")
            vw = P.op("vector", lambda v: v.tensor_scalar(vc, s2c, 1.0 / 2048.0, None, ALU.mult), inc="v")
            P.wait("vector", "v", vw)
            vw = P.op("vector", lambda v: v.tensor_tensor(s1c, mc, mc, ALU.mult), inc="v")
            P.wait("vector", "v", vw)
            vv = P.op("vector", lambda v: v.tensor_tensor(vc, vc, s1c, ALU.subtract), inc="v")
            P.wait("scalar", "v", vv)
            vs = P.op("scalar", lambda s: s.activation(src, vc, AF.Ln, bias=eps5[:]), inc="s")
            P.wait("scalar", "s", vs)
            vs = P.op("scalar", lambda s: s.activation(rc, src, AF.Exp, scale=-0.5), inc="s")
            P.wait("vector", "s", vs)
            vtiny = P.op("vector", lambda v: v.tensor_scalar(nbc, mc, rc, -1.0, ALU.mult, ALU.mult), inc="v")
            st["ln_tiny_v"][par] = vtiny
            P.wait("scalar", "v", vtiny)
            vs = P.op("scalar", lambda s, x=x_ap: s.activation(lnt[:], x, AF.Identity, bias=nbc, scale=rc), inc="s")
            st["ln_norm_s"][par] = vs
            P.wait("tensor", "s", vs)
            P.wait("tensor", "v", st["ln_evac_v"])
            vt = None
            for e in range(16):
                pt = ps[e // 8][:, (e % 8) * 128:(e % 8) * 128 + 128]
                vt = P.op("tensor", lambda t, o=pt, i=lnt[:, e * 128:(e + 1) * 128]:
                          t.transpose(o, i, identf[:]),
                          inc="t" if e in (7, 15) else None)
            st["ln_tm_t"][lpar] = vt
            P.wait("vector", "t", vt)
            dtensor = lnF if dst_pitch == 1024 else lnH
            dap = dtensor.ap()
            v_ev = None
            for g in range(2):
                dst = bass.AP(tensor=dap.tensor,
                              offset=dap.offset + g * 8 * dst_pitch + tl * 128,
                              ap=[[dap.ap[0][0], 128], [dst_pitch, 8], [1, 128]])
                v_ev = P.op("vector", lambda v, o=dst, i=ps[g][:]: v.tensor_copy(o, i), inc="v")
            st["ln_evac_v"] = v_ev
            return v_ev

        def rope_front(pq_full, n, cs_off, t_stop, kpar):
            co = cs_cos[:, cs_off:cs_off + n]; si = cs_sin[:, cs_off:cs_off + n]
            tb = t_b[:, 0:n]
            P.wait("vector", "t", t_stop)
            P.op("vector", lambda v, o=t_a[0:64, 0:n], a=pq_full[64:128, :], b=si[0:64, :]:
                 v.tensor_tensor(o, a, b, ALU.mult), inc="v")
            P.op("vector", lambda v, o=t_a[64:128, 0:n], a=pq_full[0:64, :], b=si[64:128, :]:
                 v.tensor_tensor(o, a, b, ALU.mult), inc="v")
            P.op("vector", lambda v, o=tb, a=pq_full, b=co:
                 v.tensor_tensor(o, a, b, ALU.mult), inc="v")
            P.op("vector", lambda v, o=t_b[0:64, 0:n], a=t_b[0:64, 0:n], b=t_a[0:64, 0:n]:
                 v.tensor_tensor(o, a, b, ALU.subtract), inc="v")
            P.op("vector", lambda v, o=t_b[64:128, 0:n], a=t_b[64:128, 0:n], b=t_a[64:128, 0:n]:
                 v.tensor_tensor(o, a, b, ALU.add), inc="v")
            if n == 1024:
                kr = krope_q[:, 0:n]
                P.op("vector", lambda v, o=kr, i=tb: v.tensor_copy(o, i), inc="v")
            else:
                kr = tb
            ksq = (ksq_q if n == 1024 else ksq_k)[kpar][:, 0:n]
            P.wait("vector", "t", st["rope_ksq_t"][kpar])
            vk = None
            nch = (n + 511) // 512
            for ch in range(nch):
                w = min(512, n - ch * 512)
                vk = P.op("vector", lambda v, o=ksq[:, ch * 512:ch * 512 + w],
                          a=kr[:, ch * 512:ch * 512 + w]:
                          v.tensor_tensor(o, a, a, ALU.mult), inc="v")
            return vk

        def rope_tail(pq_full, n, scale_col, dst_ap, sr_par, kpar, vk):
            ksq = (ksq_q if n == 1024 else ksq_k)[kpar][:, 0:n]
            kr = krope_q[:, 0:n] if n == 1024 else t_b[:, 0:n]
            P.wait("tensor", "v", vk)
            vt = None
            nch = (n + 511) // 512
            for ch in range(nch):
                w = min(512, n - ch * 512)
                vt = P.op("tensor", lambda t, o=pq_full[:, ch * 512:ch * 512 + w],
                          i=ksq[:, ch * 512:ch * 512 + w]:
                          t.matmul(o, ones_f[:], i, start=True, stop=True),
                          inc="t" if ch == nch - 1 else None)
            st["rope_ksq_t"][kpar] = vt
            P.wait("scalar", "t", vt)
            sr = sroot[sr_par][:, 0:n]
            rb = (rn_q if n == 1024 else rn_k)[:, 0:n]
            vs = P.op("scalar", lambda s, o=sr, i=pq_full: s.activation(o, i, AF.Ln), inc="s")
            P.wait("scalar", "s", vs)
            vs = P.op("scalar", lambda s, o=rb, i=sr: s.activation(o, i, AF.Exp, scale=0.5), inc="s")
            P.wait("scalar", "s", vs)
            vs = P.op("scalar", lambda s, o=sr, i=rb: s.activation(o, i, AF.Ln, bias=eps6[:]), inc="s")
            P.wait("scalar", "s", vs)
            vs = P.op("scalar", lambda s, o=rb, i=sr: s.activation(o, i, AF.Exp, scale=-1.0), inc="s")
            P.wait("vector", "s", vs)
            vfin = P.op("vector", lambda v, o=dst_ap, a=kr, sc=scale_col, b=rb:
                        v.scalar_tensor_tensor(o, a, sc, b, ALU.mult, ALU.mult), inc="v")
            st["rope_fin_v"] = vfin
            return vfin, vs

        v_lastev = None
        for tl in range(8):
            v_lastev = emit_ln_tile(Xr[:, tl * 2048:(tl + 1) * 2048], 1024, tl,
                                    wait_dma=("dX", vXt[tl]))
        P.wait("tensor", "dC", vC)
        P.wait("tensor", "v", vMem)
        P.wait("tensor", "v", v_lastev)
        q_sems = ["dWqA", "dWqB"]
        sq_read = [0, 0, 0, 0]
        pend = None
        for h in range(16):
            qr, mloc = h // 4, h % 4
            par = qr % 2
            if mloc == 0:
                P.wait("tensor", q_sems[par], 16 * (qr // 2 + 1))
            pq = ps[h % 4]
            P.wait("tensor", "s", sq_read[h % 4])
            vt = None
            for e in range(16):
                for th in range(2):
                    vt = P.op("tensor", lambda t, o=pq[:, th * 512:(th + 1) * 512],
                              w=slot_t[par][:, e * 512 + mloc * 128:e * 512 + mloc * 128 + 128],
                              r=lnF[:, e * 1024 + th * 512:e * 1024 + (th + 1) * 512],
                              stt=(e == 0), sp=(e == 15):
                              t.matmul(o, w, r, start=stt, stop=sp),
                              inc="t" if (e == 15 and th == 1) else None)
            if mloc == 3 and qr < 2:
                P.wait("gpsimd", "t", vt)
                vWq[par] = P.dma("gpsimd", q_sems[par], slot_t[par][:],
                                 dr["wqm"][(qr + 2) * 128:(qr + 3) * 128, :])
            if pend is not None:
                hp, pqp, vkp = pend
                _, sq = rope_tail(pqp[:], 1024, qs_c[:], q_stage[hp % 2][:], hp % 2, hp % 2, vkp)
                sq_read[hp % 4] = sq
                P.wait("sync", "v", P.cnt["v"])
                qdst = bass.AP(tensor=q_dram.tensor, offset=q_dram.offset + hp * 128,
                               ap=[[16384, 128], [2048, 8], [1, 128]])
                P.dma("sync", ["dQA", "dQB"][hp % 2], qdst, q_stage[hp % 2][:])
            if h >= 2:
                P.wait("vector", ["dQA", "dQB"][h % 2], 16 * (h // 2))
            vk = rope_front(pq[:], 1024, 0, vt, h % 2)
            pend = (h, pq, vk)
        hp, pqp, vkp = pend
        _, sq = rope_tail(pqp[:], 1024, qs_c[:], q_stage[hp % 2][:], hp % 2, hp % 2, vkp)
        sq_read[hp % 4] = sq
        P.wait("sync", "v", P.cnt["v"])
        qdst = bass.AP(tensor=q_dram.tensor, offset=q_dram.offset + hp * 128,
                       ap=[[16384, 128], [2048, 8], [1, 128]])
        P.dma("sync", ["dQA", "dQB"][hp % 2], qdst, q_stage[hp % 2][:])
        _dbg("A1a")
        P.barrier()

        P.wait("gpsimd", "dQA", P.cnt["dQA"])
        P.wait("gpsimd", "dQB", P.cnt["dQB"])
        for g in range(2):
            P.dma("gpsimd", ["dWkvA", "dWkvB"][g], wkv_s[g][:],
                  dr["wkvm"][:, g * 8192:(g + 1) * 8192])
        kv_sq = [0, 0, 0, 0]
        kv_vev = [0, 0, 0, 0]

        def emit_kv_halfslab(ln_tensor, ln_pitch, th, slab, vln_done):
            pk = [half(0, 0), half(0, 1), half(1, 0), half(1, 1)]
            pv = [half(2, 0), half(2, 1), half(3, 0), half(3, 1)]
            P.wait("tensor", "s", max(kv_sq))
            P.wait("tensor", "v", max(kv_vev))
            if vln_done is not None:
                P.wait("tensor", "v", vln_done)
            P.wait("tensor", "dWkvA", P.cnt["dWkvA"])
            P.wait("tensor", "dWkvB", P.cnt["dWkvB"])
            tk = None
            for g in range(2):
                for eL in range(8):
                    e = g * 8 + eL
                    rhs_off = e * ln_pitch + (th * 512 if ln_pitch == 1024 else 0)
                    for kv in range(4):
                        tk = P.op("tensor", lambda t, o=pk[kv],
                                  w=wkv_s[g][:, eL * 1024 + kv * 128:eL * 1024 + kv * 128 + 128],
                                  r=ln_tensor[:, rhs_off:rhs_off + 512],
                                  stt=(e == 0), sp=(e == 15):
                                  t.matmul(o, w, r, start=stt, stop=sp),
                                  inc="t" if (e == 15 and kv == 3) else None)
            vt = None
            for g in range(2):
                for eL in range(8):
                    e = g * 8 + eL
                    rhs_off = e * ln_pitch + (th * 512 if ln_pitch == 1024 else 0)
                    for vtile in range(4):
                        vt = P.op("tensor", lambda t, o=pv[vtile],
                                  w=ln_tensor[:, rhs_off + vtile * 128:rhs_off + vtile * 128 + 128],
                                  r=wkv_s[g][:, eL * 1024 + 512:eL * 1024 + 1024],
                                  stt=(e == 0), sp=(e == 15):
                                  t.matmul(o, w, r, start=stt, stop=sp),
                                  inc="t" if (e == 15 and vtile == 3) else None)
                    if e % 4 == 1:
                        kv = e // 4
                        dst = k_fm[:, kv * 3072 + slab * 1024 + th * 512:
                                   kv * 3072 + slab * 1024 + (th + 1) * 512]
                        vkf = rope_front(pk[kv], 512, th * 512, tk, kv % 2)
                        _, sq = rope_tail(pk[kv], 512, ks_c[:], dst, kv % 2, kv % 2, vkf)
                        kv_sq[kv] = sq
            P.wait("vector", "t", vt)
            for vtile in range(4):
                gidx = slab * 8 + th * 4 + vtile
                kv_vev[vtile] = P.op("vector",
                                     lambda v, o=v_tm[:, gidx * 512:(gidx + 1) * 512], i=pv[vtile]:
                                     v.tensor_copy(o, i), inc="v")

        for th in range(2):
            emit_kv_halfslab(lnF, 1024, th, 1, None)
        for sl in (0, 2):
            P.wait("sync", "v", P.cnt["v"])
            P.dma("sync", "dCS", cs_cos[:], dr["cos_in"][:, sl * 1024:(sl + 1) * 1024])
            vcs = P.dma("sync", "dCS", cs_sin[:], dr["sin_in"][:, sl * 1024:(sl + 1) * 1024])
            P.wait("vector", "dCS", vcs)
            for th in range(2):
                vln = None
                for tl4 in range(4):
                    P.wait("sync", "s", P.cnt["s"])
                    vxin = P.dma("sync", "dXin", xin[:],
                                 dr["x_ext"][sl * 1024 + (th * 4 + tl4) * 128:
                                             sl * 1024 + (th * 4 + tl4 + 1) * 128, :])
                    vln = emit_ln_tile(xin[:], 512, tl4, wait_dma=("dXin", vxin), dbuf=False)
                emit_kv_halfslab(lnH, 512, th, sl, vln)
        _dbg("A1b")
        P.barrier()

        p_tanh = {}
        t_avstop = {}
        t_av = {}
        v_dencp = {}
        v_omul = {}
        P.wait("sync", "dQA", P.cnt["dQA"])
        P.wait("sync", "dQB", P.cnt["dQB"])
        for eqh in range(2):
            P.dma("gpsimd", "dWoA", wo_a2[:, eqh * 8192:(eqh + 1) * 8192],
                  dr["wom"][eqh * 128:(eqh + 1) * 128, :])
        st["t_wo_last"] = 0
        for it0 in range(2):
            tt0, kvp0 = it0 // 2, it0 % 2
            P.dma("sync", ["dQA", "dQB"][it0 % 2], qbuf[it0 % 2][:],
                  q_dram[:, tt0 * 2048 + kvp0 * 1024:tt0 * 2048 + (kvp0 + 1) * 1024])
        def emit_a2_epilogue(ite):
            tte, kvpe = ite // 2, ite % 2
            avp = ps[1] if ite % 2 == 0 else ps[3]
            P.wait("vector", "t", t_avstop[ite])
            vcp = P.op("vector", lambda v: v.tensor_copy(a2_den[:], ps[2][:]), inc="v")
            v_dencp[ite] = vcp
            P.op("vector", lambda v: v.reciprocal(a2_rd[:], a2_den[:]), inc="v")
            vom = P.op("vector", lambda v,
                       o=o_fm[:, tte * 2048 + kvpe * 1024:tte * 2048 + (kvpe + 1) * 1024],
                       a=avp[:]:
                       v.tensor_tensor(o, a, a2_rd[:], ALU.mult), inc="v")
            v_omul[ite] = vom

        def emit_a2_wo(ite):
            tte = ite // 2
            if True:
                P.wait("tensor", "dWoA", 32)
                P.wait("tensor", "v", v_omul[ite])
                tw = None
                for eqh in range(2):
                    for hh in range(16):
                        tw = P.op("tensor", lambda t, o=ps[3][:, eqh * 512:(eqh + 1) * 512],
                                  w=o_fm[:, tte * 2048 + hh * 128:tte * 2048 + hh * 128 + 128],
                                  r=wo_a2[:, eqh * 8192 + hh * 512:eqh * 8192 + (hh + 1) * 512],
                                  stt=(hh == 0), sp=(hh == 15):
                                  t.matmul(o, w, r, start=stt, stop=sp),
                                  inc="t" if (hh == 15 and eqh == 1) else None)
                st["t_wo_last"] = tw
                P.wait("vector", "t", tw)
                vev = None
                for eqh in range(2):
                    xsl = Xr[:, tte * 2048 + eqh * 512:tte * 2048 + (eqh + 1) * 512]
                    vev = P.op("vector", lambda v, o=xsl, a=ps[3][:, eqh * 512:(eqh + 1) * 512], b=xsl:
                               v.tensor_tensor(o, a, b, ALU.add), inc="v")
                v_omul[ite] = vev

        it = -1
        for tt in range(8):
            for kvp in range(2):
                kva, kvb = kvp * 2, kvp * 2 + 1
                it += 1
                av_ps = ps[1] if it % 2 == 0 else ps[3]
                qsl = qbuf[it % 2]
                P.wait("tensor", ["dQA", "dQB"][it % 2], P.cnt[["dQA", "dQB"][it % 2]])
                for kb in range(NKB + 1):
                    if kb < NKB:
                        gstep = it * NKB + kb
                        if gstep >= 1:
                            P.wait("tensor", "s", p_tanh[gstep - 1])
                        vt = None
                        for hh, kv in ((0, kva), (1, kvb)):
                            vt = P.op("tensor", lambda t, o=ps[0][:, hh * 512:(hh + 1) * 512],
                                      w=k_fm[:, kv * 3072 + (tt + kb) * 128:kv * 3072 + (tt + kb) * 128 + 128],
                                      r=qsl[:, hh * 512:(hh + 1) * 512]:
                                      t.matmul(o, w, r, start=True, stop=True),
                                      inc="t" if hh == 1 else None)
                        P.wait("scalar", "t", vt)
                        if kb == NKB - 1:
                            nx = it + 2
                            if nx < 16:
                                ttn, kvpn = nx // 2, nx % 2
                                P.wait("sync", "t", vt)
                                P.dma("sync", ["dQA", "dQB"][nx % 2], qbuf[nx % 2][:],
                                      q_dram[:, ttn * 2048 + kvpn * 1024:ttn * 2048 + (kvpn + 1) * 1024])
                        vs = P.op("scalar", lambda s, o=a2_t1[kb % 2][:], i=ps[0][:]:
                                  s.activation(o, i, AF.Tanh, scale=SC_TANH), inc="s")
                        p_tanh[gstep] = vs
                    if kb == 0:
                        continue
                    kx = kb - 1
                    if kx >= 2:
                        P.wait("scalar", "t", t_av[(it, kx - 2)])
                    elif it >= 1:
                        P.wait("scalar", "t", t_av[(it - 1, 15 + kx)])
                    Pt = a2_P[kx % 2]
                    vs = P.op("scalar", lambda s, o=Pt[:], i=a2_t1[kx % 2][:],
                              b=biask[:, tt + kx:tt + kx + 1]:
                              s.activation(o, i, AF.Exp, bias=b, scale=CAP), inc="s")
                    need_v = None
                    if kx == 0 or kx == NKB - 1:
                        msk = maskl if kx == 0 else maskr
                        P.wait("vector", "s", vs)
                        for hh in range(2):
                            need_v = P.op("vector",
                                          lambda v, o=Pt[:, hh * 512:(hh + 1) * 512], b=msk[:]:
                                          v.tensor_tensor(o, o, b, ALU.mult), inc="v")
                    if kx == 0 and it >= 1:
                        emit_a2_epilogue(it - 1)
                    if kx == 4 and it >= 1 and (it - 1) % 2 == 1:
                        emit_a2_wo(it - 1)
                    P.wait("tensor", "s", vs)
                    if need_v is not None:
                        P.wait("tensor", "v", need_v)
                    if kx == 0:
                        if it >= 1:
                            P.wait("tensor", "v", v_dencp[it - 1])
                        if it >= 2:
                            P.wait("tensor", "v", v_omul[it - 2])
                    for hh in range(2):
                        P.op("tensor", lambda t, o=ps[2][:, hh * 512:(hh + 1) * 512],
                             r=Pt[:, hh * 512:(hh + 1) * 512],
                             stt=(kx == 0), sp=(kx == NKB - 1):
                             t.matmul(o, ones_bf[:], r, start=stt, stop=sp), inc=None)
                    vt = None
                    for hh, kv in ((0, kva), (1, kvb)):
                        vt = P.op("tensor", lambda t, o=av_ps[:, hh * 512:(hh + 1) * 512],
                                  w=v_tm[:, (tt + kx) * 512 + kv * 128:(tt + kx) * 512 + kv * 128 + 128],
                                  r=Pt[:, hh * 512:(hh + 1) * 512],
                                  stt=(kx == 0), sp=(kx == NKB - 1):
                                  t.matmul(o, w, r, start=stt, stop=sp),
                                  inc="t" if hh == 1 else None)
                    t_av[(it, kx)] = vt
                t_avstop[it] = vt
        emit_a2_epilogue(15)
        emit_a2_wo(15)
        _dbg("A2")
        P.barrier()

        P.wait("gpsimd", "t", st["t_wo_last"])
        vwo3 = 0
        for eqh in range(2):
            vwo3 = P.dma("gpsimd", "dWoB", wo_a2[:, eqh * 8192:(eqh + 1) * 8192],
                         dr["wom"][(eqh + 2) * 128:(eqh + 3) * 128, :])
        v_eq_ev = {}
        for tt in range(8):
            P.wait("tensor", "dWoB", vwo3)
            if tt >= 2:
                P.wait("tensor", "v", v_eq_ev[tt - 2])
            pw = ps[tt % 2]
            vt = None
            for eqh in range(2):
                for hh in range(16):
                    vt = P.op("tensor", lambda t, o=pw[:, eqh * 512:(eqh + 1) * 512],
                              w=o_fm[:, tt * 2048 + hh * 128:tt * 2048 + hh * 128 + 128],
                              r=wo_a2[:, eqh * 8192 + hh * 512:eqh * 8192 + (hh + 1) * 512],
                              stt=(hh == 0), sp=(hh == 15):
                              t.matmul(o, w, r, start=stt, stop=sp),
                              inc="t" if (hh == 15 and eqh == 1) else None)
            P.wait("vector", "t", vt)
            vev = None
            for eqh in range(2):
                xsl = Xr[:, tt * 2048 + (eqh + 2) * 512:tt * 2048 + (eqh + 3) * 512]
                vev = P.op("vector", lambda v, o=xsl, a=pw[:, eqh * 512:(eqh + 1) * 512], b=xsl:
                           v.tensor_tensor(o, a, b, ALU.add), inc="v")
            v_eq_ev[tt] = vev
        for cc in range(2):
            P.dma("gpsimd", ["dW1A", "dW1B"][cc], w1s[cc][:], dr["w1m"][cc * 128:(cc + 1) * 128, :])
            P.dma("gpsimd", ["dW1A", "dW1B"][cc], w3s[cc][:], dr["w3m"][cc * 128:(cc + 1) * 128, :])
        _dbg("A3")
        P.barrier()

        v_lastev = None
        for tl in range(8):
            v_lastev = emit_ln_tile(Xr[:, tl * 2048:(tl + 1) * 2048], 1024, tl)
        P.wait("tensor", "v", v_lastev)
        s_silu = {}
        v_hid = {}
        s_ev = {}
        t_p1 = {}

        def emit_pass1(cc):
            P.wait("tensor", "v", v_hid[cc])
            if cc >= 1:
                P.wait("tensor", "s", s_ev[cc - 1])
            vt1 = None
            for th in range(2):
                vt1 = P.op("tensor", lambda t, o=ps[2][:, th * 512:(th + 1) * 512],
                           r=f1_hid[cc % 2][:, th * 512:(th + 1) * 512]:
                           t.matmul(o, h128c[:], r, start=True, stop=True),
                           inc="t" if th == 1 else None)
            t_p1[cc] = vt1
            P.wait("scalar", "t", vt1)
            if cc >= 2:
                P.wait("scalar", "dHid", 16 * (cc - 1))
            vse = P.op("scalar", lambda s, o=f1_h2[cc % 2][:]:
                       s.activation(o, ps[2][:], AF.Identity), inc="s")
            s_ev[cc] = vse
            P.wait("sync", "s", vse)
            P.dma("sync", "dHid", hid2_dram[cc * 128:(cc + 1) * 128, :], f1_h2[cc % 2][:])

        for c in range(64):
            par = c % 2
            P.wait("tensor", ["dW1A", "dW1B"][par], 32 * (c // 2 + 1))
            if c >= 1:
                P.wait("tensor", "s", s_silu[c - 1])
            if c >= 2:
                P.wait("tensor", "v", v_hid[c - 2])
            psL = ps[1] if par == 0 else ps[3]
            vtg = None
            for e in range(16):
                for th in range(2):
                    vtg = P.op("tensor", lambda t, o=ps[0][:, th * 512:(th + 1) * 512],
                               w=w1s[par][:, e * 128:(e + 1) * 128],
                               r=lnF[:, e * 1024 + th * 512:e * 1024 + (th + 1) * 512],
                               stt=(e == 0), sp=(e == 15):
                               t.matmul(o, w, r, start=stt, stop=sp),
                               inc="t" if (e == 15 and th == 1) else None)
            vtl = None
            for e in range(16):
                for th in range(2):
                    vtl = P.op("tensor", lambda t, o=psL[:, th * 512:(th + 1) * 512],
                               w=w3s[par][:, e * 128:(e + 1) * 128],
                               r=lnF[:, e * 1024 + th * 512:e * 1024 + (th + 1) * 512],
                               stt=(e == 0), sp=(e == 15):
                               t.matmul(o, w, r, start=stt, stop=sp),
                               inc="t" if (e == 15 and th == 1) else None)
            if c < 62:
                P.wait("gpsimd", "t", vtl)
                P.dma("gpsimd", ["dW1A", "dW1B"][par], w1s[par][:],
                      dr["w1m"][(c + 2) * 128:(c + 3) * 128, :])
                P.dma("gpsimd", ["dW1A", "dW1B"][par], w3s[par][:],
                      dr["w3m"][(c + 2) * 128:(c + 3) * 128, :])
            if c >= 1:
                emit_pass1(c - 1)
            P.wait("scalar", "t", vtg)
            if c >= 2:
                P.wait("scalar", "v", v_hid[c - 2])
            vss = P.op("scalar", lambda s, o=f1_sg[par][:]: s.activation(o, ps[0][:], AF.Silu), inc="s")
            s_silu[c] = vss
            P.wait("vector", "t", vtl)
            P.wait("vector", "s", vss)
            if c >= 2:
                P.wait("vector", "t", t_p1[c - 2])
            vvh = P.op("vector", lambda v, o=f1_hid[c % 2][:], a=f1_sg[par][:], b=psL[:]:
                       v.tensor_tensor(o, a, b, ALU.mult), inc="v")
            v_hid[c] = vvh
        emit_pass1(63)
        vhid_total = P.cnt["dHid"]
        _dbg("F1")
        P.barrier()

        P.wait("sync", "dHid", vhid_total)
        P.wait("scalar", "dHid", vhid_total)
        h2ap = hid2_dram
        s_p2ev = {}
        t_p2 = {}
        swp_sems = ["dSwpA", "dSwpB", "dSwpC", "dSwpD"]
        for j in range(64):
            sl = j % 4
            q_eng = "sync" if j % 2 == 0 else "scalar"
            if j >= 4:
                P.wait(q_eng, "t", t_p2[j - 4])
            srcap = bass.AP(tensor=h2ap.tensor, offset=h2ap.offset + j * 2048,
                            ap=[[131072, 64], [1024, 2], [1, 1024]])
            vsw = P.dma(q_eng, swp_sems[sl], swp[sl][:], srcap)
            P.wait("tensor", swp_sems[sl], vsw)
            pp = ps[j % 2]
            if j >= 2:
                P.wait("tensor", "s", s_p2ev[j - 2])
            vt = None
            for th in range(2):
                vt = P.op("tensor", lambda t, o=pp[:, th * 512:(th + 1) * 512],
                          r=swp[sl][:, th * 512:(th + 1) * 512]:
                          t.matmul(o, hp2c[:], r, start=True, stop=True),
                          inc="t" if th == 1 else None)
            t_p2[j] = vt
            P.wait("scalar", "t", vt)
            vse = P.op("scalar", lambda s, o=fw[:, j * 1024:(j + 1) * 1024], i=pp[:]:
                       s.activation(o, i, AF.Identity, scale=SC_FWHT), inc="s")
            s_p2ev[j] = vse
        _dbg("FW2a")
        P.barrier()

        po = [half(i, hh) for i in range(4) for hh in range(2)]
        w2_sems = ["dW2A", "dW2B"]
        for parx in range(2):
            P.dma("gpsimd", w2_sems[parx], w2s[parx][:], dr["w2g"][parx * 128:(parx + 1) * 128, :])
        v_ds_ev = {}
        for ds in range(4):
            if ds >= 1:
                P.wait("tensor", "v", v_ds_ev[ds - 1])
            vt = None
            for jg in range(16):
                gi = ds * 16 + jg
                par = gi % 2
                P.wait("tensor", w2_sems[par], 16 * (gi // 2 + 1))
                for j4 in range(4):
                    j = jg * 4 + j4
                    for tt in range(8):
                        vt = P.op("tensor", lambda t, o=po[tt],
                                  w=fw[:, j * 1024 + tt * 128:j * 1024 + tt * 128 + 128],
                                  r=w2s[par][:, j4 * 512:(j4 + 1) * 512],
                                  stt=(jg == 0 and j4 == 0), sp=(jg == 15 and j4 == 3):
                                  t.matmul(o, w, r, start=stt, stop=sp),
                                  inc="t" if (j4 == 3 and tt == 7) else None)
                if gi + 2 < 64:
                    P.wait("gpsimd", "t", vt)
                    P.dma("gpsimd", w2_sems[par], w2s[par][:],
                          dr["w2g"][(gi + 2) * 128:(gi + 3) * 128, :])
            P.wait("vector", "t", vt)
            vev = None
            for tt in range(8):
                xsl = Xr[:, tt * 2048 + ds * 512:tt * 2048 + (ds + 1) * 512]
                vev = P.op("vector", lambda v, o=xsl, a=po[tt], b=xsl:
                           v.tensor_tensor(o, a, b, ALU.add), inc="v")
                P.wait("sync", "v", vev)
                P.dma("sync", "dOut", out[tt * 128:(tt + 1) * 128, ds * 512:(ds + 1) * 512], xsl)
            v_ds_ev[ds] = vev
        P.wait("sync", "dOut", P.cnt["dOut"])

        with nc.Block() as block:
            block.sync(lambda e: [f(e) for f in P.prog["sync"]])
            block.gpsimd(lambda e: [f(e) for f in P.prog["gpsimd"]])
            block.vector(lambda e: [f(e) for f in P.prog["vector"]])
            block.scalar(lambda e: [f(e) for f in P.prog["scalar"]])
            block.tensor(lambda e: [f(e) for f in P.prog["tensor"]])
    return nc


_SHARED = None

def _shared_consts():
    global _SHARED
    if _SHARED is not None:
        return _SHARED
    tri = np.arange(128)
    ml = (tri[:, None] >= tri[None, :]).astype(np.float32)
    mr = (tri[:, None] <= tri[None, :]).astype(np.float32)
    maskL = np.ascontiguousarray(np.tile(ml, (1, 4)).astype(BF))
    maskR = np.ascontiguousarray(np.tile(mr, (1, 4)).astype(BF))
    ident = np.eye(128, dtype=BF)
    identf = np.eye(128, dtype=np.float32)
    h128 = np.ascontiguousarray(_hadamard(128).astype(BF))
    hp2 = np.ascontiguousarray(np.kron(_hadamard(64), np.eye(2)).astype(BF))
    half_d = 64
    inv_freq = 1.0 / (10000.0 ** (np.arange(0, half_d, dtype=np.float64) / half_d))
    angles = np.outer(np.arange(S, dtype=np.float64), inv_freq)
    ang = np.concatenate([angles, angles], axis=1)
    cosT = np.ascontiguousarray(np.cos(ang).T.astype(np.float32))
    sinT = np.ascontiguousarray(np.sin(ang).T.astype(np.float32))
    _SHARED = dict(maskL=maskL, maskR=maskR, ident=ident, identf=identf, h128=h128, hp2=hp2,
                   cosT=cosT, sinT=sinT)
    return _SHARED


def _prep_weights(wq, wk, wv, wo, W1, W2, W3):
    wq4 = wq.reshape(16, 128, 4, 4, 128)
    wqm = np.ascontiguousarray(wq4.transpose(2, 1, 0, 3, 4).reshape(512, 8192))
    wk3 = wk.reshape(16, 128, 512)
    wv3 = wv.reshape(16, 128, 512)
    wkv = np.concatenate([wk3, wv3], axis=2)
    wkvm = np.ascontiguousarray(wkv.transpose(1, 0, 2).reshape(128, 16384))
    wo4 = wo.reshape(16, 128, 4, 512)
    wom = np.ascontiguousarray(wo4.transpose(2, 1, 0, 3).reshape(512, 8192))
    w1r = W1.reshape(64, 128, 16, 128)
    w1m = np.ascontiguousarray(w1r.transpose(0, 3, 2, 1).reshape(8192, 2048))
    w3r = W3.reshape(64, 128, 16, 128)
    w3m = np.ascontiguousarray(w3r.transpose(0, 3, 2, 1).reshape(8192, 2048))
    j_i, c_i, s_i = np.meshgrid(np.arange(64), np.arange(64), np.arange(2), indexing="ij")
    perm = (c_i * 128 + 2 * j_i + s_i).reshape(64, 128)
    W2T = np.ascontiguousarray(W2.T)
    w2g = np.empty((8192, 2048), dtype=np.float32)
    for dsi in range(4):
        for jg in range(16):
            for jj in range(4):
                j = jg * 4 + jj
                w2g[(dsi * 16 + jg) * 128:(dsi * 16 + jg + 1) * 128,
                    jj * 512:(jj + 1) * 512] = W2T[perm[j], dsi * 512:(dsi + 1) * 512]
    return wqm, wkvm, wom, w1m, w3m, np.ascontiguousarray(w2g)


_NC_CACHE = None
_LAST_RES = None

def kernel(L0_attn_hidden_states, L0_attn_wq, L0_attn_wk, L0_attn_wv, L0_attn_wo,
           L0_attn_query_scale, L0_attn_key_scale, L0_ffn_W1, L0_ffn_W2, L0_ffn_W3):
    global _NC_CACHE
    x = np.asarray(L0_attn_hidden_states, dtype=np.float32)
    sh = _shared_consts()
    wqm, wkvm, wom, w1m, w3m, w2g = _prep_weights(
        np.asarray(L0_attn_wq, np.float32), np.asarray(L0_attn_wk, np.float32),
        np.asarray(L0_attn_wv, np.float32), np.asarray(L0_attn_wo, np.float32),
        np.asarray(L0_ffn_W1, np.float32), np.asarray(L0_ffn_W2, np.float32),
        np.asarray(L0_ffn_W3, np.float32))
    qs = np.ascontiguousarray(np.asarray(L0_attn_query_scale, np.float32).reshape(128, 1))
    ks = np.ascontiguousarray(np.asarray(L0_attn_key_scale, np.float32).reshape(128, 1))

    in_maps = []
    for c in range(NCORE):
        seq, jj = c // 4, c % 4
        lo = jj * 1024 - 1024
        x_ext = np.zeros((KEXT, E), np.float32)
        a = max(0, lo); bnd = min(S, lo + KEXT)
        x_ext[a - lo:bnd - lo] = x[seq, a:bnd]
        cos_e = np.zeros((128, KEXT), np.float32)
        sin_e = np.zeros((128, KEXT), np.float32)
        cos_e[:, a - lo:bnd - lo] = sh["cosT"][:, a:bnd]
        sin_e[:, a - lo:bnd - lo] = sh["sinT"][:, a:bnd]
        pos = lo + np.arange(KEXT)
        valid = (pos >= 0) & (pos < S)
        bk = np.ascontiguousarray(
            np.where(valid, 0.0, -30000.0).astype(np.float32).reshape(24, 128).T)
        in_maps.append(dict(
            x_ext=x_ext, cos_in=cos_e, sin_in=sin_e, biask_in=bk,
            qs_in=qs, ks_in=ks,
            ident_in=sh["ident"], identf_in=sh["identf"], h128_in=sh["h128"], hp2_in=sh["hp2"],
            maskl_in=sh["maskL"], maskr_in=sh["maskR"],
            wqm=wqm, wkvm=wkvm, wom=wom, w1m=w1m, w3m=w3m, w2g=w2g))
    if _NC_CACHE is None:
        _NC_CACHE = build_program()
    res = run_bass_kernel_spmd(_NC_CACHE, in_maps, core_ids=list(range(NCORE)))
    global _LAST_RES
    _LAST_RES = res
    outs = [np.asarray(res.results[c]["out"]) for c in range(NCORE)]
    return np.concatenate(outs, axis=0).reshape(B, S, E).astype(np.float32)


# version 09
# speedup vs baseline: 1.1144x; 1.0241x over previous
import numpy as np
import ml_dtypes
from contextlib import ExitStack

import concourse.bass as bass
import concourse.mybir as mybir
from concourse.bass_utils import run_bass_kernel_spmd

F32 = mybir.dt.float32
BF16 = mybir.dt.bfloat16
AF = mybir.ActivationFunctionType
ALU = mybir.AluOpType
BF = ml_dtypes.bfloat16

B, S, E = 2, 4096, 2048
H, KV, D = 16, 4, 128
HID = 8192
NCORE = 8
TOK = 1024
KEXT = 3072
NKB = 17
CAP = 50.0
SC_TANH = float(1.0 / (CAP * np.sqrt(128.0)))
SC_FWHT = float(1.0 / np.sqrt(HID))

OFF_CONST = 16896
OFF_X = 21312
OFF_Q = 86848
OFF_K = 119616
OFF_V = 144192
OFF_EO = 168768
OFF_G = 201536

G_TA = OFF_G + 0
G_TB = OFF_G + 4096
G_SR0 = OFF_G + 8192
G_SR1 = OFF_G + 12288
G_JUNK = OFF_G + 16384
G_CS = OFF_G + 18432

ENGS = ("sync", "gpsimd", "vector", "scalar", "tensor")
DEBUG_HOOK = None


def _hadamard(n):
    h = np.array([[1.0]], dtype=np.float64)
    while h.shape[0] < n:
        h = np.block([[h, h], [h, -h]])
    return h


class Plan:
    def __init__(self, nc, sems):
        self.nc = nc
        self.sems = sems
        self.prog = {e: [] for e in ENGS}
        self.cnt = {name: 0 for name in sems}
        self.nbar = 0

    def em(self, eng, fn):
        self.prog[eng].append(fn)

    def wait(self, eng, sem, val):
        if val is None or val <= 0:
            return
        h = self.sems[sem]
        self.em(eng, lambda e, h=h, v=val: e.wait_ge(h, v))

    def op(self, eng, fn, inc=None):
        if inc is None:
            self.em(eng, fn)
            return None
        h = self.sems[inc]
        self.cnt[inc] += 1
        v = self.cnt[inc]
        self.em(eng, lambda e, fn=fn, h=h: fn(e).then_inc(h, 1))
        return v

    def dma(self, eng, sem, out_ap, in_ap):
        h = self.sems[sem]
        self.cnt[sem] += 16
        v = self.cnt[sem]
        self.em(eng, lambda e, o=out_ap, i=in_ap, h=h: e.dma_start(out=o, in_=i).then_inc(h, 16))
        return v

    def barrier(self):
        self.nbar += 1
        tgt = self.nbar * len(ENGS)
        h = self.sems["bar"]
        for e in ENGS:
            self.em(e, lambda eng, h=h: eng.nop().then_inc(h, 1))
        for e in ENGS:
            self.em(e, lambda eng, h=h, t=tgt: eng.wait_ge(h, t))


def build_program():
    nc = bass.Bass("TRN2", target_bir_lowering=False)

    dr = {}
    def di(name, shape, dtype=F32):
        dr[name] = nc.dram_tensor(name, list(shape), dtype, kind="ExternalInput").ap()
    di("x_ext", (KEXT, E))
    di("cos_in", (128, KEXT)); di("sin_in", (128, KEXT))
    di("biask_in", (128, 24))
    di("qs_in", (128, 1)); di("ks_in", (128, 1))
    di("ident_in", (128, 128), BF16); di("identf_in", (128, 128)); di("h128_in", (128, 128), BF16)
    di("hp2_in", (128, 128), BF16)
    di("maskl_in", (128, 512), BF16); di("maskr_in", (128, 512), BF16)
    di("wqm", (512, 8192))
    di("wkvm", (128, 16384))
    di("wom", (512, 8192))
    di("w1m", (8192, 2048)); di("w3m", (8192, 2048))
    di("w2g", (8192, 2048))
    out = nc.dram_tensor("out", [TOK, E], F32, kind="ExternalOutput").ap()
    hid2_dram = nc.dram_tensor("hid2", [HID, TOK], BF16, kind="Internal").ap()
    q_dram = nc.dram_tensor("qdram", [128, 16384], BF16, kind="Internal").ap()

    A = nc.alloc_sbuf_tensor_at
    ones_bf = A("ones_bf", [128, 128], BF16, offset=OFF_CONST + 0)
    ones_f = A("ones_f", [128, 128], F32, offset=OFF_CONST + 256)
    ident = A("ident", [128, 128], BF16, offset=OFF_CONST + 768)
    identf = A("identf", [128, 128], F32, offset=OFF_CONST + 1024)
    h128c = A("h128c", [128, 128], BF16, offset=OFF_CONST + 1536)
    hp2c = A("hp2c", [128, 128], BF16, offset=OFF_CONST + 1792)
    maskl = A("maskl", [128, 512], BF16, offset=OFF_CONST + 2048)
    maskr = A("maskr", [128, 512], BF16, offset=OFF_CONST + 3072)
    biask = A("biask", [128, 24], F32, offset=OFF_CONST + 4096)
    qs_c = A("qs_c", [128, 1], F32, offset=OFF_CONST + 4224)
    ks_c = A("ks_c", [128, 1], F32, offset=OFF_CONST + 4256)
    eps5 = A("eps5", [128, 1], F32, offset=OFF_CONST + 4288)
    eps6 = A("eps6", [128, 1], F32, offset=OFF_CONST + 4320)
    ln_cols = A("ln_cols", [128, 16], F32, offset=OFF_CONST + 4352)

    Xr = A("Xr", [128, 16384], F32, offset=OFF_X)
    q_stage = [A("q_stage0", [128, 1024], BF16, offset=OFF_Q),
               A("q_stage1", [128, 1024], BF16, offset=OFF_Q + 2048)]
    qbuf = [A("qbuf0", [128, 1024], BF16, offset=G_JUNK),
            A("qbuf1", [128, 1024], BF16, offset=G_CS)]
    k_fm = A("k_fm", [128, 12288], BF16, offset=OFF_K)
    v_tm = A("v_tm", [128, 12288], BF16, offset=OFF_V)
    slotA = A("slotA", [128, 8192], BF16, offset=OFF_K)
    slotB = A("slotB", [128, 8192], BF16, offset=OFF_K + 16384)
    slot_t = [slotA, slotB]
    wkv_s = [A("wkvA", [128, 8192], BF16, offset=OFF_Q),
             A("wkvB", [128, 8192], BF16, offset=OFF_Q + 16384)]
    wo_a2 = A("wo_a2", [128, 16384], BF16, offset=OFF_Q)
    w1s = [A("w1s0", [128, 2048], BF16, offset=OFF_V + 8192),
           A("w1s1", [128, 2048], BF16, offset=OFF_V + 12288)]
    w3s = [A("w3s0", [128, 2048], BF16, offset=OFF_V + 16384),
           A("w3s1", [128, 2048], BF16, offset=OFF_V + 20480)]

    lnF = A("lnF", [128, 16384], BF16, offset=OFF_EO)
    lnH = A("lnH", [128, 8192], BF16, offset=OFF_EO)
    xin = A("xin", [128, 2048], F32, offset=OFF_EO + 16384)
    o_fm = A("o_fm", [128, 16384], BF16, offset=OFF_EO)

    t_a = A("t_a", [128, 1024], F32, offset=G_TA)
    t_b = A("t_b", [128, 1024], F32, offset=G_TB)
    sroot = [A("sroot0", [128, 1024], F32, offset=G_SR0),
             A("sroot1", [128, 1024], F32, offset=G_SR1)]
    junk2 = A("junk2", [128, 2048], F32, offset=G_SR0)
    rn_q = A("rn_q", [128, 1024], F32, offset=OFF_Q + 4096)
    rn_k = A("rn_k", [128, 512], F32, offset=G_SR1 + 2048)
    ln_tmf = [A("ln_tmf0", [128, 2048], F32, offset=G_TA),
              A("ln_tmf1", [128, 2048], F32, offset=OFF_Q + 8192)]
    krope_q = A("krope_q", [128, 1024], F32, offset=OFF_Q + 24576)
    ksq_q = [A("ksq_q0", [128, 1024], F32, offset=OFF_Q + 16384),
             A("ksq_q1", [128, 1024], F32, offset=OFF_Q + 20480)]
    ksq_k = [A("ksq_k0", [128, 512], F32, offset=G_JUNK),
             A("ksq_k1", [128, 512], F32, offset=G_SR0 + 2048)]
    cs_cos = A("cs_cos", [128, 1024], F32, offset=G_CS)
    cs_sin = A("cs_sin", [128, 1024], F32, offset=G_CS + 4096)
    a2_t1 = [A("a2_t1a", [128, 1024], F32, offset=G_TA),
             A("a2_t1b", [128, 1024], F32, offset=G_SR1)]
    a2_P = [A("a2_P0", [128, 1024], BF16, offset=G_TB),
            A("a2_P1", [128, 1024], BF16, offset=G_TB + 2048)]
    a2_rd = A("a2_rd", [128, 1024], F32, offset=G_SR0)
    a2_den = A("a2_den", [128, 1024], F32, offset=G_CS + 2048)
    f1_sg = [A("f1_sga", [128, 1024], F32, offset=G_TA),
             A("f1_sgb", [128, 1024], F32, offset=G_CS)]
    f1_hid = [A("f1_hid0", [128, 1024], BF16, offset=G_TB),
              A("f1_hid1", [128, 1024], BF16, offset=G_TB + 2048)]
    f1_h2 = [A("f1_h2_0", [128, 1024], BF16, offset=G_SR0),
             A("f1_h2_1", [128, 1024], BF16, offset=G_SR0 + 2048)]
    fw = A("fw", [128, 65536], BF16, offset=OFF_Q)
    swp = [A("swp0", [128, 1024], BF16, offset=G_JUNK),
           A("swp1", [128, 1024], BF16, offset=G_CS),
           A("swp2", [128, 1024], BF16, offset=G_CS + 2048),
           A("swp3", [128, 1024], BF16, offset=G_CS + 4096)]
    w2s = [A("w2s0", [128, 2048], BF16, offset=G_JUNK),
           A("w2s1", [128, 2048], BF16, offset=G_CS + 2048)]

    ctx = ExitStack()
    sem_names = ["bar", "t", "v", "s", "dC", "dX", "dXin", "dCS", "dWqA", "dWqB",
                 "dWkvA", "dWkvB", "dWoA", "dWoB", "dW1A", "dW1B", "dHid",
                 "dSwpA", "dSwpB", "dSwpC", "dSwpD", "dW2A", "dW2B", "dOut", "dQA", "dQB", "dQS"]
    with ctx:
        sems = {n: ctx.enter_context(nc.semaphore(n)) for n in sem_names}
        P = Plan(nc, sems)
        ps = [ctx.enter_context(nc.psum_tensor(f"ps{i}", [128, 1024], F32))
              for i in range(4)]
        half = lambda i, hh: ps[i][:, hh * 512:(hh + 1) * 512]

        def _dbg(tag):
            if DEBUG_HOOK is not None:
                DEBUG_HOOK(tag, nc, P, dict(lnF=lnF, lnH=lnH, q_dram=q_dram, k_fm=k_fm,
                                            v_tm=v_tm, o_fm=o_fm, Xr=Xr, fw=fw,
                                            hid2_dram=hid2_dram, ln_cols=ln_cols,
                                            f1_sg=f1_sg[1], f1_hid1=f1_hid[1], f1_h2_1=f1_h2[1],
                                            biask=biask, cs_cos=cs_cos, cs_sin=cs_sin))

        st = {
            "ln_tiny_v": {0: 0, 1: 0},
            "ln_norm_s": {0: 0, 1: 0},
            "ln_tm_t": {0: 0, 1: 0},
            "ln_evac_v": 0,
            "rope_ksq_t": {0: 0, 1: 0},
            "rope_fin_v": 0,
        }

        vC = 0
        for name, dst in [("ident_in", ident), ("identf_in", identf), ("h128_in", h128c), ("hp2_in", hp2c),
                          ("maskl_in", maskl), ("maskr_in", maskr), ("biask_in", biask),
                          ("qs_in", qs_c), ("ks_in", ks_c)]:
            vC = P.dma("sync", "dC", dst[:], dr[name])
        vC = P.dma("sync", "dC", cs_cos[:], dr["cos_in"][:, 1024:2048])
        vC = P.dma("sync", "dC", cs_sin[:], dr["sin_in"][:, 1024:2048])
        vXt = []
        for tl in range(8):
            vXt.append(P.dma("sync", "dX", Xr[:, tl * 2048:(tl + 1) * 2048],
                             dr["x_ext"][1024 + tl * 128:1024 + (tl + 1) * 128, :]))
        vWq = {0: P.dma("gpsimd", "dWqA", slotA[:], dr["wqm"][0:128, :]),
               1: P.dma("gpsimd", "dWqB", slotB[:], dr["wqm"][128:256, :])}
        P.op("vector", lambda v: v.memset(ones_bf[:], 1.0), inc="v")
        P.op("vector", lambda v: v.memset(ones_f[:], 1.0), inc="v")
        P.op("vector", lambda v: v.memset(eps5[:], 1e-5), inc="v")
        vMem = P.op("vector", lambda v: v.memset(eps6[:], 1e-6), inc="v")

        def emit_ln_tile(x_ap, dst_pitch, tl, wait_dma=None, dbuf=True):
            par = tl % 2
            lpar = par if dbuf else 0
            lnt = ln_tmf[lpar]
            c0 = par * 8
            s1c = ln_cols[:, c0:c0 + 1]; s2c = ln_cols[:, c0 + 1:c0 + 2]
            mc = ln_cols[:, c0 + 2:c0 + 3]; vc = ln_cols[:, c0 + 3:c0 + 4]
            rc = ln_cols[:, c0 + 4:c0 + 5]; nbc = ln_cols[:, c0 + 5:c0 + 6]
            src = ln_cols[:, c0 + 6:c0 + 7]
            if wait_dma is not None:
                P.wait("scalar", wait_dma[0], wait_dma[1])
                P.wait("vector", wait_dma[0], wait_dma[1])
            P.wait("vector", "t", st["ln_tm_t"][lpar])
            P.wait("vector", "v", st["rope_fin_v"])
            P.wait("vector", "t", max(st["rope_ksq_t"].values()))
            P.op("vector", lambda v, x=x_ap: v.tensor_scalar(
                lnt[:], x, 1.0, None, ALU.mult, ALU.add, accum_out=s1c), inc="v")
            jout = lnt if dbuf else junk2
            vsum = P.op("vector", lambda v, x=x_ap, o=jout: v.scalar_tensor_tensor(
                o[:], x, 1.0, x, ALU.mult, ALU.mult, accum_out=s2c), inc="v")
            P.wait("vector", "s", st["ln_norm_s"][par])
            P.wait("vector", "v", vsum)
            P.op("vector", lambda v: v.tensor_scalar(mc, s1c, 1.0 / 2048.0, None, ALU.mult), inc="v")
            vw = P.op("vector", lambda v: v.tensor_scalar(vc, s2c, 1.0 / 2048.0, None, ALU.mult), inc="v")
            P.wait("vector", "v", vw)
            vw = P.op("vector", lambda v: v.tensor_tensor(s1c, mc, mc, ALU.mult), inc="v")
            P.wait("vector", "v", vw)
            vv = P.op("vector", lambda v: v.tensor_tensor(vc, vc, s1c, ALU.subtract), inc="v")
            P.wait("scalar", "v", vv)
            vs = P.op("scalar", lambda s: s.activation(src, vc, AF.Ln, bias=eps5[:]), inc="s")
            P.wait("scalar", "s", vs)
            vs = P.op("scalar", lambda s: s.activation(rc, src, AF.Exp, scale=-0.5), inc="s")
            P.wait("vector", "s", vs)
            vtiny = P.op("vector", lambda v: v.tensor_scalar(nbc, mc, rc, -1.0, ALU.mult, ALU.mult), inc="v")
            st["ln_tiny_v"][par] = vtiny
            P.wait("scalar", "v", vtiny)
            vs = P.op("scalar", lambda s, x=x_ap: s.activation(lnt[:], x, AF.Identity, bias=nbc, scale=rc), inc="s")
            st["ln_norm_s"][par] = vs
            P.wait("tensor", "s", vs)
            P.wait("tensor", "v", st["ln_evac_v"])
            vt = None
            for e in range(16):
                pt = ps[e // 8][:, (e % 8) * 128:(e % 8) * 128 + 128]
                vt = P.op("tensor", lambda t, o=pt, i=lnt[:, e * 128:(e + 1) * 128]:
                          t.transpose(o, i, identf[:]),
                          inc="t" if e in (7, 15) else None)
            st["ln_tm_t"][lpar] = vt
            P.wait("vector", "t", vt)
            dtensor = lnF if dst_pitch == 1024 else lnH
            dap = dtensor.ap()
            v_ev = None
            for g in range(2):
                dst = bass.AP(tensor=dap.tensor,
                              offset=dap.offset + g * 8 * dst_pitch + tl * 128,
                              ap=[[dap.ap[0][0], 128], [dst_pitch, 8], [1, 128]])
                v_ev = P.op("vector", lambda v, o=dst, i=ps[g][:]: v.tensor_copy(o, i), inc="v")
            st["ln_evac_v"] = v_ev
            return v_ev

        def rope_front(pq_full, n, cs_off, t_stop, kpar):
            co = cs_cos[:, cs_off:cs_off + n]; si = cs_sin[:, cs_off:cs_off + n]
            tb = t_b[:, 0:n]
            P.wait("vector", "t", t_stop)
            P.op("vector", lambda v, o=t_a[0:64, 0:n], a=pq_full[64:128, :], b=si[0:64, :]:
                 v.tensor_tensor(o, a, b, ALU.mult), inc="v")
            P.op("vector", lambda v, o=t_a[64:128, 0:n], a=pq_full[0:64, :], b=si[64:128, :]:
                 v.tensor_tensor(o, a, b, ALU.mult), inc="v")
            P.op("vector", lambda v, o=tb, a=pq_full, b=co:
                 v.tensor_tensor(o, a, b, ALU.mult), inc="v")
            P.op("vector", lambda v, o=t_b[0:64, 0:n], a=t_b[0:64, 0:n], b=t_a[0:64, 0:n]:
                 v.tensor_tensor(o, a, b, ALU.subtract), inc="v")
            P.op("vector", lambda v, o=t_b[64:128, 0:n], a=t_b[64:128, 0:n], b=t_a[64:128, 0:n]:
                 v.tensor_tensor(o, a, b, ALU.add), inc="v")
            if n == 1024:
                kr = krope_q[:, 0:n]
                P.op("vector", lambda v, o=kr, i=tb: v.tensor_copy(o, i), inc="v")
            else:
                kr = tb
            ksq = (ksq_q if n == 1024 else ksq_k)[kpar][:, 0:n]
            P.wait("vector", "t", st["rope_ksq_t"][kpar])
            vk = None
            nch = (n + 511) // 512
            for ch in range(nch):
                w = min(512, n - ch * 512)
                vk = P.op("vector", lambda v, o=ksq[:, ch * 512:ch * 512 + w],
                          a=kr[:, ch * 512:ch * 512 + w]:
                          v.tensor_tensor(o, a, a, ALU.mult), inc="v")
            return vk

        def rope_tail(pq_full, n, scale_col, dst_ap, sr_par, kpar, vk):
            ksq = (ksq_q if n == 1024 else ksq_k)[kpar][:, 0:n]
            kr = krope_q[:, 0:n] if n == 1024 else t_b[:, 0:n]
            P.wait("tensor", "v", vk)
            vt = None
            nch = (n + 511) // 512
            for ch in range(nch):
                w = min(512, n - ch * 512)
                vt = P.op("tensor", lambda t, o=pq_full[:, ch * 512:ch * 512 + w],
                          i=ksq[:, ch * 512:ch * 512 + w]:
                          t.matmul(o, ones_f[:], i, start=True, stop=True),
                          inc="t" if ch == nch - 1 else None)
            st["rope_ksq_t"][kpar] = vt
            P.wait("scalar", "t", vt)
            sr = sroot[sr_par][:, 0:n]
            rb = (rn_q if n == 1024 else rn_k)[:, 0:n]
            vs = P.op("scalar", lambda s, o=sr, i=pq_full: s.activation(o, i, AF.Ln), inc="s")
            P.wait("scalar", "s", vs)
            vs = P.op("scalar", lambda s, o=rb, i=sr: s.activation(o, i, AF.Exp, scale=0.5), inc="s")
            P.wait("scalar", "s", vs)
            vs = P.op("scalar", lambda s, o=sr, i=rb: s.activation(o, i, AF.Ln, bias=eps6[:]), inc="s")
            P.wait("scalar", "s", vs)
            vs = P.op("scalar", lambda s, o=rb, i=sr: s.activation(o, i, AF.Exp, scale=-1.0), inc="s")
            P.wait("vector", "s", vs)
            vfin = P.op("vector", lambda v, o=dst_ap, a=kr, sc=scale_col, b=rb:
                        v.scalar_tensor_tensor(o, a, sc, b, ALU.mult, ALU.mult), inc="v")
            st["rope_fin_v"] = vfin
            return vfin, vs

        v_lastev = None
        for tl in range(8):
            v_lastev = emit_ln_tile(Xr[:, tl * 2048:(tl + 1) * 2048], 1024, tl,
                                    wait_dma=("dX", vXt[tl]))
        P.wait("tensor", "dC", vC)
        P.wait("tensor", "v", vMem)
        P.wait("tensor", "v", v_lastev)
        q_sems = ["dWqA", "dWqB"]
        sq_read = [0, 0, 0, 0]
        pend = None
        for h in range(16):
            qr, mloc = h // 4, h % 4
            par = qr % 2
            if mloc == 0:
                P.wait("tensor", q_sems[par], 16 * (qr // 2 + 1))
            pq = ps[h % 4]
            P.wait("tensor", "s", sq_read[h % 4])
            vt = None
            for e in range(16):
                for th in range(2):
                    vt = P.op("tensor", lambda t, o=pq[:, th * 512:(th + 1) * 512],
                              w=slot_t[par][:, e * 512 + mloc * 128:e * 512 + mloc * 128 + 128],
                              r=lnF[:, e * 1024 + th * 512:e * 1024 + (th + 1) * 512],
                              stt=(e == 0), sp=(e == 15):
                              t.matmul(o, w, r, start=stt, stop=sp),
                              inc="t" if (e == 15 and th == 1) else None)
            if mloc == 3 and qr < 2:
                P.wait("gpsimd", "t", vt)
                vWq[par] = P.dma("gpsimd", q_sems[par], slot_t[par][:],
                                 dr["wqm"][(qr + 2) * 128:(qr + 3) * 128, :])
            if pend is not None:
                hp, pqp, vkp = pend
                _, sq = rope_tail(pqp[:], 1024, qs_c[:], q_stage[hp % 2][:], hp % 2, hp % 2, vkp)
                sq_read[hp % 4] = sq
                P.wait("sync", "v", P.cnt["v"])
                qdst = bass.AP(tensor=q_dram.tensor, offset=q_dram.offset + hp * 128,
                               ap=[[16384, 128], [2048, 8], [1, 128]])
                P.dma("sync", ["dQA", "dQB"][hp % 2], qdst, q_stage[hp % 2][:])
            if h >= 2:
                P.wait("vector", ["dQA", "dQB"][h % 2], 16 * (h // 2))
            vk = rope_front(pq[:], 1024, 0, vt, h % 2)
            pend = (h, pq, vk)
        hp, pqp, vkp = pend
        _, sq = rope_tail(pqp[:], 1024, qs_c[:], q_stage[hp % 2][:], hp % 2, hp % 2, vkp)
        sq_read[hp % 4] = sq
        P.wait("sync", "v", P.cnt["v"])
        qdst = bass.AP(tensor=q_dram.tensor, offset=q_dram.offset + hp * 128,
                       ap=[[16384, 128], [2048, 8], [1, 128]])
        P.dma("sync", ["dQA", "dQB"][hp % 2], qdst, q_stage[hp % 2][:])
        _dbg("A1a")
        P.barrier()

        P.wait("gpsimd", "dQA", P.cnt["dQA"])
        P.wait("gpsimd", "dQB", P.cnt["dQB"])
        for g in range(2):
            P.dma("gpsimd", ["dWkvA", "dWkvB"][g], wkv_s[g][:],
                  dr["wkvm"][:, g * 8192:(g + 1) * 8192])
        kv_sq = [0, 0, 0, 0]
        kv_vev = [0, 0, 0, 0]

        def emit_kv_halfslab(ln_tensor, ln_pitch, th, slab, vln_done):
            pk = [half(0, 0), half(0, 1), half(1, 0), half(1, 1)]
            pv = [half(2, 0), half(2, 1), half(3, 0), half(3, 1)]
            P.wait("tensor", "s", max(kv_sq))
            P.wait("tensor", "v", max(kv_vev))
            if vln_done is not None:
                P.wait("tensor", "v", vln_done)
            P.wait("tensor", "dWkvA", P.cnt["dWkvA"])
            P.wait("tensor", "dWkvB", P.cnt["dWkvB"])
            tk = None
            for g in range(2):
                for eL in range(8):
                    e = g * 8 + eL
                    rhs_off = e * ln_pitch + (th * 512 if ln_pitch == 1024 else 0)
                    for kv in range(4):
                        tk = P.op("tensor", lambda t, o=pk[kv],
                                  w=wkv_s[g][:, eL * 1024 + kv * 128:eL * 1024 + kv * 128 + 128],
                                  r=ln_tensor[:, rhs_off:rhs_off + 512],
                                  stt=(e == 0), sp=(e == 15):
                                  t.matmul(o, w, r, start=stt, stop=sp),
                                  inc="t" if (e == 15 and kv == 3) else None)
            vt = None
            for g in range(2):
                for eL in range(8):
                    e = g * 8 + eL
                    rhs_off = e * ln_pitch + (th * 512 if ln_pitch == 1024 else 0)
                    for vtile in range(4):
                        vt = P.op("tensor", lambda t, o=pv[vtile],
                                  w=ln_tensor[:, rhs_off + vtile * 128:rhs_off + vtile * 128 + 128],
                                  r=wkv_s[g][:, eL * 1024 + 512:eL * 1024 + 1024],
                                  stt=(e == 0), sp=(e == 15):
                                  t.matmul(o, w, r, start=stt, stop=sp),
                                  inc="t" if (e == 15 and vtile == 3) else None)
                    if e % 4 == 1:
                        kv = e // 4
                        dst = k_fm[:, kv * 3072 + slab * 1024 + th * 512:
                                   kv * 3072 + slab * 1024 + (th + 1) * 512]
                        vkf = rope_front(pk[kv], 512, th * 512, tk, kv % 2)
                        _, sq = rope_tail(pk[kv], 512, ks_c[:], dst, kv % 2, kv % 2, vkf)
                        kv_sq[kv] = sq
            P.wait("vector", "t", vt)
            for vtile in range(4):
                gidx = slab * 8 + th * 4 + vtile
                kv_vev[vtile] = P.op("vector",
                                     lambda v, o=v_tm[:, gidx * 512:(gidx + 1) * 512], i=pv[vtile]:
                                     v.tensor_copy(o, i), inc="v")

        for th in range(2):
            emit_kv_halfslab(lnF, 1024, th, 1, None)
        for sl in (0, 2):
            P.wait("sync", "v", P.cnt["v"])
            P.dma("sync", "dCS", cs_cos[:], dr["cos_in"][:, sl * 1024:(sl + 1) * 1024])
            vcs = P.dma("sync", "dCS", cs_sin[:], dr["sin_in"][:, sl * 1024:(sl + 1) * 1024])
            P.wait("vector", "dCS", vcs)
            for th in range(2):
                vln = None
                for tl4 in range(4):
                    P.wait("sync", "s", P.cnt["s"])
                    vxin = P.dma("sync", "dXin", xin[:],
                                 dr["x_ext"][sl * 1024 + (th * 4 + tl4) * 128:
                                             sl * 1024 + (th * 4 + tl4 + 1) * 128, :])
                    vln = emit_ln_tile(xin[:], 512, tl4, wait_dma=("dXin", vxin), dbuf=False)
                emit_kv_halfslab(lnH, 512, th, sl, vln)
        _dbg("A1b")
        P.barrier()

        p_tanh = {}
        t_avstop = {}
        t_av = {}
        v_dencp = {}
        v_omul = {}
        P.wait("sync", "dQA", P.cnt["dQA"])
        P.wait("sync", "dQB", P.cnt["dQB"])
        for eqh in range(2):
            P.dma("gpsimd", "dWoA", wo_a2[:, eqh * 8192:(eqh + 1) * 8192],
                  dr["wom"][eqh * 128:(eqh + 1) * 128, :])
        st["t_wo_last"] = 0
        for it0 in range(2):
            tt0, kvp0 = it0 // 2, it0 % 2
            P.dma("sync", ["dQA", "dQB"][it0 % 2], qbuf[it0 % 2][:],
                  q_dram[:, tt0 * 2048 + kvp0 * 1024:tt0 * 2048 + (kvp0 + 1) * 1024])
        def emit_a2_epilogue(ite):
            tte, kvpe = ite // 2, ite % 2
            avp = ps[1] if ite % 2 == 0 else ps[3]
            P.wait("vector", "t", t_avstop[ite])
            vcp = P.op("vector", lambda v: v.tensor_copy(a2_den[:], ps[2][:]), inc="v")
            v_dencp[ite] = vcp
            P.op("vector", lambda v: v.reciprocal(a2_rd[:], a2_den[:]), inc="v")
            vom = P.op("vector", lambda v,
                       o=o_fm[:, tte * 2048 + kvpe * 1024:tte * 2048 + (kvpe + 1) * 1024],
                       a=avp[:]:
                       v.tensor_tensor(o, a, a2_rd[:], ALU.mult), inc="v")
            v_omul[ite] = vom

        wo_queue = []

        def queue_a2_wo(ite):
            tte = ite // 2
            first = [True]
            for eqh in range(2):
                for hh in range(16):
                    wo_queue.append((ite, tte, eqh, hh, first))

        def drain_a2_wo(nmax):
            tw = None
            for _ in range(min(nmax, len(wo_queue))):
                ite, tte, eqh, hh, first = wo_queue.pop(0)
                if first[0]:
                    P.wait("tensor", "dWoA", 32)
                    P.wait("tensor", "v", v_omul[ite])
                    first[0] = False
                tw = P.op("tensor", lambda t, o=ps[3][:, eqh * 512:(eqh + 1) * 512],
                          w=o_fm[:, tte * 2048 + hh * 128:tte * 2048 + hh * 128 + 128],
                          r=wo_a2[:, eqh * 8192 + hh * 512:eqh * 8192 + (hh + 1) * 512],
                          stt=(hh == 0), sp=(hh == 15):
                          t.matmul(o, w, r, start=stt, stop=sp),
                          inc="t" if (eqh == 1 and hh == 15) else None)
                if eqh == 1 and hh == 15:
                    st["t_wo_last"] = tw
                    P.wait("vector", "t", tw)
                    vev = None
                    for e2 in range(2):
                        xsl = Xr[:, tte * 2048 + e2 * 512:tte * 2048 + (e2 + 1) * 512]
                        vev = P.op("vector", lambda v, o=xsl,
                                   a=ps[3][:, e2 * 512:(e2 + 1) * 512], b=xsl:
                                   v.tensor_tensor(o, a, b, ALU.add), inc="v")
                    v_omul[ite] = vev

        it = -1
        for tt in range(8):
            for kvp in range(2):
                kva, kvb = kvp * 2, kvp * 2 + 1
                it += 1
                av_ps = ps[1] if it % 2 == 0 else ps[3]
                qsl = qbuf[it % 2]
                P.wait("tensor", ["dQA", "dQB"][it % 2], P.cnt[["dQA", "dQB"][it % 2]])
                for kb in range(NKB + 1):
                    if kb < NKB:
                        gstep = it * NKB + kb
                        if gstep >= 1:
                            P.wait("tensor", "s", p_tanh[gstep - 1])
                        vt = None
                        for hh, kv in ((0, kva), (1, kvb)):
                            vt = P.op("tensor", lambda t, o=ps[0][:, hh * 512:(hh + 1) * 512],
                                      w=k_fm[:, kv * 3072 + (tt + kb) * 128:kv * 3072 + (tt + kb) * 128 + 128],
                                      r=qsl[:, hh * 512:(hh + 1) * 512]:
                                      t.matmul(o, w, r, start=True, stop=True),
                                      inc="t" if hh == 1 else None)
                        P.wait("scalar", "t", vt)
                        if kb == NKB - 1:
                            nx = it + 2
                            if nx < 16:
                                ttn, kvpn = nx // 2, nx % 2
                                P.wait("sync", "t", vt)
                                P.dma("sync", ["dQA", "dQB"][nx % 2], qbuf[nx % 2][:],
                                      q_dram[:, ttn * 2048 + kvpn * 1024:ttn * 2048 + (kvpn + 1) * 1024])
                        vs = P.op("scalar", lambda s, o=a2_t1[kb % 2][:], i=ps[0][:]:
                                  s.activation(o, i, AF.Tanh, scale=SC_TANH), inc="s")
                        p_tanh[gstep] = vs
                    if kb == 0:
                        continue
                    kx = kb - 1
                    if kx >= 2:
                        P.wait("scalar", "t", t_av[(it, kx - 2)])
                    elif it >= 1:
                        P.wait("scalar", "t", t_av[(it - 1, 15 + kx)])
                    Pt = a2_P[kx % 2]
                    vs = P.op("scalar", lambda s, o=Pt[:], i=a2_t1[kx % 2][:],
                              b=biask[:, tt + kx:tt + kx + 1]:
                              s.activation(o, i, AF.Exp, bias=b, scale=CAP), inc="s")
                    need_v = None
                    if kx == 0 or kx == NKB - 1:
                        msk = maskl if kx == 0 else maskr
                        P.wait("vector", "s", vs)
                        for hh in range(2):
                            need_v = P.op("vector",
                                          lambda v, o=Pt[:, hh * 512:(hh + 1) * 512], b=msk[:]:
                                          v.tensor_tensor(o, o, b, ALU.mult), inc="v")
                    if kx == 0 and it >= 1:
                        emit_a2_epilogue(it - 1)
                        if (it - 1) % 2 == 1:
                            queue_a2_wo(it - 1)
                    if kx >= 4:
                        drain_a2_wo(3)
                    P.wait("tensor", "s", vs)
                    if need_v is not None:
                        P.wait("tensor", "v", need_v)
                    if kx == 0:
                        if it >= 1:
                            P.wait("tensor", "v", v_dencp[it - 1])
                        if it >= 2:
                            P.wait("tensor", "v", v_omul[it - 2])
                    for hh in range(2):
                        P.op("tensor", lambda t, o=ps[2][:, hh * 512:(hh + 1) * 512],
                             r=Pt[:, hh * 512:(hh + 1) * 512],
                             stt=(kx == 0), sp=(kx == NKB - 1):
                             t.matmul(o, ones_bf[:], r, start=stt, stop=sp), inc=None)
                    vt = None
                    for hh, kv in ((0, kva), (1, kvb)):
                        vt = P.op("tensor", lambda t, o=av_ps[:, hh * 512:(hh + 1) * 512],
                                  w=v_tm[:, (tt + kx) * 512 + kv * 128:(tt + kx) * 512 + kv * 128 + 128],
                                  r=Pt[:, hh * 512:(hh + 1) * 512],
                                  stt=(kx == 0), sp=(kx == NKB - 1):
                                  t.matmul(o, w, r, start=stt, stop=sp),
                                  inc="t" if hh == 1 else None)
                    t_av[(it, kx)] = vt
                t_avstop[it] = vt
        emit_a2_epilogue(15)
        queue_a2_wo(15)
        drain_a2_wo(64)
        _dbg("A2")
        P.barrier()

        P.wait("gpsimd", "t", st["t_wo_last"])
        vwo3 = 0
        for eqh in range(2):
            vwo3 = P.dma("gpsimd", "dWoB", wo_a2[:, eqh * 8192:(eqh + 1) * 8192],
                         dr["wom"][(eqh + 2) * 128:(eqh + 3) * 128, :])
        v_eq_ev = {}
        for tt in range(8):
            P.wait("tensor", "dWoB", vwo3)
            if tt >= 2:
                P.wait("tensor", "v", v_eq_ev[tt - 2])
            pw = ps[tt % 2]
            vt = None
            for eqh in range(2):
                for hh in range(16):
                    vt = P.op("tensor", lambda t, o=pw[:, eqh * 512:(eqh + 1) * 512],
                              w=o_fm[:, tt * 2048 + hh * 128:tt * 2048 + hh * 128 + 128],
                              r=wo_a2[:, eqh * 8192 + hh * 512:eqh * 8192 + (hh + 1) * 512],
                              stt=(hh == 0), sp=(hh == 15):
                              t.matmul(o, w, r, start=stt, stop=sp),
                              inc="t" if (hh == 15 and eqh == 1) else None)
            P.wait("vector", "t", vt)
            vev = None
            for eqh in range(2):
                xsl = Xr[:, tt * 2048 + (eqh + 2) * 512:tt * 2048 + (eqh + 3) * 512]
                vev = P.op("vector", lambda v, o=xsl, a=pw[:, eqh * 512:(eqh + 1) * 512], b=xsl:
                           v.tensor_tensor(o, a, b, ALU.add), inc="v")
            v_eq_ev[tt] = vev
        for cc in range(2):
            P.dma("gpsimd", ["dW1A", "dW1B"][cc], w1s[cc][:], dr["w1m"][cc * 128:(cc + 1) * 128, :])
            P.dma("gpsimd", ["dW1A", "dW1B"][cc], w3s[cc][:], dr["w3m"][cc * 128:(cc + 1) * 128, :])
        _dbg("A3")
        P.barrier()

        v_lastev = None
        for tl in range(8):
            v_lastev = emit_ln_tile(Xr[:, tl * 2048:(tl + 1) * 2048], 1024, tl)
        P.wait("tensor", "v", v_lastev)
        s_silu = {}
        v_hid = {}
        s_ev = {}
        t_p1 = {}

        def emit_pass1(cc):
            P.wait("tensor", "v", v_hid[cc])
            if cc >= 1:
                P.wait("tensor", "s", s_ev[cc - 1])
            vt1 = None
            for th in range(2):
                vt1 = P.op("tensor", lambda t, o=ps[2][:, th * 512:(th + 1) * 512],
                           r=f1_hid[cc % 2][:, th * 512:(th + 1) * 512]:
                           t.matmul(o, h128c[:], r, start=True, stop=True),
                           inc="t" if th == 1 else None)
            t_p1[cc] = vt1
            P.wait("scalar", "t", vt1)
            if cc >= 2:
                P.wait("scalar", "dHid", 16 * (cc - 1))
            vse = P.op("scalar", lambda s, o=f1_h2[cc % 2][:]:
                       s.activation(o, ps[2][:], AF.Identity), inc="s")
            s_ev[cc] = vse
            P.wait("sync", "s", vse)
            P.dma("sync", "dHid", hid2_dram[cc * 128:(cc + 1) * 128, :], f1_h2[cc % 2][:])

        for c in range(64):
            par = c % 2
            P.wait("tensor", ["dW1A", "dW1B"][par], 32 * (c // 2 + 1))
            if c >= 1:
                P.wait("tensor", "s", s_silu[c - 1])
            if c >= 2:
                P.wait("tensor", "v", v_hid[c - 2])
            psL = ps[1] if par == 0 else ps[3]
            vtg = None
            for e in range(16):
                for th in range(2):
                    vtg = P.op("tensor", lambda t, o=ps[0][:, th * 512:(th + 1) * 512],
                               w=w1s[par][:, e * 128:(e + 1) * 128],
                               r=lnF[:, e * 1024 + th * 512:e * 1024 + (th + 1) * 512],
                               stt=(e == 0), sp=(e == 15):
                               t.matmul(o, w, r, start=stt, stop=sp),
                               inc="t" if (e == 15 and th == 1) else None)
            vtl = None
            for e in range(16):
                for th in range(2):
                    vtl = P.op("tensor", lambda t, o=psL[:, th * 512:(th + 1) * 512],
                               w=w3s[par][:, e * 128:(e + 1) * 128],
                               r=lnF[:, e * 1024 + th * 512:e * 1024 + (th + 1) * 512],
                               stt=(e == 0), sp=(e == 15):
                               t.matmul(o, w, r, start=stt, stop=sp),
                               inc="t" if (e == 15 and th == 1) else None)
            if c < 62:
                P.wait("gpsimd", "t", vtl)
                P.dma("gpsimd", ["dW1A", "dW1B"][par], w1s[par][:],
                      dr["w1m"][(c + 2) * 128:(c + 3) * 128, :])
                P.dma("gpsimd", ["dW1A", "dW1B"][par], w3s[par][:],
                      dr["w3m"][(c + 2) * 128:(c + 3) * 128, :])
            if c >= 1:
                emit_pass1(c - 1)
            P.wait("scalar", "t", vtg)
            if c >= 2:
                P.wait("scalar", "v", v_hid[c - 2])
            vss = P.op("scalar", lambda s, o=f1_sg[par][:]: s.activation(o, ps[0][:], AF.Silu), inc="s")
            s_silu[c] = vss
            P.wait("vector", "t", vtl)
            P.wait("vector", "s", vss)
            if c >= 2:
                P.wait("vector", "t", t_p1[c - 2])
            vvh = P.op("vector", lambda v, o=f1_hid[c % 2][:], a=f1_sg[par][:], b=psL[:]:
                       v.tensor_tensor(o, a, b, ALU.mult), inc="v")
            v_hid[c] = vvh
        emit_pass1(63)
        vhid_total = P.cnt["dHid"]
        _dbg("F1")
        P.barrier()

        P.wait("sync", "dHid", vhid_total)
        P.wait("scalar", "dHid", vhid_total)
        h2ap = hid2_dram
        s_p2ev = {}
        t_p2 = {}
        swp_sems = ["dSwpA", "dSwpB", "dSwpC", "dSwpD"]
        for j in range(64):
            sl = j % 4
            q_eng = "sync" if j % 2 == 0 else "scalar"
            if j >= 4:
                P.wait(q_eng, "t", t_p2[j - 4])
            srcap = bass.AP(tensor=h2ap.tensor, offset=h2ap.offset + j * 2048,
                            ap=[[131072, 64], [1024, 2], [1, 1024]])
            vsw = P.dma(q_eng, swp_sems[sl], swp[sl][:], srcap)
            P.wait("tensor", swp_sems[sl], vsw)
            pp = ps[j % 2]
            if j >= 2:
                P.wait("tensor", "s", s_p2ev[j - 2])
            vt = None
            for th in range(2):
                vt = P.op("tensor", lambda t, o=pp[:, th * 512:(th + 1) * 512],
                          r=swp[sl][:, th * 512:(th + 1) * 512]:
                          t.matmul(o, hp2c[:], r, start=True, stop=True),
                          inc="t" if th == 1 else None)
            t_p2[j] = vt
            P.wait("scalar", "t", vt)
            vse = P.op("scalar", lambda s, o=fw[:, j * 1024:(j + 1) * 1024], i=pp[:]:
                       s.activation(o, i, AF.Identity, scale=SC_FWHT), inc="s")
            s_p2ev[j] = vse
        _dbg("FW2a")
        P.barrier()

        po = [half(i, hh) for i in range(4) for hh in range(2)]
        w2_sems = ["dW2A", "dW2B"]
        for parx in range(2):
            P.dma("gpsimd", w2_sems[parx], w2s[parx][:], dr["w2g"][parx * 128:(parx + 1) * 128, :])
        v_ds_ev = {}
        for ds in range(4):
            if ds >= 1:
                P.wait("tensor", "v", v_ds_ev[ds - 1])
            vt = None
            for jg in range(16):
                gi = ds * 16 + jg
                par = gi % 2
                P.wait("tensor", w2_sems[par], 16 * (gi // 2 + 1))
                for j4 in range(4):
                    j = jg * 4 + j4
                    for tt in range(8):
                        vt = P.op("tensor", lambda t, o=po[tt],
                                  w=fw[:, j * 1024 + tt * 128:j * 1024 + tt * 128 + 128],
                                  r=w2s[par][:, j4 * 512:(j4 + 1) * 512],
                                  stt=(jg == 0 and j4 == 0), sp=(jg == 15 and j4 == 3):
                                  t.matmul(o, w, r, start=stt, stop=sp),
                                  inc="t" if (j4 == 3 and tt == 7) else None)
                if gi + 2 < 64:
                    P.wait("gpsimd", "t", vt)
                    P.dma("gpsimd", w2_sems[par], w2s[par][:],
                          dr["w2g"][(gi + 2) * 128:(gi + 3) * 128, :])
            P.wait("vector", "t", vt)
            vev = None
            for tt in range(8):
                xsl = Xr[:, tt * 2048 + ds * 512:tt * 2048 + (ds + 1) * 512]
                vev = P.op("vector", lambda v, o=xsl, a=po[tt], b=xsl:
                           v.tensor_tensor(o, a, b, ALU.add), inc="v")
                P.wait("sync", "v", vev)
                P.dma("sync", "dOut", out[tt * 128:(tt + 1) * 128, ds * 512:(ds + 1) * 512], xsl)
            v_ds_ev[ds] = vev
        P.wait("sync", "dOut", P.cnt["dOut"])

        with nc.Block() as block:
            block.sync(lambda e: [f(e) for f in P.prog["sync"]])
            block.gpsimd(lambda e: [f(e) for f in P.prog["gpsimd"]])
            block.vector(lambda e: [f(e) for f in P.prog["vector"]])
            block.scalar(lambda e: [f(e) for f in P.prog["scalar"]])
            block.tensor(lambda e: [f(e) for f in P.prog["tensor"]])
    return nc


_SHARED = None

def _shared_consts():
    global _SHARED
    if _SHARED is not None:
        return _SHARED
    tri = np.arange(128)
    ml = (tri[:, None] >= tri[None, :]).astype(np.float32)
    mr = (tri[:, None] <= tri[None, :]).astype(np.float32)
    maskL = np.ascontiguousarray(np.tile(ml, (1, 4)).astype(BF))
    maskR = np.ascontiguousarray(np.tile(mr, (1, 4)).astype(BF))
    ident = np.eye(128, dtype=BF)
    identf = np.eye(128, dtype=np.float32)
    h128 = np.ascontiguousarray(_hadamard(128).astype(BF))
    hp2 = np.ascontiguousarray(np.kron(_hadamard(64), np.eye(2)).astype(BF))
    half_d = 64
    inv_freq = 1.0 / (10000.0 ** (np.arange(0, half_d, dtype=np.float64) / half_d))
    angles = np.outer(np.arange(S, dtype=np.float64), inv_freq)
    ang = np.concatenate([angles, angles], axis=1)
    cosT = np.ascontiguousarray(np.cos(ang).T.astype(np.float32))
    sinT = np.ascontiguousarray(np.sin(ang).T.astype(np.float32))
    _SHARED = dict(maskL=maskL, maskR=maskR, ident=ident, identf=identf, h128=h128, hp2=hp2,
                   cosT=cosT, sinT=sinT)
    return _SHARED


def _prep_weights(wq, wk, wv, wo, W1, W2, W3):
    wq4 = wq.reshape(16, 128, 4, 4, 128)
    wqm = np.ascontiguousarray(wq4.transpose(2, 1, 0, 3, 4).reshape(512, 8192))
    wk3 = wk.reshape(16, 128, 512)
    wv3 = wv.reshape(16, 128, 512)
    wkv = np.concatenate([wk3, wv3], axis=2)
    wkvm = np.ascontiguousarray(wkv.transpose(1, 0, 2).reshape(128, 16384))
    wo4 = wo.reshape(16, 128, 4, 512)
    wom = np.ascontiguousarray(wo4.transpose(2, 1, 0, 3).reshape(512, 8192))
    w1r = W1.reshape(64, 128, 16, 128)
    w1m = np.ascontiguousarray(w1r.transpose(0, 3, 2, 1).reshape(8192, 2048))
    w3r = W3.reshape(64, 128, 16, 128)
    w3m = np.ascontiguousarray(w3r.transpose(0, 3, 2, 1).reshape(8192, 2048))
    j_i, c_i, s_i = np.meshgrid(np.arange(64), np.arange(64), np.arange(2), indexing="ij")
    perm = (c_i * 128 + 2 * j_i + s_i).reshape(64, 128)
    W2T = np.ascontiguousarray(W2.T)
    w2g = np.empty((8192, 2048), dtype=np.float32)
    for dsi in range(4):
        for jg in range(16):
            for jj in range(4):
                j = jg * 4 + jj
                w2g[(dsi * 16 + jg) * 128:(dsi * 16 + jg + 1) * 128,
                    jj * 512:(jj + 1) * 512] = W2T[perm[j], dsi * 512:(dsi + 1) * 512]
    return wqm, wkvm, wom, w1m, w3m, np.ascontiguousarray(w2g)


_NC_CACHE = None
_LAST_RES = None

def kernel(L0_attn_hidden_states, L0_attn_wq, L0_attn_wk, L0_attn_wv, L0_attn_wo,
           L0_attn_query_scale, L0_attn_key_scale, L0_ffn_W1, L0_ffn_W2, L0_ffn_W3):
    global _NC_CACHE
    x = np.asarray(L0_attn_hidden_states, dtype=np.float32)
    sh = _shared_consts()
    wqm, wkvm, wom, w1m, w3m, w2g = _prep_weights(
        np.asarray(L0_attn_wq, np.float32), np.asarray(L0_attn_wk, np.float32),
        np.asarray(L0_attn_wv, np.float32), np.asarray(L0_attn_wo, np.float32),
        np.asarray(L0_ffn_W1, np.float32), np.asarray(L0_ffn_W2, np.float32),
        np.asarray(L0_ffn_W3, np.float32))
    qs = np.ascontiguousarray(np.asarray(L0_attn_query_scale, np.float32).reshape(128, 1))
    ks = np.ascontiguousarray(np.asarray(L0_attn_key_scale, np.float32).reshape(128, 1))

    in_maps = []
    for c in range(NCORE):
        seq, jj = c // 4, c % 4
        lo = jj * 1024 - 1024
        x_ext = np.zeros((KEXT, E), np.float32)
        a = max(0, lo); bnd = min(S, lo + KEXT)
        x_ext[a - lo:bnd - lo] = x[seq, a:bnd]
        cos_e = np.zeros((128, KEXT), np.float32)
        sin_e = np.zeros((128, KEXT), np.float32)
        cos_e[:, a - lo:bnd - lo] = sh["cosT"][:, a:bnd]
        sin_e[:, a - lo:bnd - lo] = sh["sinT"][:, a:bnd]
        pos = lo + np.arange(KEXT)
        valid = (pos >= 0) & (pos < S)
        bk = np.ascontiguousarray(
            np.where(valid, 0.0, -30000.0).astype(np.float32).reshape(24, 128).T)
        in_maps.append(dict(
            x_ext=x_ext, cos_in=cos_e, sin_in=sin_e, biask_in=bk,
            qs_in=qs, ks_in=ks,
            ident_in=sh["ident"], identf_in=sh["identf"], h128_in=sh["h128"], hp2_in=sh["hp2"],
            maskl_in=sh["maskL"], maskr_in=sh["maskR"],
            wqm=wqm, wkvm=wkvm, wom=wom, w1m=w1m, w3m=w3m, w2g=w2g))
    if _NC_CACHE is None:
        _NC_CACHE = build_program()
    res = run_bass_kernel_spmd(_NC_CACHE, in_maps, core_ids=list(range(NCORE)))
    global _LAST_RES
    _LAST_RES = res
    outs = [np.asarray(res.results[c]["out"]) for c in range(NCORE)]
    return np.concatenate(outs, axis=0).reshape(B, S, E).astype(np.float32)
